# version 02
# baseline (speedup 1.0000x reference)
import numpy as np
import ml_dtypes
import concourse.bass as bass
import concourse.mybir as mybir
from concourse.bass_utils import run_bass_kernel_spmd
from contextlib import ExitStack

F32 = mybir.dt.float32
BF16 = mybir.dt.bfloat16
AF = mybir.ActivationFunctionType
ALU = mybir.AluOpType
AX = mybir.AxisListType

NP, NC = 128, 512
B, S, D = 4, 4096, 1024
TH = S // 2
KT = D // NP
RSQRT2 = 2.0 ** (-0.5)
E, I_DIM, PASS_I = 8, 2688, 384
H0 = 8
LN_EPS = 1e-5
NT8, NT4 = S // NC, TH // NC
PASSES = I_DIM // PASS_I
TOTAL_P = E * PASSES
PI_MT = (PASS_I + NP - 1) // NP


class _Mock:
    def __getattr__(self, k):
        return self
    def __call__(self, *a, **kw):
        return self
    def __getitem__(self, k):
        return self
    def __enter__(self):
        return self
    def __exit__(self, *a):
        return False


class Ctx:
    def __init__(self, sem_names):
        self.marks = {}
        self.sem_names = sem_names

    def start(self, real, nc=None, es=None):
        self.real = real
        self.cnt = {n: 0 for n in self.sem_names}
        self.sem = ({n: es.enter_context(nc.semaphore(n)) for n in self.sem_names}
                    if real else {n: None for n in self.sem_names})

    def inc(self, inst, name, k=1):
        self.cnt[name] += k
        if self.real:
            inst.then_inc(self.sem[name], k)
        return inst

    def mark(self, label, offset=0):
        self.marks[label] = self.cnt[label.split(":")[0]] + offset

    def wait(self, eng, label, offset=0):
        val = self.marks.get(label, 0) + offset
        if self.real and val > 0:
            eng.wait_ge(self.sem[label.split(":")[0]], val)

    def wait_cnt(self, eng, name, val):
        if self.real and val > 0:
            eng.wait_ge(self.sem[name], val)


def accum_mm(t, out_ap, pairs):
    mi = None
    n = len(pairs)
    for i, (l, r) in enumerate(pairs):
        mi = t.matmul(out_ap, l, r, start=(i == 0), stop=(i == n - 1))
    return mi


def rows_of(it):
    return NP if (it + 1) * NP <= PASS_I else PASS_I - it * NP


def build_launch_a(debug=()):
    ctx = Ctx(["lg", "lh", "lw", "lsc", "mm", "mmy", "ve", "se", "st", "vi", "vo", "silu"])

    def emit(real, nc=None):
        es = ExitStack()
        if real:
            def di(name, shape, dt=F32):
                return nc.dram_tensor(name, shape, dt, kind="ExternalInput").ap()
            sbuf = lambda name, shape, dt: es.enter_context(nc.sbuf_tensor(name, shape, dt))
            psum = lambda name, shape, dt=F32: es.enter_context(nc.psum_tensor(name, shape, dt))
        else:
            di = lambda *a, **k: _Mock()
            sbuf = lambda *a, **k: _Mock()
            psum = lambda *a, **k: _Mock()

        xt = di("xt", [D, S])
        wq_d = di("wq", [D, 512]); wk_d = di("wk", [D, 512]); wv_d = di("wv", [D, 512])
        wout_d = di("wout", [512, D])
        bq_c = di("bq_c", [NP, 4]); bk_c = di("bk_c", [NP, 4])
        bv_row = di("bv_row", [1, 512]); bout_c = di("bout_c", [NP, 8])
        gam_c = di("gam_c", [NP, 8]); bet_c = di("bet_c", [NP, 8])
        rw_t = di("rw_t", [D, 8]); a1_t = di("a1_t", [D, 64]); a2_t = di("a2_t", [64, 8])
        b1_c = di("b1_c", [64, 1]); b2_c = di("b2_c", [8, 1])
        gw = di("gw", [E, D, I_DIM]); uw = di("uw", [E, D, I_DIM]); dw = di("dw", [E, I_DIM, D])
        onec_in = di("onec_in", [NP, 1], BF16)
        argw_in = di("argw_in", [1, 8])
        ident_in = di("ident_in", [NP, NP], BF16)
        if real:
            x1t = nc.dram_tensor("x1t", [D, TH], F32, kind="ExternalOutput").ap()
            scr = nc.dram_tensor("scr", [16, S], F32, kind="Internal").ap()
            scores_dram = nc.dram_tensor("scores_dram", [E, TH], F32, kind="Internal").ap()
            s_dram = nc.dram_tensor("s_dram", [D, TH], BF16, kind="Internal").ap()
            dbg = {}
            for nm in debug:
                shape = {"d_xn": [D, S], "d_scores": [E, TH], "d_s": [D, TH],
                         "d_o": [512, TH], "d_k": [512, S], "d_q": [512, TH],
                         "d_lg": [40, TH], "d_moe": [D, TH], "d_scr": [8, S],
                         "d_ah": [64, TH], "d_tm": [NP, 16 * 40],
                         "d_spl": [NP, KT * 16], "d_xnlo": [D, TH]}[nm]
                dt_ = F32 if nm in ("d_scores", "d_lg", "d_moe", "d_scr", "d_ah", "d_tm") else BF16
                dbg[nm] = nc.dram_tensor(nm, shape, dt_, kind="ExternalOutput").ap()
        else:
            x1t = scr = scores_dram = s_dram = _Mock()
            dbg = {nm: _Mock() for nm in debug}

        cols = sbuf("cols", [NP, 64], F32)
        onec = sbuf("onec", [NP, 1], BF16)
        ident = sbuf("ident", [NP, NP], BF16)
        onecB = sbuf("onecB", [NP, NP], BF16)
        s128 = sbuf("s128", [NP, 96], F32)
        rowstage = sbuf("rowstage", [1, 2 * NC], F32)
        scs = sbuf("scs", [NP, NC], F32)

        def scoped(es_, side="left"):
            return (lambda name, shape, dt: es_.enter_context(nc.sbuf_tensor(name, shape, dt, side=side))) if real else (lambda *a: _Mock())
        xn_own = sbuf("xn_own", [NP, KT * TH], BF16)
        es_o = ExitStack(); sb_o = scoped(es_o, "right")
        xn_oth = sb_o("xn_oth", [NP, KT * TH], BF16)
        es_x = ExitStack(); sb_x = scoped(es_x)
        xbf = sb_x("xbf", [NP, KT * S], BF16)
        sq0 = sb_x("sq0", [NP, 8 * NC], BF16)
        rt0 = sb_x("rt0", [NP, S], BF16)
        st0 = sb_x("st0", [NP, S], BF16)
        es_l = ExitStack(); sb_l = scoped(es_l, "right")
        xnlo = sb_l("xnlo", [NP, KT * TH], BF16)
        if real:
            es_x.close()
        es_w = ExitStack(); sb_w = scoped(es_w)
        wq_sb = sb_w("wq_sb", [NP, KT * 512], BF16)
        wk_sb = sb_w("wk_sb", [NP, KT * 512], BF16)
        es_d = ExitStack(); sb_d = scoped(es_d)
        ah = sb_d("ah", [64, TH], F32)
        ah2 = sb_d("ah2", [64, TH], F32)
        ahhi = sb_d("ahhi", [64, TH], BF16); ahlo = sb_d("ahlo", [64, TH], BF16)
        lgfm = sb_d("lgfm", [40, TH], F32)
        lg_hi = sb_d("lg_hi", [40, TH], BF16)
        lg_lo = sb_d("lg_lo", [40, TH], BF16)
        wf32 = sb_d("wf32", [NP, 80], F32)
        rwhi = sb_d("rwhi", [NP, KT * 8], BF16); rwlo = sb_d("rwlo", [NP, KT * 8], BF16)
        a1hi = sb_d("a1hi", [NP, KT * 64], BF16); a1lo = sb_d("a1lo", [NP, KT * 64], BF16)
        a2hi = sb_d("a2hi", [64, 8], BF16); a2lo = sb_d("a2lo", [64, 8], BF16)
        tm = sb_d("tm", [NP, 16 * 40], F32)
        tmw = sb_d("tmw", [NP, 16 * 16], F32)
        tmc = sb_d("tmc", [NP, 16 * 8], F32)
        tmx = sb_d("tmx", [NP, 16], F32)
        scb16 = sb_d("scb16", [NP, 16 * 8], BF16)
        argw = sb_d("argw", [NP, 8], F32)
        if real:
            es_l.close()
        es_k = ExitStack(); sb_k = scoped(es_k, "right")
        kfm = sb_k("kfm", [NP, 4 * S], BF16)
        qfm = sb_k("qfm", [NP, 4 * TH], BF16)
        if real:
            es_d.close()
            es_w.close()
        es_f = ExitStack(); sb_f = scoped(es_f)
        wv_sb = sb_f("wv_sb", [NP, KT * 512], BF16)
        wo_sb = sb_f("wo_sb", [NP, 4 * D], BF16)
        vtok = sb_f("vtok", [NP, 32 * 520], BF16)
        ofm = sb_f("ofm", [NP, 4 * TH], BF16)
        ychunk = sb_f("ychunk", [NP, 8 * NC], BF16)
        expb = [sb_f(f"expb{i}", [NP, NC], BF16) for i in range(2)]
        nrm = sb_f("nrm", [NP, TH], BF16)
        bvt = sb_f("bvt", [NP, 512], BF16)
        rt2 = sb_f("rt2", [NP, TH], BF16)
        st2 = sb_f("st2", [NP, TH], BF16)
        if real:
            es_f.close()
            es_k.close()
            es_o.close()
        wg_b = [sbuf(f"wg_b{i}", [NP, KT * PASS_I], BF16) for i in range(2)]
        wu_b = [sbuf(f"wu_b{i}", [NP, KT * PASS_I], BF16) for i in range(2)]
        wd_b = [sbuf(f"wd_b{i}", [NP, PI_MT * D], BF16) for i in range(2)]
        moe_out = sbuf("moe_out", [NP, KT * TH], F32)
        inter = sbuf("inter", [NP, PI_MT * TH], BF16)
        sc_b = sbuf("sc_b", [NP, TH], F32)
        xf32 = sbuf("xf32", [NP, TH], F32)
        sbb = sbuf("sbb", [NP, TH], BF16)
        gtmp = sbuf("gtmp", [NP, TH], BF16)

        bk = [psum(f"bk{i}", [NP, NC]) for i in range(6)]
        p_row = psum("p_row", [1, NC])
        p_tr = psum("p_tr", [NP, 416], BF16)

        ctx.start(real, nc, es)
        block = es.enter_context(nc.Block()) if real else _Mock()

        fsl = lambda buf, kt, c0, w=NC, Tlen=S: buf[:, kt * Tlen + c0: kt * Tlen + c0 + w]
        def xnsl(kt, c0, w=NC):
            if c0 < TH:
                return xn_own[:, kt * TH + c0: kt * TH + c0 + w]
            return xn_oth[:, kt * TH + (c0 - TH): kt * TH + (c0 - TH) + w]
        hb = lambda hh: (hh % 2) * 64
        QL = 4 * NT4 - 1

        def gps(g):
            for kt in range(KT):
                ctx.inc(g.dma_start(out=fsl(xbf, kt, 0, S), in_=xt[kt * NP:(kt + 1) * NP, :]), "lg", 16)
            ctx.inc(g.dma_start(out=onec[:], in_=onec_in), "lg", 16)
            ctx.inc(g.dma_start(out=ident[:], in_=ident_in), "lg", 16)
            ctx.inc(g.dma_start(out=cols[:, 0:4], in_=bq_c), "lg", 16)
            ctx.inc(g.dma_start(out=cols[:, 4:8], in_=bk_c), "lg", 16)
            ctx.inc(g.dma_start(out=cols[:, 8:16], in_=bout_c), "lg", 16)
            ctx.inc(g.dma_start(out=cols[:, 16:24], in_=gam_c), "lg", 16)
            ctx.inc(g.dma_start(out=cols[:, 24:32], in_=bet_c), "lg", 16)
            ctx.inc(g.dma_start(out=cols[0:64, 32:33], in_=b1_c), "lg", 16)
            ctx.inc(g.dma_start(out=cols[0:8, 33:34], in_=b2_c), "lg", 16)
            ctx.mark("lg:init")
            ctx.wait(g, "lh:ln0rows")
            ctx.inc(g.dma_start(out=rt0[:], in_=scr[2:3, :].partition_broadcast(NP)), "lg", 16)
            ctx.inc(g.dma_start(out=st0[:], in_=scr[3:4, :].partition_broadcast(NP)), "lg", 16)
            ctx.mark("lg:rs")
            ctx.wait(g, "ve:xn_done")
            for w_d, w_s in ((wq_d, wq_sb), (wk_d, wk_sb)):
                for kt in range(KT):
                    ctx.inc(g.dma_start(out=w_s[:, kt * 512:(kt + 1) * 512],
                                        in_=w_d[kt * NP:(kt + 1) * NP, :]), "lg", 16)
            ctx.inc(g.dma_start(out=argw[:], in_=argw_in.partition_broadcast(NP)), "lg", 16)
            ctx.mark("lg:wqk")
            for kt in range(KT):
                if kt > 0:
                    ctx.wait(g, f"ve:split{kt - 1}")
                ctx.inc(g.dma_start(out=wf32[:, 0:8], in_=rw_t[kt * NP:(kt + 1) * NP, :]), "lg", 16)
                ctx.inc(g.dma_start(out=wf32[:, 8:72], in_=a1_t[kt * NP:(kt + 1) * NP, :]), "lg", 16)
                ctx.mark(f"lg:wf{kt}")
            ctx.wait(g, f"ve:split{KT - 1}")
            ctx.inc(g.dma_start(out=wf32[0:64, 0:8], in_=a2_t), "lg", 16)
            ctx.mark("lg:a2")
            ctx.wait(g, f"mm:q{QL}")
            ctx.wait(g, "lh:scores")
            for kt in range(KT):
                ctx.inc(g.dma_start(out=wv_sb[:, kt * 512:(kt + 1) * 512],
                                    in_=wv_d[kt * NP:(kt + 1) * NP, :]), "lg", 16)
            for kt in range(4):
                ctx.inc(g.dma_start(out=wo_sb[:, kt * D:(kt + 1) * D],
                                    in_=wout_d[kt * NP:(kt + 1) * NP, :]), "lg", 16)
            ctx.inc(g.dma_start(out=bvt[:, :], in_=bv_row.partition_broadcast(NP)), "lg", 16)
            ctx.mark("lg:wv")
            ctx.wait(g, "lh:boundrow")
            ctx.inc(g.dma_start(out=cols[:, 34:38], in_=scr[1:2, 0:4].partition_broadcast(NP)), "lg", 16)
            ctx.mark("lg:bound")
            for hh in range(H0):
                ctx.wait(g, f"lh:recip{hh}")
                if hh >= 1:
                    ctx.wait(g, f"ve:onorm{hh - 1}")
                ctx.inc(g.dma_start(out=nrm[:, :], in_=scr[8 + hh:9 + hh, 0:TH].partition_broadcast(NP)), "lg", 16)
                ctx.mark(f"lg:nrm{hh}")
            ctx.wait(g, "lh:ln1rows")
            ctx.inc(g.dma_start(out=rt2[:, :], in_=scr[6:7, 0:TH].partition_broadcast(NP)), "lg", 16)
            ctx.inc(g.dma_start(out=st2[:, :], in_=scr[7:8, 0:TH].partition_broadcast(NP)), "lg", 16)
            ctx.mark("lg:rs2")
            for gi in range(TOTAL_P):
                e, p = divmod(gi, PASSES)
                slot = gi % 2
                if p == 0:
                    if e >= 1:
                        ctx.wait_cnt(g, "vo", e * PASSES * KT * NT4)
                    else:
                        ctx.wait(g, "ve:sdone")
                    ctx.inc(g.dma_start(out=sc_b[:, :],
                                        in_=scores_dram[e:e + 1, :].partition_broadcast(NP)), "lsc", 16)
                if gi == 0:
                    ctx.wait(g, "ve:sdone")
                if gi >= 2:
                    ctx.wait_cnt(g, "mmy", (gi - 1) * KT * NT4)
                i0 = p * PASS_I
                for kt in range(KT):
                    ctx.inc(g.dma_start(out=wg_b[slot][:, kt * PASS_I:(kt + 1) * PASS_I],
                                        in_=gw[e][kt * NP:(kt + 1) * NP, i0:i0 + PASS_I]), "lw", 16)
                    ctx.inc(g.dma_start(out=wu_b[slot][:, kt * PASS_I:(kt + 1) * PASS_I],
                                        in_=uw[e][kt * NP:(kt + 1) * NP, i0:i0 + PASS_I]), "lw", 16)
                for it in range(PI_MT):
                    ctx.inc(g.dma_start(out=wd_b[slot][:, it * D:(it + 1) * D],
                                        in_=dw[e][i0 + it * NP:i0 + (it + 1) * NP, :]), "lw", 16)

        def syn(sy):
            for ch in range(NT8):
                ctx.wait(sy, f"ve:sta{ch}")
                ctx.inc(sy.dma_start(out=scr[0:1, ch * NC:(ch + 1) * NC], in_=rowstage[0:1, 0:NC]), "lh", 16)
                ctx.mark(f"lh:sta{ch}")
                ctx.wait(sy, f"ve:stevac{ch}")
                ctx.inc(sy.dma_start(out=scr[4:5, ch * NC:(ch + 1) * NC], in_=rowstage[0:1, NC:2 * NC]), "lh", 16)
                ctx.mark(f"lh:st{ch}")
            ctx.inc(sy.dma_start(out=s128[:, 0:32], in_=scr[0:1, :].rearrange("o (p c) -> (o p) c", p=NP)), "lh", 16)
            ctx.inc(sy.dma_start(out=s128[:, 32:64], in_=scr[4:5, :].rearrange("o (p c) -> (o p) c", p=NP)), "lh", 16)
            ctx.mark("lh:ln0cols")
            ctx.wait(sy, "ve:ln0rs")
            ctx.inc(sy.dma_start(out=scr[2:3, :].rearrange("o (p c) -> (o p) c", p=NP), in_=s128[:, 32:64]), "lh", 16)
            ctx.inc(sy.dma_start(out=scr[3:4, :].rearrange("o (p c) -> (o p) c", p=NP), in_=s128[:, 64:96]), "lh", 16)
            ctx.mark("lh:ln0rows")
            if "d_xn" in dbg:
                ctx.wait(sy, "ve:xn_done")
                for kt in range(KT):
                    ctx.inc(sy.dma_start(out=dbg["d_xn"][kt * NP:(kt + 1) * NP, 0:TH],
                                         in_=xn_own[:, kt * TH:(kt + 1) * TH]), "st", 16)
                    ctx.inc(sy.dma_start(out=dbg["d_xn"][kt * NP:(kt + 1) * NP, TH:S],
                                         in_=xn_oth[:, kt * TH:(kt + 1) * TH]), "st", 16)
            if "d_spl" in dbg:
                ctx.wait(sy, f"ve:split{KT - 1}")
                ctx.inc(sy.dma_start(out=dbg["d_spl"][:, 0:KT * 8], in_=rwhi[:, :]), "st", 16)
                ctx.inc(sy.dma_start(out=dbg["d_spl"][:, KT * 8:KT * 16], in_=rwlo[:, :]), "st", 16)
            if "d_xnlo" in dbg:
                ctx.wait(sy, "ve:xn_done")
                for kt in range(KT):
                    ctx.inc(sy.dma_start(out=dbg["d_xnlo"][kt * NP:(kt + 1) * NP, :],
                                         in_=xnlo[:, kt * TH:(kt + 1) * TH]), "st", 16)
            if "d_ah" in dbg:
                ctx.wait(sy, "ve:ah_split")
                ctx.inc(sy.dma_start(out=dbg["d_ah"][:, :], in_=ah[:, :]), "st", 16)
            if "d_tm" in dbg:
                ctx.wait(sy, f"ve:tr{15}")
                ctx.inc(sy.dma_start(out=dbg["d_tm"][:, :], in_=tm[:, :]), "st", 16)
            if "d_scr" in dbg:
                ctx.wait(sy, "lh:ln0rows")
                ctx.inc(sy.dma_start(out=dbg["d_scr"][:, :], in_=scr[0:8, :]), "st", 16)
            ctx.wait(sy, "ve:scores")
            ctx.inc(sy.dma_start(out=scores_dram[:, :], in_=lgfm[0:8, :]), "lh", 16)
            ctx.mark("lh:scores")
            if "d_scores" in dbg:
                ctx.inc(sy.dma_start(out=dbg["d_scores"][:, :], in_=lgfm[0:8, :]), "st", 16)
            if "d_lg" in dbg:
                ctx.wait(sy, "ve:lgdone")
                ctx.inc(sy.dma_start(out=dbg["d_lg"][:, :], in_=lgfm[:, :]), "st", 16)
            if "d_k" in dbg:
                ctx.wait(sy, f"ve:kev{4 * NT8 - 1}")
                for mt in range(4):
                    ctx.inc(sy.dma_start(out=dbg["d_k"][mt * NP:(mt + 1) * NP, :], in_=fsl(kfm, mt, 0, S)), "st", 16)
            if "d_q" in dbg:
                ctx.wait(sy, f"ve:qev{QL}")
                for mt in range(4):
                    ctx.inc(sy.dma_start(out=dbg["d_q"][mt * NP:(mt + 1) * NP, :], in_=fsl(qfm, mt, 0, TH, TH)), "st", 16)
            ctx.wait(sy, "ve:bound")
            ctx.inc(sy.dma_start(out=scr[1:2, 0:4], in_=rowstage[0:1, 8:12]), "lh", 16)
            ctx.mark("lh:boundrow")
            for hh in range(H0):
                for qc in range(NT4):
                    ctx.wait(sy, f"ve:den{hh}_{qc}")
                    ctx.inc(sy.dma_start(out=scr[8 + hh:9 + hh, qc * NC:(qc + 1) * NC],
                                         in_=rowstage[0:1, (qc % 2) * NC:(qc % 2 + 1) * NC]), "lh", 16)
                    ctx.mark(f"lh:den{hh}_{qc}")
                ctx.inc(sy.dma_start(out=s128[:, 48:64],
                                     in_=scr[8 + hh:9 + hh, 0:TH].rearrange("o (p c) -> (o p) c", p=NP)), "lh", 16)
                ctx.mark(f"lh:dload{hh}")
                ctx.wait(sy, f"ve:recip{hh}")
                ctx.inc(sy.dma_start(out=scr[8 + hh:9 + hh, 0:TH].rearrange("o (p c) -> (o p) c", p=NP),
                                     in_=s128[:, 48:64]), "lh", 16)
                ctx.mark(f"lh:recip{hh}")
            if "d_o" in dbg:
                ctx.wait(sy, "ve:onorm7")
                for mt in range(4):
                    ctx.inc(sy.dma_start(out=dbg["d_o"][mt * NP:(mt + 1) * NP, :], in_=fsl(ofm, mt, 0, TH, TH)), "st", 16)
            for ch in range(NT4):
                ctx.wait(sy, f"ve:ysta{ch}")
                ctx.inc(sy.dma_start(out=scr[5:6, ch * NC:(ch + 1) * NC], in_=rowstage[0:1, 0:NC]), "lh", 16)
                ctx.mark(f"lh:ysta{ch}")
                ctx.wait(sy, f"ve:ystevac{ch}")
                ctx.inc(sy.dma_start(out=scr[5:6, TH + ch * NC:TH + (ch + 1) * NC], in_=rowstage[0:1, NC:2 * NC]), "lh", 16)
                ctx.mark(f"lh:yst{ch}")
            ctx.inc(sy.dma_start(out=s128[:, 0:16], in_=scr[5:6, 0:TH].rearrange("o (p c) -> (o p) c", p=NP)), "lh", 16)
            ctx.inc(sy.dma_start(out=s128[:, 16:32], in_=scr[5:6, TH:2 * TH].rearrange("o (p c) -> (o p) c", p=NP)), "lh", 16)
            ctx.mark("lh:ln1cols")
            ctx.wait(sy, "ve:ln1rs")
            ctx.inc(sy.dma_start(out=scr[6:7, 0:TH].rearrange("o (p c) -> (o p) c", p=NP), in_=s128[:, 16:32]), "lh", 16)
            ctx.inc(sy.dma_start(out=scr[7:8, 0:TH].rearrange("o (p c) -> (o p) c", p=NP), in_=s128[:, 32:48]), "lh", 16)
            ctx.mark("lh:ln1rows")
            for ntc in range(NT4):
                for mt in range(KT):
                    ctx.wait(sy, f"ve:s{ntc}_{mt}")
                    ctx.inc(sy.dma_start(out=s_dram[mt * NP:(mt + 1) * NP, ntc * NC:(ntc + 1) * NC],
                                         in_=ychunk[:, mt * NC:(mt + 1) * NC]), "lh", 16)
                    ctx.mark(f"lh:s{ntc}_{mt}")
            ctx.mark("lh:sspill")
            if "d_s" in dbg:
                for kt in range(KT):
                    ctx.inc(sy.dma_start(out=dbg["d_s"][kt * NP:(kt + 1) * NP, :],
                                         in_=s_dram[kt * NP:(kt + 1) * NP, :]), "st", 16)
            for kt in range(KT):
                if kt >= 1:
                    ctx.wait(sy, f"ve:x1_{kt - 1}")
                ctx.inc(sy.dma_start(out=xf32[:, :], in_=xt[kt * NP:(kt + 1) * NP, 0:TH]), "lh", 16)
                ctx.inc(sy.dma_start(out=sbb[:, :], in_=s_dram[kt * NP:(kt + 1) * NP, :]), "lh", 16)
                ctx.mark(f"lh:ep{kt}")
            if "d_moe" in dbg:
                ctx.wait_cnt(sy, "vo", TOTAL_P * KT * NT4)
                for kt in range(KT):
                    ctx.inc(sy.dma_start(out=dbg["d_moe"][kt * NP:(kt + 1) * NP, :], in_=fsl(moe_out, kt, 0, TH, TH)), "st", 16)
            for kt in range(KT):
                ctx.wait(sy, f"ve:x1_{kt}")
                ctx.inc(sy.dma_start(out=x1t[kt * NP:(kt + 1) * NP, :], in_=fsl(moe_out, kt, 0, TH, TH)), "st", 16)
            if real:
                sy.wait_ge(ctx.sem["st"], ctx.cnt["st"])

        def ten(t):
            ctx.wait(t, "lg:init")
            for ch in range(NT8):
                mi = accum_mm(t, p_row[0:1, :], [(onec[:, :], fsl(xbf, kt, ch * NC)) for kt in range(KT)])
                ctx.inc(mi, "mm", 1)
                ctx.mark(f"mm:sta{ch}")
                ctx.wait(t, f"ve:sta{ch}")
                ctx.wait(t, f"se:xsq{ch}")
                mi = accum_mm(t, p_row[0:1, :], [(onec[:, :], sq0[:, kt * NC:(kt + 1) * NC]) for kt in range(KT)])
                ctx.inc(mi, "mm", 1)
                ctx.mark(f"mm:st{ch}")
                ctx.wait(t, f"ve:stevac{ch}")
            ctx.wait(t, "ve:xn_done")
            ctx.wait(t, f"ve:split{KT - 1}")
            for ntc in range(NT4):
                pairs = []
                for kt in range(KT):
                    pairs += [(rwhi[:, kt * 8:(kt + 1) * 8], xnsl(kt, ntc * NC)),
                              (rwlo[:, kt * 8:(kt + 1) * 8], xnsl(kt, ntc * NC)),
                              (rwhi[:, kt * 8:(kt + 1) * 8], fsl(xnlo, kt, ntc * NC, NC, TH))]
                ctx.inc(accum_mm(t, bk[0][0:8, :], pairs), "mm", 1)
                ctx.mark(f"mm:rl{ntc}")
                pairs = []
                for kt in range(KT):
                    pairs += [(a1hi[:, kt * 64:(kt + 1) * 64], xnsl(kt, ntc * NC)),
                              (a1lo[:, kt * 64:(kt + 1) * 64], xnsl(kt, ntc * NC)),
                              (a1hi[:, kt * 64:(kt + 1) * 64], fsl(xnlo, kt, ntc * NC, NC, TH))]
                ctx.inc(accum_mm(t, bk[1][0:64, :], pairs), "mm", 1)
                ctx.mark(f"mm:a1_{ntc}")
                ctx.wait(t, f"ve:rlevac{ntc}")
            ctx.wait(t, "ve:ah_split")
            ctx.wait(t, "ve:a2split")
            for ntc in range(NT4):
                pairs = [(a2hi[:, :], ahhi[:, ntc * NC:(ntc + 1) * NC]),
                         (a2lo[:, :], ahhi[:, ntc * NC:(ntc + 1) * NC]),
                         (a2hi[:, :], ahlo[:, ntc * NC:(ntc + 1) * NC])]
                ctx.inc(accum_mm(t, bk[0][0:8, :], pairs), "mm", 1)
                ctx.mark(f"mm:a2_{ntc}")
                ctx.wait(t, f"ve:a2evac{ntc}")
            ctx.wait(t, "ve:lg_split")
            for blk in range(16):
                q0 = (blk % 2) * 80
                if blk % 2 == 0 and blk >= 2:
                    ctx.wait(t, f"ve:tr{blk - 1}")
                t.transpose(p_tr[:, q0:q0 + 40], lg_hi[:, blk * 128:(blk + 1) * 128], ident[0:40, 0:40])
                mi = t.transpose(p_tr[:, q0 + 40:q0 + 80], lg_lo[:, blk * 128:(blk + 1) * 128], ident[0:40, 0:40])
                ctx.inc(mi, "mm", 1)
                ctx.mark(f"mm:tr{blk}")
            for blk in range(16):
                ctx.wait(t, f"ve:scb{blk}")
                if blk >= 1:
                    ctx.wait(t, f"ve:sctr{blk - 1}")
                mi = t.transpose(p_tr[0:8, 160 + (blk % 2) * 128: 160 + (blk % 2) * 128 + 128],
                                 scb16[:, blk * 8:(blk + 1) * 8], ident[:, :])
                ctx.inc(mi, "mm", 1)
                ctx.mark(f"mm:sctr{blk}")
            ctx.wait(t, "lg:wqk")
            for mt in range(4):
                for ntc in range(NT8):
                    idx = mt * NT8 + ntc
                    pairs = [(wk_sb[:, kt * 512 + mt * NP: kt * 512 + (mt + 1) * NP], xnsl(kt, ntc * NC))
                             for kt in range(KT)]
                    ctx.inc(accum_mm(t, bk[idx % 2][:, :], pairs), "mm", 1)
                    ctx.mark(f"mm:k{idx}")
                    if idx >= 1:
                        ctx.wait(t, f"ve:kev{idx - 1}")
            for mt in range(4):
                for ntc in range(NT4):
                    idx = mt * NT4 + ntc
                    pairs = [(wq_sb[:, kt * 512 + mt * NP: kt * 512 + (mt + 1) * NP], xnsl(kt, ntc * NC))
                             for kt in range(KT)]
                    ctx.inc(accum_mm(t, bk[idx % 2][:, :], pairs), "mm", 1)
                    ctx.mark(f"mm:q{idx}")
                    if idx >= 1:
                        ctx.wait(t, f"ve:qev{idx - 1}")
            ctx.wait(t, "lg:wv")
            for jt in range(32):
                pairs = [(xnsl(kt, jt * NP, NP), wv_sb[:, kt * 512:(kt + 1) * 512])
                         for kt in range(KT)]
                ctx.inc(accum_mm(t, bk[2 + jt % 2][:, :], pairs), "mm", 1)
                ctx.mark(f"mm:v{jt}")
                if jt >= 1:
                    ctx.wait(t, f"ve:vev{jt - 1}")
            bnd = [("k", mt, ch) for mt in range(4) for ch in range(NT8)] + \
                  [("q", mt, ch) for mt in range(4) for ch in range(NT4)]
            for gidx, (tag, mt, ch) in enumerate(bnd):
                ctx.wait(t, f"se:{tag}sq{mt}_{ch}")
                mi = t.matmul(p_row[0:1, :], onec[:, :],
                              expb[gidx % 2][:, :], start=True, stop=True)
                ctx.inc(mi, "mm", 1)
                ctx.mark(f"mm:{tag}sq{mt}_{ch}")
                ctx.wait(t, f"ve:{tag}sqm{mt}_{ch}")
            ctx.wait(t, "lg:bound")
            ctx.wait(t, "ve:vones")
            for hh in range(H0):
                for qc in range(NT4):
                    for jt in range(32):
                        if jt >= 2:
                            ctx.wait(t, f"se:exp{hh}_{qc}_{jt - 2}")
                        mi = t.matmul(bk[jt % 2][:, :],
                                      kfm[hb(hh):hb(hh) + 64, (hh // 2) * S + jt * NP:(hh // 2) * S + (jt + 1) * NP],
                                      qfm[hb(hh):hb(hh) + 64, (hh // 2) * TH + qc * NC:(hh // 2) * TH + (qc + 1) * NC],
                                      start=True, stop=True)
                        ctx.inc(mi, "mm", 1)
                        ctx.mark(f"mm:sim{hh}_{qc}_{jt}")
                        ctx.wait(t, f"se:exp{hh}_{qc}_{jt}")
                        mi = t.matmul(bk[2][0:65, :], vtok[:, jt * 520 + hh * 65: jt * 520 + (hh + 1) * 65],
                                      expb[jt % 2][:, :], start=(jt == 0), stop=(jt == 31))
                        ctx.inc(mi, "mm", 1)
                        ctx.mark(f"mm:av{hh}_{qc}_{jt}")
                    ctx.wait(t, f"ve:oevac{hh}_{qc}")
            ctx.wait(t, "ve:onorm7")
            for ntc in range(NT4):
                for mt in range(KT):
                    pairs = [(wo_sb[:, kt * D + mt * NP: kt * D + (mt + 1) * NP],
                              fsl(ofm, kt, ntc * NC, NC, TH)) for kt in range(4)]
                    ctx.inc(accum_mm(t, bk[mt % 2][:, :], pairs), "mm", 1)
                    ctx.mark(f"mm:y1_{ntc}_{mt}")
                    if mt >= 1:
                        ctx.wait(t, f"ve:yev1_{ntc}_{mt - 1}")
                ctx.wait(t, f"ve:yev1_{ntc}_{KT - 1}")
                mi = accum_mm(t, p_row[0:1, :], [(onec[:, :], ychunk[:, mt2 * NC:(mt2 + 1) * NC]) for mt2 in range(KT)])
                ctx.inc(mi, "mm", 1)
                ctx.mark(f"mm:ysta{ntc}")
                ctx.wait(t, f"ve:ysta{ntc}")
                for mt2 in range(KT):
                    ctx.wait(t, f"se:ysq{ntc}_{mt2}")
                    mi = t.matmul(p_row[0:1, :], onec[:, :], expb[mt2 % 2][:, :],
                                  start=(mt2 == 0), stop=(mt2 == KT - 1))
                    ctx.inc(mi, "mm", 1)
                    ctx.mark(f"mm:ysqc{ntc}_{mt2}")
                ctx.mark(f"mm:yst{ntc}")
                ctx.wait(t, f"ve:ystevac{ntc}")
            ctx.wait(t, "lg:rs2")
            for ntc in range(NT4):
                for mt in range(KT):
                    pairs = [(wo_sb[:, kt * D + mt * NP: kt * D + (mt + 1) * NP],
                              fsl(ofm, kt, ntc * NC, NC, TH)) for kt in range(4)]
                    ctx.inc(accum_mm(t, bk[mt % 2][:, :], pairs), "mm", 1)
                    ctx.mark(f"mm:y2_{ntc}_{mt}")
                    if mt >= 1:
                        ctx.wait(t, f"ve:s{ntc}_{mt - 1}")
                ctx.wait(t, f"ve:s{ntc}_{KT - 1}")
            n_gu, n_y = 0, 0
            for gi in range(TOTAL_P):
                slot = gi % 2
                ctx.wait_cnt(t, "lw", 16 * (2 * KT + PI_MT) * (gi + 1))
                for it in range(PI_MT):
                    for ntc in range(NT4):
                        par = (it * NT4 + ntc) % 2
                        if n_gu >= 2:
                            ctx.wait_cnt(t, "vi", n_gu - 1)
                        accum_mm(t, bk[par][:, :],
                                 [(wg_b[slot][:, kt * PASS_I + it * NP: kt * PASS_I + (it + 1) * NP],
                                   xnsl(kt, ntc * NC)) for kt in range(KT)])
                        mi = accum_mm(t, bk[2 + par][:, :],
                                      [(wu_b[slot][:, kt * PASS_I + it * NP: kt * PASS_I + (it + 1) * NP],
                                        xnsl(kt, ntc * NC)) for kt in range(KT)])
                        ctx.inc(mi, "mm", 1)
                        ctx.mark(f"mm:gu{n_gu}")
                        n_gu += 1
                ctx.wait_cnt(t, "vi", n_gu)
                for mt in range(KT):
                    for ntc in range(NT4):
                        par = (mt * NT4 + ntc) % 2
                        if n_y >= 2:
                            ctx.wait_cnt(t, "vo", n_y - 1)
                        mi = accum_mm(t, bk[4 + par][:, :],
                                      [(wd_b[slot][:, it * D + mt * NP: it * D + (mt + 1) * NP],
                                        inter[:, it * TH + ntc * NC: it * TH + (ntc + 1) * NC])
                                       for it in range(PI_MT)])
                        ctx.inc(mi, "mmy", 1)
                        n_y += 1

        def sca(s):
            ctx.wait(s, "lg:init")
            for ch in range(NT8):
                if ch >= 1:
                    ctx.wait(s, f"mm:st{ch - 1}")
                mi = None
                for kt in range(KT):
                    mi = s.activation(sq0[:, kt * NC:(kt + 1) * NC], fsl(xbf, kt, ch * NC), AF.Square)
                ctx.inc(mi, "se", 1)
                ctx.mark(f"se:xsq{ch}")
            ctx.wait(s, "ve:ln0var")
            ctx.inc(s.activation(s128[:, 32:64], s128[:, 32:64], AF.Sqrt, bias=cols[:, 38:39]), "se", 1)
            ctx.mark("se:ln0sqrt")
            ctx.wait(s, "ve:ah_raw")
            ctx.inc(s.activation(ah2[:, :], ah[:, :], AF.Square), "se", 1)
            ctx.mark("se:ah2")
            ctx.wait(s, "ve:ah_u")
            ctx.inc(s.activation(ah2[:, :], ah2[:, :], AF.Tanh, scale=0.7978845608028654), "se", 1)
            ctx.mark("se:ah_tanh")
            for blk in range(16):
                ctx.wait(s, f"ve:tmmax{blk}")
                ctx.inc(s.activation(tmw[:, blk * 16:blk * 16 + 8], tm[:, blk * 40:blk * 40 + 8], AF.Exp,
                                     bias=tmc[:, blk * 8:blk * 8 + 1],
                                     accum_out=tmc[:, blk * 8 + 1:blk * 8 + 2]), "se", 1)
                ctx.mark(f"se:pexp{blk}")
            ctx.wait(s, f"mm:q{QL}")
            bnd = [("k", mt, ch, kfm, S) for mt in range(4) for ch in range(NT8)] + \
                  [("q", mt, ch, qfm, TH) for mt in range(4) for ch in range(NT4)]
            for gidx, (tag, mt, ch, buf, Tlen) in enumerate(bnd):
                nt_ = NT8 if tag == "k" else NT4
                ctx.wait(s, f"ve:{tag}ev{mt * nt_ + ch}")
                if gidx >= 2:
                    pt, pm, pc, _, _ = bnd[gidx - 2]
                    ctx.wait(s, f"mm:{pt}sq{pm}_{pc}")
                ctx.inc(s.activation(expb[gidx % 2][:, :], fsl(buf, mt, ch * NC, NC, Tlen), AF.Square), "se", 1)
                ctx.mark(f"se:{tag}sq{mt}_{ch}")
            ctx.wait(s, "ve:boundprod")
            ctx.inc(s.activation(rowstage[0:1, 28:32], rowstage[0:1, 28:32], AF.Sqrt), "se", 1)
            ctx.mark("se:bsqrt")
            ctx.wait(s, "lg:bound")
            for hh in range(H0):
                for qc in range(NT4):
                    for jt in range(32):
                        ctx.wait(s, f"mm:sim{hh}_{qc}_{jt}")
                        if jt >= 2:
                            ctx.wait(s, f"mm:av{hh}_{qc}_{jt - 2}")
                        ctx.inc(s.activation(expb[jt % 2][:, :], bk[jt % 2][:, :], AF.Exp,
                                             scale=0.125, bias=cols[:, 34 + hh // 2:35 + hh // 2]), "se", 1)
                        ctx.mark(f"se:exp{hh}_{qc}_{jt}")
            for ntc in range(NT4):
                ctx.wait(s, f"mm:ysta{ntc}")
                for mt in range(KT):
                    if mt >= 2:
                        ctx.wait(s, f"mm:ysqc{ntc}_{mt - 2}")
                    elif ntc >= 1 and mt < 2:
                        ctx.wait(s, f"mm:ysqc{ntc - 1}_{KT - 2 + mt}")
                    ctx.inc(s.activation(expb[mt % 2][:, :], ychunk[:, mt * NC:(mt + 1) * NC], AF.Square), "se", 1)
                    ctx.mark(f"se:ysq{ntc}_{mt}")
            ctx.wait(s, "ve:ln1var")
            ctx.inc(s.activation(s128[:, 16:32], s128[:, 16:32], AF.Sqrt, bias=cols[:, 38:39]), "se", 1)
            ctx.mark("se:ln1sqrt")
            n_gu = 0
            for gi in range(TOTAL_P):
                for it in range(PI_MT):
                    for ntc in range(NT4):
                        par = (it * NT4 + ntc) % 2
                        ctx.wait(s, f"mm:gu{n_gu}")
                        ctx.inc(s.activation(inter[:, it * TH + ntc * NC: it * TH + (ntc + 1) * NC],
                                             bk[par][:, :], AF.Silu), "silu", 1)
                        n_gu += 1
            ctx.wait_cnt(s, "vo", TOTAL_P * KT * NT4)
            for kt in range(KT):
                if kt >= 1:
                    ctx.wait(s, f"ve:gused{kt - 1}")
                mi = s.activation(gtmp[:, :], fsl(moe_out, kt, 0, TH, TH), AF.Silu)
                ctx.inc(mi, "se", 1)
                ctx.mark(f"se:g{kt}")

        def vec(v):
            v.memset(cols[:, 38:39], LN_EPS)
            for ch in range(NT8):
                ctx.wait(v, f"mm:sta{ch}")
                if ch >= 1:
                    ctx.wait(v, f"lh:sta{ch - 1}")
                mi = v.tensor_copy(rowstage[0:1, 0:NC], p_row[0:1, :])
                ctx.inc(mi, "ve", 1)
                ctx.mark(f"ve:sta{ch}")
                ctx.wait(v, f"mm:st{ch}")
                if ch >= 1:
                    ctx.wait(v, f"lh:st{ch - 1}")
                mi = v.tensor_copy(rowstage[0:1, NC:2 * NC], p_row[0:1, :])
                ctx.inc(mi, "ve", 1)
                ctx.mark(f"ve:stevac{ch}")
            ctx.wait(v, "lh:ln0cols")
            v.tensor_scalar(s128[:, 0:32], s128[:, 0:32], 1.0 / D, None, ALU.mult)
            v.tensor_scalar(s128[:, 32:64], s128[:, 32:64], 1.0 / D, None, ALU.mult)
            v.tensor_mul(s128[:, 64:96], s128[:, 0:32], s128[:, 0:32])
            v.tensor_copy(cols[:, 39:40], cols[:, 38:39])
            mi = v.tensor_sub(s128[:, 32:64], s128[:, 32:64], s128[:, 64:96])
            ctx.inc(mi, "ve", 1)
            ctx.mark("ve:ln0var")
            ctx.wait(v, "se:ln0sqrt")
            v.reciprocal(s128[:, 32:64], s128[:, 32:64])
            v.tensor_copy(cols[:, 39:40], cols[:, 38:39])
            mi = v.tensor_mul(s128[:, 64:96], s128[:, 0:32], s128[:, 32:64])
            ctx.inc(mi, "ve", 1)
            ctx.mark("ve:ln0rs")
            ctx.wait(v, "lg:rs")
            for kt in range(KT):
                for cc in range(NT4):
                    mo = bk[3][:, :]
                    c0 = cc * NC
                    v.tensor_mul(mo, fsl(xbf, kt, c0), rt0[:, c0:c0 + NC])
                    v.tensor_sub(mo, mo, st0[:, c0:c0 + NC])
                    v.tensor_copy(xn_own[:, kt * TH + c0: kt * TH + c0 + NC], mo)
                    v.tensor_sub(fsl(xnlo, kt, c0, NC, TH), mo, xn_own[:, kt * TH + c0: kt * TH + c0 + NC])
                for cc in range(NT4):
                    c0 = TH + cc * NC
                    dst = xn_oth[:, kt * TH + cc * NC: kt * TH + (cc + 1) * NC]
                    v.tensor_mul(dst, fsl(xbf, kt, c0), rt0[:, c0:c0 + NC])
                    mi = v.tensor_sub(dst, dst, st0[:, c0:c0 + NC])
            ctx.inc(mi, "ve", 1)
            ctx.mark("ve:xn_done")
            v.memset(lgfm[:, :], 0.0)
            for kt in range(KT):
                ctx.wait(v, f"lg:wf{kt}")
                v.tensor_copy(rwhi[:, kt * 8:(kt + 1) * 8], wf32[:, 0:8])
                v.tensor_copy(a1hi[:, kt * 64:(kt + 1) * 64], wf32[:, 8:72])
                v.scalar_tensor_tensor(rwlo[:, kt * 8:(kt + 1) * 8], rwhi[:, kt * 8:(kt + 1) * 8],
                                       -1.0, wf32[:, 0:8], ALU.mult, ALU.add)
                mi = v.scalar_tensor_tensor(a1lo[:, kt * 64:(kt + 1) * 64], a1hi[:, kt * 64:(kt + 1) * 64],
                                            -1.0, wf32[:, 8:72], ALU.mult, ALU.add)
                ctx.inc(mi, "ve", 1)
                ctx.mark(f"ve:split{kt}")
            ctx.wait(v, "lg:a2")
            v.tensor_copy(a2hi[:, :], wf32[0:64, 0:8])
            v.tensor_copy(cols[:, 39:40], cols[:, 38:39])
            mi = v.scalar_tensor_tensor(a2lo[:, :], a2hi[:, :], -1.0, wf32[0:64, 0:8], ALU.mult, ALU.add)
            ctx.inc(mi, "ve", 1)
            ctx.mark("ve:a2split")
            for ntc in range(NT4):
                ctx.wait(v, f"mm:rl{ntc}")
                v.tensor_copy(lgfm[0:8, ntc * NC:(ntc + 1) * NC], bk[0][0:8, :])
                ctx.wait(v, f"mm:a1_{ntc}")
                mi = v.tensor_scalar(ah[:, ntc * NC:(ntc + 1) * NC], bk[1][0:64, :],
                                     cols[0:64, 32:33], None, ALU.add)
                ctx.inc(mi, "ve", 1)
                ctx.mark(f"ve:rlevac{ntc}")
            ctx.mark("ve:ah_raw")
            ctx.wait(v, "se:ah2")
            v.tensor_mul(ah2[:, :], ah2[:, :], ah[:, :])
            mi = v.scalar_tensor_tensor(ah2[:, :], ah2[:, :], 0.044715, ah[:, :], ALU.mult, ALU.add)
            ctx.inc(mi, "ve", 1)
            ctx.mark("ve:ah_u")
            ctx.wait(v, "se:ah_tanh")
            v.tensor_scalar(ah2[:, :], ah2[:, :], 1.0, None, ALU.add)
            v.tensor_mul(ah[:, :], ah[:, :], ah2[:, :])
            v.tensor_scalar(ah[:, :], ah[:, :], 0.5, None, ALU.mult)
            v.tensor_copy(ahhi[:, :], ah[:, :])
            v.tensor_copy(cols[:, 39:40], cols[:, 38:39])
            mi = v.scalar_tensor_tensor(ahlo[:, :], ahhi[:, :], -1.0, ah[:, :], ALU.mult, ALU.add)
            ctx.inc(mi, "ve", 1)
            ctx.mark("ve:ah_split")
            for ntc in range(NT4):
                ctx.wait(v, f"mm:a2_{ntc}")
                v.tensor_scalar(lgfm[32:40, ntc * NC:(ntc + 1) * NC], bk[0][0:8, :],
                                cols[0:8, 33:34], None, ALU.add)
                mi = v.tensor_scalar(lgfm[32:40, ntc * NC:(ntc + 1) * NC],
                                     lgfm[32:40, ntc * NC:(ntc + 1) * NC], 30.0, -30.0, ALU.min, ALU.max)
                ctx.inc(mi, "ve", 1)
                ctx.mark(f"ve:a2evac{ntc}")
            ctx.mark("ve:lgdone")
            v.tensor_copy(lg_hi[:, :], lgfm[:, :])
            mi = v.scalar_tensor_tensor(lg_lo[:, :], lg_hi[:, :], -1.0, lgfm[:, :], ALU.mult, ALU.add)
            ctx.inc(mi, "ve", 1)
            ctx.mark("ve:lg_split")
            for bp in range(8):
                B0, B1 = 2 * bp, 2 * bp + 1
                def _c(blk):
                    return dict(b16=blk * 16, b40=blk * 40, b8=blk * 8, q0=(blk % 2) * 80,
                                tx=(blk % 2) * 8)
                cA, cB = _c(B0), _c(B1)
                ctx.wait(v, f"mm:tr{B1}")
                v.tensor_copy(tm[:, cA["b40"]:cA["b40"] + 40], p_tr[:, cA["q0"]:cA["q0"] + 40])
                v.tensor_copy(tm[:, cB["b40"]:cB["b40"] + 40], p_tr[:, cB["q0"]:cB["q0"] + 40])
                mi = v.tensor_add(tm[:, cA["b40"]:cA["b40"] + 40], tm[:, cA["b40"]:cA["b40"] + 40],
                                  p_tr[:, cA["q0"] + 40:cA["q0"] + 80])
                ctx.inc(mi, "ve", 1)
                ctx.mark(f"ve:tr{B0}")
                mi = v.tensor_add(tm[:, cB["b40"]:cB["b40"] + 40], tm[:, cB["b40"]:cB["b40"] + 40],
                                  p_tr[:, cB["q0"] + 40:cB["q0"] + 80])
                ctx.inc(mi, "ve", 1)
                ctx.mark(f"ve:tr{B1}")
                for c in (cA, cB):
                    v.reduce_max(tmc[:, c["b8"]:c["b8"] + 1], tm[:, c["b40"]:c["b40"] + 8], axis=AX.X)
                for c, Bx in ((cA, B0), (cB, B1)):
                    mi = v.tensor_scalar(tmc[:, c["b8"]:c["b8"] + 1], tmc[:, c["b8"]:c["b8"] + 1], -1.0, None, ALU.mult)
                    ctx.inc(mi, "ve", 1)
                    ctx.mark(f"ve:tmmax{Bx}")
                ctx.wait(v, f"se:pexp{B0}")
                v.reciprocal(tmc[:, cA["b8"] + 2:cA["b8"] + 3], tmc[:, cA["b8"] + 1:cA["b8"] + 2])
                ctx.wait(v, f"se:pexp{B1}")
                v.reciprocal(tmc[:, cB["b8"] + 2:cB["b8"] + 3], tmc[:, cB["b8"] + 1:cB["b8"] + 2])
                for c in (cA, cB):
                    v.tensor_scalar(tmw[:, c["b16"]:c["b16"] + 8], tmw[:, c["b16"]:c["b16"] + 8],
                                    tmc[:, c["b8"] + 2:c["b8"] + 3], None, ALU.mult)
                for c in (cA, cB):
                    v.memset(tmw[:, c["b16"] + 8:c["b16"] + 16], 0.0)
                for ep in range(8):
                    for c in (cA, cB):
                        v.tensor_scalar(tmx[:, c["tx"]:c["tx"] + 8], tm[:, c["b40"]:c["b40"] + 8],
                                        tm[:, c["b40"] + ep:c["b40"] + ep + 1], None, ALU.is_lt)
                    for c in (cA, cB):
                        v.tensor_add(tmw[:, c["b16"] + 8:c["b16"] + 16], tmw[:, c["b16"] + 8:c["b16"] + 16],
                                     tmx[:, c["tx"]:c["tx"] + 8])
                for c in (cA, cB):
                    v.reduce_max(tmc[:, c["b8"] + 3:c["b8"] + 4], tm[:, c["b40"] + 32:c["b40"] + 40], axis=AX.X)
                for c in (cA, cB):
                    v.tensor_scalar(tmx[:, c["tx"]:c["tx"] + 8], tm[:, c["b40"] + 32:c["b40"] + 40],
                                    tmc[:, c["b8"] + 3:c["b8"] + 4], None, ALU.is_ge)
                for c in (cA, cB):
                    v.tensor_mul(tmx[:, c["tx"]:c["tx"] + 8], tmx[:, c["tx"]:c["tx"] + 8], argw[:, :])
                for c in (cA, cB):
                    v.reduce_max(tmc[:, c["b8"] + 4:c["b8"] + 5], tmx[:, c["tx"]:c["tx"] + 8], axis=AX.X)
                for c in (cA, cB):
                    v.tensor_scalar(tmc[:, c["b8"] + 5:c["b8"] + 6], tmc[:, c["b8"] + 4:c["b8"] + 5],
                                    -1.0, 7.0, ALU.mult, ALU.add)
                for c in (cA, cB):
                    v.tensor_scalar(tmx[:, c["tx"]:c["tx"] + 8], tmw[:, c["b16"] + 8:c["b16"] + 16],
                                    tmc[:, c["b8"] + 5:c["b8"] + 6], None, ALU.is_lt)
                for c in (cA, cB):
                    v.tensor_mul(tmw[:, c["b16"]:c["b16"] + 8], tmw[:, c["b16"]:c["b16"] + 8],
                                 tmx[:, c["tx"]:c["tx"] + 8])
                for c, Bx in ((cA, B0), (cB, B1)):
                    mi = v.tensor_copy(scb16[:, c["b8"]:c["b8"] + 8], tmw[:, c["b16"]:c["b16"] + 8])
                    ctx.inc(mi, "ve", 1)
                    ctx.mark(f"ve:scb{Bx}")
            for blk in range(16):
                ctx.wait(v, f"mm:sctr{blk}")
                mi = v.tensor_copy(lgfm[0:8, blk * 128:(blk + 1) * 128],
                                   p_tr[0:8, 160 + (blk % 2) * 128: 160 + (blk % 2) * 128 + 128])
                ctx.inc(mi, "ve", 1)
                ctx.mark(f"ve:sctr{blk}")
            ctx.mark("ve:scores")
            for nt_, tag, buf, Tlen, bcol in ((NT8, "k", kfm, S, 4), (NT4, "q", qfm, TH, 0)):
                for mt in range(4):
                    for ntc in range(nt_):
                        idx = mt * nt_ + ntc
                        ctx.wait(v, f"mm:{tag}{idx}")
                        if tag == "q" and idx == 0:
                            ctx.wait(v, "lh:scores")
                        mi = v.tensor_scalar(fsl(buf, mt, ntc * NC, NC, Tlen), bk[idx % 2][:, :],
                                             cols[:, bcol + mt:bcol + mt + 1], None, ALU.add)
                        ctx.inc(mi, "ve", 1)
                        ctx.mark(f"ve:{tag}ev{idx}")
            ctx.wait(v, "lg:wv")
            v3 = vtok[:].rearrange("p (j h s) -> p j h s", h=8, s=65) if real else _Mock()
            ctx.inc(v.memset(v3[:, :, :, 64:65] if real else v3, 1.0), "ve", 1)
            ctx.mark("ve:vones")
            for jt in range(32):
                ctx.wait(v, f"mm:v{jt}")
                mi = None
                for hh in range(H0):
                    mi = v.tensor_add(vtok[:, jt * 520 + hh * 65: jt * 520 + hh * 65 + 64],
                                      bk[2 + jt % 2][:, hh * 64:(hh + 1) * 64], bvt[:, hh * 64:(hh + 1) * 64])
                ctx.inc(mi, "ve", 1)
                ctx.mark(f"ve:vev{jt}")
            for nt_, tag, base in ((NT8, "k", 12), (NT4, "q", 24)):
                for mt in range(4):
                    for ch in range(nt_):
                        ctx.wait(v, f"mm:{tag}sq{mt}_{ch}")
                        if ch == 0:
                            mi = v.reduce_max(rowstage[0:1, base + mt:base + mt + 1], p_row[0:1, :], axis=AX.X)
                        else:
                            v.reduce_max(rowstage[0:1, 20:21], p_row[0:1, :], axis=AX.X)
                            v.tensor_copy(cols[:, 39:40], cols[:, 38:39])
                            mi = v.tensor_max(rowstage[0:1, base + mt:base + mt + 1],
                                              rowstage[0:1, base + mt:base + mt + 1], rowstage[0:1, 20:21])
                        ctx.inc(mi, "ve", 1)
                        ctx.mark(f"ve:{tag}sqm{mt}_{ch}")
            mi = v.tensor_mul(rowstage[0:1, 28:32], rowstage[0:1, 12:16], rowstage[0:1, 24:28])
            ctx.inc(mi, "ve", 1)
            ctx.mark("ve:boundprod")
            ctx.wait(v, "se:bsqrt")
            mi = v.tensor_scalar(rowstage[0:1, 8:12], rowstage[0:1, 28:32], -0.125, None, ALU.mult)
            ctx.inc(mi, "ve", 1)
            ctx.mark("ve:bound")
            for hh in range(H0):
                for qc in range(NT4):
                    ctx.wait(v, f"mm:av{hh}_{qc}_31")
                    if qc >= 2:
                        ctx.wait(v, f"lh:den{hh}_{qc - 2}")
                    ctx.inc(v.tensor_copy(rowstage[0:1, (qc % 2) * NC:(qc % 2 + 1) * NC], bk[2][64:65, :]), "ve", 1)
                    ctx.mark(f"ve:den{hh}_{qc}")
                    mi = v.tensor_copy(ofm[hb(hh):hb(hh) + 64, (hh // 2) * TH + qc * NC:(hh // 2) * TH + (qc + 1) * NC],
                                       bk[2][0:64, :])
                    ctx.inc(mi, "ve", 1)
                    ctx.mark(f"ve:oevac{hh}_{qc}")
                ctx.wait(v, f"lh:dload{hh}")
                mi = v.reciprocal(s128[:, 48:64], s128[:, 48:64])
                ctx.inc(mi, "ve", 1)
                ctx.mark(f"ve:recip{hh}")
                ctx.wait(v, f"lg:nrm{hh}")
                mi = v.tensor_mul(ofm[hb(hh):hb(hh) + 64, (hh // 2) * TH: (hh // 2) * TH + TH],
                                  ofm[hb(hh):hb(hh) + 64, (hh // 2) * TH: (hh // 2) * TH + TH],
                                  nrm[hb(hh):hb(hh) + 64, :])
                ctx.inc(mi, "ve", 1)
                ctx.mark(f"ve:onorm{hh}")
            for ntc in range(NT4):
                for mt in range(KT):
                    ctx.wait(v, f"mm:y1_{ntc}_{mt}")
                    mi = v.tensor_scalar(ychunk[:, mt * NC:(mt + 1) * NC], bk[mt % 2][:, :],
                                         cols[:, 8 + mt:9 + mt], None, ALU.add)
                    ctx.inc(mi, "ve", 1)
                    ctx.mark(f"ve:yev1_{ntc}_{mt}")
                ctx.wait(v, f"mm:ysta{ntc}")
                if ntc >= 1:
                    ctx.wait(v, f"lh:ysta{ntc - 1}")
                mi = v.tensor_copy(rowstage[0:1, 0:NC], p_row[0:1, :])
                ctx.inc(mi, "ve", 1)
                ctx.mark(f"ve:ysta{ntc}")
                ctx.wait(v, f"mm:yst{ntc}")
                if ntc >= 1:
                    ctx.wait(v, f"lh:yst{ntc - 1}")
                mi = v.tensor_copy(rowstage[0:1, NC:2 * NC], p_row[0:1, :])
                ctx.inc(mi, "ve", 1)
                ctx.mark(f"ve:ystevac{ntc}")
            ctx.wait(v, "lh:ln1cols")
            v.tensor_scalar(s128[:, 0:16], s128[:, 0:16], 1.0 / D, None, ALU.mult)
            v.tensor_scalar(s128[:, 16:32], s128[:, 16:32], 1.0 / D, None, ALU.mult)
            v.tensor_mul(s128[:, 32:48], s128[:, 0:16], s128[:, 0:16])
            v.tensor_copy(cols[:, 39:40], cols[:, 38:39])
            mi = v.tensor_sub(s128[:, 16:32], s128[:, 16:32], s128[:, 32:48])
            ctx.inc(mi, "ve", 1)
            ctx.mark("ve:ln1var")
            ctx.wait(v, "se:ln1sqrt")
            v.reciprocal(s128[:, 16:32], s128[:, 16:32])
            v.tensor_copy(cols[:, 39:40], cols[:, 38:39])
            mi = v.tensor_mul(s128[:, 32:48], s128[:, 0:16], s128[:, 16:32])
            ctx.inc(mi, "ve", 1)
            ctx.mark("ve:ln1rs")
            v.tensor_scalar(cols[:, 40:48], cols[:, 16:24], RSQRT2, None, ALU.mult)
            v.tensor_scalar(cols[:, 48:56], cols[:, 24:32], RSQRT2, None, ALU.mult)
            for ntc in range(NT4):
                for mt in range(KT):
                    ctx.wait(v, f"mm:y2_{ntc}_{mt}")
                    if ntc >= 1:
                        ctx.wait(v, f"lh:s{ntc - 1}_{mt}")
                    ysl = ychunk[:, mt * NC:(mt + 1) * NC]
                    v.tensor_scalar(scs[:, :], bk[mt % 2][:, :], cols[:, 8 + mt:9 + mt], None, ALU.add)
                    v.tensor_mul(scs[:, :], scs[:, :], rt2[:, ntc * NC:(ntc + 1) * NC])
                    v.tensor_sub(scs[:, :], scs[:, :], st2[:, ntc * NC:(ntc + 1) * NC])
                    v.tensor_scalar(scs[:, :], scs[:, :], cols[:, 40 + mt:41 + mt], cols[:, 48 + mt:49 + mt],
                                    ALU.mult, ALU.add)
                    mi = v.scalar_tensor_tensor(ysl, xnsl(mt, ntc * NC), RSQRT2, scs[:, :], ALU.mult, ALU.add)
                    ctx.inc(mi, "ve", 1)
                    ctx.mark(f"ve:s{ntc}_{mt}")
            ctx.mark("ve:sdone")
            ctx.wait(v, "lh:sspill")
            n_gu, n_vo = 0, 0
            for gi in range(TOTAL_P):
                e, p = divmod(gi, PASSES)
                if p == 0:
                    ctx.wait_cnt(v, "lsc", 16 * (e + 1))
                for it in range(PI_MT):
                    for ntc in range(NT4):
                        par = (it * NT4 + ntc) % 2
                        ctx.wait_cnt(v, "silu", n_gu + 1)
                        sl = slice(it * TH + ntc * NC, it * TH + (ntc + 1) * NC)
                        ctx.inc(v.tensor_mul(inter[:, sl], inter[:, sl], bk[2 + par][:, :]), "vi", 1)
                        n_gu += 1
                for mt in range(KT):
                    for ntc in range(NT4):
                        par = (mt * NT4 + ntc) % 2
                        ctx.wait_cnt(v, "mmy", n_vo + 1)
                        osl = slice(mt * TH + ntc * NC, mt * TH + (ntc + 1) * NC)
                        scrow = sc_b[:, ntc * NC:(ntc + 1) * NC]
                        if gi == 0:
                            ctx.inc(v.tensor_mul(moe_out[:, osl], bk[4 + par][:, :], scrow), "vo", 1)
                        else:
                            v.tensor_mul(bk[4 + par][:, :], bk[4 + par][:, :], scrow)
                            ctx.inc(v.tensor_add(moe_out[:, osl], moe_out[:, osl], bk[4 + par][:, :]), "vo", 1)
                        n_vo += 1
            for kt in range(KT):
                ctx.wait(v, f"lh:ep{kt}")
                ctx.wait(v, f"se:g{kt}")
                mo = fsl(moe_out, kt, 0, TH, TH)
                ctx.inc(v.tensor_mul(mo, sbb[:, :], gtmp[:, :]), "ve", 1)
                ctx.mark(f"ve:gused{kt}")
                mi = v.tensor_add(mo, mo, xf32[:, :])
                ctx.inc(mi, "ve", 1)
                ctx.mark(f"ve:x1_{kt}")

        for name, fn in (("gpsimd", gps), ("sync", syn), ("scalar", sca),
                         ("tensor", ten), ("vector", vec)):
            if real:
                getattr(block, name)(fn)
            else:
                fn(_Mock())
        es.close()

    emit(False)
    emit(False)
    nc = bass.Bass("TRN2", target_bir_lowering=False, debug=False)
    emit(True, nc)
    return nc


def _np(x):
    return np.ascontiguousarray(x, dtype=np.float32)


def host_inputs_a(inp, core):
    b, h = core // 2, core % 2
    x = np.asarray(inp["L0_attn_x"])[b]
    if h == 1:
        x = np.concatenate([x[TH:], x[:TH]], axis=0)
    W_qk = np.asarray(inp["L0_attn_W_qk"])
    b_qk = np.asarray(inp["L0_attn_b_qk"])
    d = {
        "xt": _np(x.T),
        "wq": _np(W_qk[:, 0::2]),
        "wk": _np(W_qk[:, 1::2]),
        "wv": _np(inp["L0_attn_W_v"]),
        "wout": _np(inp["L0_attn_W_out"]),
        "bq_c": _np(b_qk[0::2].reshape(4, NP).T),
        "bk_c": _np(b_qk[1::2].reshape(4, NP).T),
        "bv_row": _np(np.asarray(inp["L0_attn_b_v"]).reshape(1, 512)),
        "bout_c": _np(np.asarray(inp["L0_attn_b_out"]).reshape(8, NP).T),
        "gam_c": _np(np.asarray(inp["L0_attn_ln_gamma"]).reshape(8, NP).T),
        "bet_c": _np(np.asarray(inp["L0_attn_ln_beta"]).reshape(8, NP).T),
        "rw_t": _np(np.asarray(inp["L0_ffn_router_w"]).T),
        "a1_t": _np(np.pad(np.asarray(inp["L0_ffn_actor_w1"]), ((0, 14), (0, 0))).T),
        "a2_t": _np(np.pad(np.asarray(inp["L0_ffn_actor_w2"]), ((0, 2), (0, 14))).T),
        "b1_c": _np(np.pad(np.asarray(inp["L0_ffn_actor_b1"]), (0, 14)).reshape(64, 1)),
        "b2_c": _np(np.concatenate([np.asarray(inp["L0_ffn_actor_b2"]),
                                    np.full(2, -1e30, np.float32)]).reshape(8, 1)),
        "gw": _np(np.asarray(inp["L0_ffn_gate_w"]).transpose(0, 2, 1)),
        "uw": _np(np.asarray(inp["L0_ffn_up_w"]).transpose(0, 2, 1)),
        "dw": _np(np.asarray(inp["L0_ffn_down_w"]).transpose(0, 2, 1)),
        "onec_in": np.ones((NP, 1), ml_dtypes.bfloat16),
        "argw_in": _np(np.array([[6, 5, 4, 3, 2, 1, 0, 0]], np.float32)),
        "ident_in": np.eye(NP, dtype=ml_dtypes.bfloat16),
    }
    return d


CH = 64
CH_N = S // CH


def build_launch_b(debug=()):
    ctx = Ctx(["lg", "lh", "mm", "ve", "se", "st", "so"])

    def emit(real, nc=None):
        es = ExitStack()
        if real:
            def di(name, shape, dt=F32):
                return nc.dram_tensor(name, shape, dt, kind="ExternalInput").ap()
            sbuf = lambda name, shape, dt: es.enter_context(nc.sbuf_tensor(name, shape, dt))
            psum = lambda name, shape, dt=F32: es.enter_context(nc.psum_tensor(name, shape, dt))
        else:
            di = lambda *a, **k: _Mock()
            sbuf = lambda *a, **k: _Mock()
            psum = lambda *a, **k: _Mock()

        x1t = di("x1t", [D, S])
        wqp = di("wqp", [D, 512]); wkp = di("wkp", [D, 512]); wvp = di("wvp", [D, 512])
        wbp = di("wbp", [D, 2])
        cq_c = di("cq_c", [NP, 16]); ck_c = di("ck_c", [NP, 16]); cv_c = di("cv_c", [NP, 16])
        eyeb_in = di("eyeb_in", [NP, 64], BF16)
        masks_in = di("masks_in", [NP, 128], BF16)
        if real:
            dseqo = [nc.dram_tensor(nm, [S, 256], BF16, kind="ExternalOutput").ap()
                     for nm in ("dseqA", "dseqB")]
            vfm_out = nc.dram_tensor("vfm_out", [512, S], BF16, kind="ExternalOutput").ap()
            scr = nc.dram_tensor("scr", [16, S], F32, kind="Internal").ap()
            dbg = {}
            for nm in debug:
                shape = {"d_xn1": [D, S], "d_q": [512, S], "d_k": [512, S],
                         "d_beta": [NP, 64], "d_s0": [256, 256], "d_scr": [16, S]}[nm]
                dt_ = BF16 if nm in ("d_xn1", "d_q", "d_k", "d_s0") else F32
                dbg[nm] = nc.dram_tensor(nm, shape, dt_, kind="ExternalOutput").ap()
        else:
            dseqo = [_Mock(), _Mock()]
            vfm_out = scr = _Mock()
            dbg = {nm: _Mock() for nm in debug}

        cols = sbuf("cols", [NP, 64], F32)
        onec = sbuf("onec", [NP, 1], BF16)
        eyeb = sbuf("eyeb", [NP, 64], BF16)
        masks = sbuf("masks", [NP, 128], BF16)
        s128 = sbuf("s128", [NP, 96], F32)
        rowstage = sbuf("rowstage", [1, 2 * NC], F32)
        bcol = sbuf("bcol", [NP, 64], BF16)
        bcolf = sbuf("bcolf", [NP, 64], F32)
        bstg = sbuf("bstg", [32, NP], BF16)
        def scoped(es_, side="left"):
            return (lambda name, shape, dt: es_.enter_context(nc.sbuf_tensor(name, shape, dt, side=side))) if real else (lambda *a: _Mock())
        es_pw = ExitStack(); sb_pw = scoped(es_pw)
        wq_sb = sb_pw("wq_sb", [NP, KT * 512], BF16)
        wk_sb = sb_pw("wk_sb", [NP, KT * 512], BF16)
        wv_sb = sb_pw("wv_sb", [NP, KT * 512], BF16)
        wb_sb = sb_pw("wb_sb", [NP, KT * 2], BF16)
        praw = sb_pw("praw", [NP, S + 8], BF16)
        es_xn = ExitStack(); sb_xn = scoped(es_xn)
        xn1 = sb_xn("xn1", [NP, KT * S], BF16)
        es_x1 = ExitStack(); sb_x1 = scoped(es_x1, "right")
        x1bf = sb_x1("x1bf", [NP, KT * S], BF16)
        rt0 = sb_x1("rt0", [NP, S], BF16)
        st0 = sb_x1("st0", [NP, S], BF16)
        sq0 = sb_x1("sq0", [NP, 8 * NC], BF16)
        if real:
            es_x1.close()
        es_qk = ExitStack(); sb_qk = scoped(es_qk, "right")
        qfm = sb_qk("qfm", [NP, 4 * S], BF16)
        kfm = sb_qk("kfm", [NP, 4 * S], BF16)
        vfm = sb_qk("vfm", [NP, 4 * S], BF16)
        if real:
            es_xn.close()
            es_pw.close()
        nrmt = sbuf("nrmt", [NP, S], BF16)
        sq1 = sbuf("sq1", [NP, 2 * NC], BF16)
        kseq = [sbuf(f"kseq{h}", [NP, 32 * 256], BF16) for h in range(2)]
        vseq = [sbuf(f"vseq{h}", [NP, 32 * 256], BF16) for h in range(2)]
        dstg = [sbuf(f"dstg{h}", [NP, 256], BF16) for h in range(2)]
        S_sb = [sbuf(f"S_sb{h}", [NP, 512], BF16) for h in range(2)]
        kbc = [sbuf(f"kbc{h}", [NP, 256], BF16) for h in range(2)]
        ui_sb = [sbuf(f"ui{h}", [NP, 256], BF16) for h in range(2)]
        wtn = [sbuf(f"wtn{h}", [NP, 128], BF16) for h in range(2)]
        L_sb = [sbuf(f"L_sb{h}", [NP, 64], BF16) for h in range(2)]
        LT_sb = [sbuf(f"LT_sb{h}", [NP, 64], BF16) for h in range(2)]
        P_sb = [[sbuf(f"P{h}_{i}", [NP, 64], BF16) for i in range(2)] for h in range(2)]
        Q_sb = [[sbuf(f"Q{h}_{i}", [NP, 64], BF16) for i in range(2)] for h in range(2)]
        MT_sb = [sbuf(f"MT{h}", [NP, 64], BF16) for h in range(2)]
        atn = [sbuf(f"atn{h}", [NP, 64], BF16) for h in range(2)]

        bk = [psum(f"bk{i}", [NP, NC]) for i in range(4)]
        S_ps = [psum(f"S_ps{i}", [NP, 512]) for i in range(2)]
        p_row = psum("p_row", [1, NC])
        p_trB = psum("p_trB", [NP, 64], BF16)

        ctx.start(real, nc, es)
        block = es.enter_context(nc.Block()) if real else _Mock()
        fsl = lambda buf, kt, c0, w=NC, Tlen=S: buf[:, kt * Tlen + c0: kt * Tlen + c0 + w]
        def cseq(buf, c, w=256):
            return buf[(c % 2) * 64:(c % 2) * 64 + 64, (c // 2) * w:(c // 2 + 1) * w]
        def bslice(h, c):
            return bcolf[(c % 2) * 64:(c % 2) * 64 + 64, h * 32 + c // 2: h * 32 + c // 2 + 1]
        LQ = 3 * 4 * NT8 - 1

        def gps(g):
            for kt in range(KT):
                ctx.inc(g.dma_start(out=fsl(x1bf, kt, 0, S), in_=x1t[kt * NP:(kt + 1) * NP, :]), "lg", 16)
            ctx.inc(g.dma_start(out=eyeb[:], in_=eyeb_in), "lg", 16)
            ctx.inc(g.dma_start(out=masks[:], in_=masks_in), "lg", 16)
            ctx.inc(g.dma_start(out=cols[:, 0:16], in_=cq_c), "lg", 16)
            ctx.inc(g.dma_start(out=cols[:, 16:32], in_=ck_c), "lg", 16)
            ctx.inc(g.dma_start(out=cols[:, 32:48], in_=cv_c), "lg", 16)
            for w_d, w_s, wd in ((wqp, wq_sb, 512), (wkp, wk_sb, 512), (wvp, wv_sb, 512), (wbp, wb_sb, 2)):
                for kt in range(KT):
                    ctx.inc(g.dma_start(out=w_s[:, kt * wd:(kt + 1) * wd],
                                        in_=w_d[kt * NP:(kt + 1) * NP, :]), "lg", 16)
            ctx.mark("lg:init")
            ctx.wait(g, "lh:ln0rows")
            ctx.inc(g.dma_start(out=rt0[:], in_=scr[2:3, :].partition_broadcast(NP)), "lg", 16)
            ctx.inc(g.dma_start(out=st0[:], in_=scr[3:4, :].partition_broadcast(NP)), "lg", 16)
            ctx.mark("lg:rs")
            for h in range(2):
                ctx.wait(g, f"lh:bev{h}_{NT8 - 1}")
                if h == 1:
                    ctx.wait(g, "lh:btr0")
                ctx.inc(g.dma_start(out=bstg[:, :],
                                    in_=scr[4 + h:5 + h, :].rearrange("o (c p) -> (o c) p", p=NP)), "lg", 16)
                ctx.mark(f"lg:bst{h}")
            for i in range(4):
                ctx.wait(g, f"lh:l2r{i}")
                if i >= 1:
                    ctx.wait(g, f"ve:l2s{i - 1}")
                ctx.inc(g.dma_start(out=nrmt[:, :], in_=scr[10 + i:11 + i, :].partition_broadcast(NP)), "lg", 16)
                ctx.mark(f"lg:nrm{i}")

        def syn(sy):
            for ch in range(NT8):
                ctx.wait(sy, f"ve:sta{ch}")
                ctx.inc(sy.dma_start(out=scr[0:1, ch * NC:(ch + 1) * NC], in_=rowstage[0:1, 0:NC]), "lh", 16)
                ctx.mark(f"lh:sta{ch}")
                ctx.wait(sy, f"ve:stevac{ch}")
                ctx.inc(sy.dma_start(out=scr[1:2, ch * NC:(ch + 1) * NC], in_=rowstage[0:1, NC:2 * NC]), "lh", 16)
                ctx.mark(f"lh:st{ch}")
            ctx.wait(sy, "lh:st7")
            ctx.inc(sy.dma_start(out=s128[:, 0:32], in_=scr[0:1, :].rearrange("o (p c) -> (o p) c", p=NP)), "lh", 16)
            ctx.inc(sy.dma_start(out=s128[:, 32:64], in_=scr[1:2, :].rearrange("o (p c) -> (o p) c", p=NP)), "lh", 16)
            ctx.mark("lh:ln0cols")
            ctx.wait(sy, "ve:ln0rs")
            ctx.inc(sy.dma_start(out=scr[2:3, :].rearrange("o (p c) -> (o p) c", p=NP), in_=s128[:, 32:64]), "lh", 16)
            ctx.inc(sy.dma_start(out=scr[3:4, :].rearrange("o (p c) -> (o p) c", p=NP), in_=s128[:, 64:96]), "lh", 16)
            ctx.mark("lh:ln0rows")
            if "d_xn1" in dbg:
                ctx.wait(sy, "ve:xn_done")
                for kt in range(KT):
                    ctx.inc(sy.dma_start(out=dbg["d_xn1"][kt * NP:(kt + 1) * NP, :], in_=fsl(xn1, kt, 0, S)), "st", 16)
            for h in range(2):
                for ch in range(NT8):
                    ctx.wait(sy, f"ve:bev{h}_{ch}")
                    ctx.inc(sy.dma_start(out=scr[4 + h:5 + h, ch * NC:(ch + 1) * NC],
                                         in_=rowstage[0:1, (ch % 2) * NC:(ch % 2 + 1) * NC]), "lh", 16)
                    ctx.mark(f"lh:bev{h}_{ch}")
            for h in range(2):
                ctx.wait(sy, f"lg:bst{h}")
                ctx.inc(sy.dma_start_transpose(bcol[:, h * 32:(h + 1) * 32], bstg[:, :]), "lh", 16)
                ctx.mark(f"lh:btr{h}")
            ctx.mark("lh:bcol")
            if "d_beta" in dbg:
                ctx.wait(sy, "se:bsig")
                ctx.inc(sy.dma_start(out=dbg["d_beta"], in_=bcolf[:, 0:64]), "st", 16)
            for i in range(4):
                for ch in range(NT8):
                    ctx.wait(sy, f"ve:l2e{i}_{ch}")
                    ctx.inc(sy.dma_start(out=scr[6 + i:7 + i, ch * NC:(ch + 1) * NC],
                                         in_=rowstage[0:1, (ch % 2) * NC:(ch % 2 + 1) * NC]), "lh", 16)
                    ctx.mark(f"lh:l2e{i}_{ch}")
                if i >= 1:
                    ctx.wait(sy, f"se:l2sq{i - 1}")
                ctx.wait(sy, f"lh:l2e{i}_{NT8 - 1}")
                ctx.inc(sy.dma_start(out=s128[:, 0:32],
                                     in_=scr[6 + i:7 + i, :].rearrange("o (p c) -> (o p) c", p=NP)), "lh", 16)
                ctx.mark(f"lh:l2c{i}")
                ctx.wait(sy, f"ve:l2r{i}")
                ctx.inc(sy.dma_start(out=scr[10 + i:11 + i, :].rearrange("o (p c) -> (o p) c", p=NP),
                                     in_=s128[:, 64:96]), "lh", 16)
                ctx.mark(f"lh:l2r{i}")
            ctx.wait(sy, f"mm:b1_{NT8 - 1}")
            ctx.wait(sy, "se:vsilu")
            for h in range(2):
                for tj in range(32):
                    for dt2 in range(2):
                        ctx.inc(sy.dma_start_transpose(
                            vseq[h][:, tj * 256 + dt2 * 128: tj * 256 + (dt2 + 1) * 128],
                            vfm[:, (2 * h + dt2) * S + tj * NP: (2 * h + dt2) * S + (tj + 1) * NP]), "lh", 16)
                ctx.mark(f"lh:vseq{h}")
            for h in range(2):
                ctx.wait(sy, f"ve:l2s{1 + 2 * h}")
                for tj in range(32):
                    for dt2 in range(2):
                        ctx.inc(sy.dma_start_transpose(
                            kseq[h][:, tj * 256 + dt2 * 128: tj * 256 + (dt2 + 1) * 128],
                            kfm[:, (2 * h + dt2) * S + tj * NP: (2 * h + dt2) * S + (tj + 1) * NP]), "lh", 16)
                ctx.mark(f"lh:kseq{h}")
            for mt in range(4):
                ctx.inc(sy.dma_start(out=vfm_out[mt * NP:(mt + 1) * NP, :], in_=fsl(vfm, mt, 0, S)), "st", 16)
            if "d_q" in dbg:
                ctx.wait(sy, "ve:l2s3")
                for mt in range(4):
                    ctx.inc(sy.dma_start(out=dbg["d_q"][mt * NP:(mt + 1) * NP, :], in_=fsl(qfm, mt, 0, S)), "st", 16)
            if "d_k" in dbg:
                ctx.wait(sy, "ve:l2s3")
                for mt in range(4):
                    ctx.inc(sy.dma_start(out=dbg["d_k"][mt * NP:(mt + 1) * NP, :], in_=fsl(kfm, mt, 0, S)), "st", 16)
            for c in range(CH_N):
                b0 = (c % 2) * 64
                for h in range(2):
                    ctx.wait(sy, f"ve:oev{h}_{c}")
                    ctx.inc(sy.dma_start(out=dseqo[h][c * CH:(c + 1) * CH, :],
                                         in_=dstg[h][b0:b0 + 64, :]), "so", 16)
            if "d_scr" in dbg:
                ctx.wait(sy, "ve:l2s3")
                ctx.inc(sy.dma_start(out=dbg["d_scr"], in_=scr[:, :]), "st", 16)
            if "d_s0" in dbg:
                ctx.wait(sy, f"ve:sref0_{CH_N - 1}")
                out_ap = dbg["d_s0"].rearrange("(a p) d -> p (a d)", p=NP) if real else dbg["d_s0"]
                ctx.inc(sy.dma_start(out=out_ap, in_=S_sb[0][:, :]), "st", 16)
            if real:
                sy.wait_ge(ctx.sem["st"], ctx.cnt["st"])
                sy.wait_ge(ctx.sem["so"], ctx.cnt["so"])

        def ten(t):
            ctx.wait(t, "lg:init")
            ctx.wait(t, "ve:ones")
            for ch in range(NT8):
                mi = accum_mm(t, p_row[0:1, :], [(onec[:, :], fsl(x1bf, kt, ch * NC)) for kt in range(KT)])
                ctx.inc(mi, "mm", 1)
                ctx.mark(f"mm:sta{ch}")
                ctx.wait(t, f"ve:sta{ch}")
                ctx.wait(t, f"se:xsq{ch}")
                mi = accum_mm(t, p_row[0:1, :], [(onec[:, :], sq0[:, kt * NC:(kt + 1) * NC]) for kt in range(KT)])
                ctx.inc(mi, "mm", 1)
                ctx.mark(f"mm:st{ch}")
                ctx.wait(t, f"ve:stevac{ch}")
            ctx.wait(t, "ve:xn_done")
            for wi, w_s in enumerate((wq_sb, wk_sb, wv_sb)):
                for mt in range(4):
                    for ntc in range(NT8):
                        idx = wi * 4 * NT8 + mt * NT8 + ntc
                        pairs = [(w_s[:, kt * 512 + mt * NP: kt * 512 + (mt + 1) * NP], fsl(xn1, kt, ntc * NC))
                                 for kt in range(KT)]
                        ctx.inc(accum_mm(t, bk[idx % 2][:, :], pairs), "mm", 1)
                        ctx.mark(f"mm:p{idx}")
                        if idx >= 1:
                            ctx.wait(t, f"ve:pev{idx - 1}")
            ctx.wait(t, f"ve:pev{LQ}")
            for h in range(2):
                for ch in range(NT8):
                    pairs = [(wb_sb[:, kt * 2 + h: kt * 2 + h + 1], fsl(xn1, kt, ch * NC)) for kt in range(KT)]
                    ctx.inc(accum_mm(t, p_row[0:1, :], pairs), "mm", 1)
                    ctx.mark(f"mm:b{h}_{ch}")
                    ctx.wait(t, f"ve:bev{h}_{ch}")
            for i in range(4):
                for ch in range(NT8):
                    ctx.wait(t, f"se:l2q{i}_{ch}")
                    mi = accum_mm(t, p_row[0:1, :], [(onec[:, :], sq1[:, j * NC:(j + 1) * NC]) for j in range(2)])
                    ctx.inc(mi, "mm", 1)
                    ctx.mark(f"mm:l2{i}_{ch}")
                    ctx.wait(t, f"ve:l2e{i}_{ch}")
            for w8 in ("lh:bcol", "se:bsig", "ve:l2s3", "lh:vseq1", "lh:kseq1",
                       "ve:vb0", "ve:vb1", "ve:s00", "ve:s01"):
                ctx.wait(t, w8)
            for c in range(CH_N):
                b0 = (c % 2) * 64
                hs = lambda buf: buf[b0:b0 + 64, :]
                for h in range(2):
                    w0, w1 = bk[2 * h], bk[2 * h + 1]
                    tA, tB = 2 * h, 2 * h + 1
                    ksl = lambda tt: kfm[:, tt * S + c * CH: tt * S + (c + 1) * CH]
                    qsl = lambda tt: qfm[:, tt * S + c * CH: tt * S + (c + 1) * CH]
                    mi = accum_mm(t, w0[b0:b0 + 64, 0:64], [(ksl(tA), ksl(tA)), (ksl(tB), ksl(tB))])
                    ctx.inc(mi, "mm", 1)
                    ctx.mark(f"mm:L{h}_{c}")
                    ctx.wait(t, f"ve:L{h}_{c}")
                    if not (c == 0 and h == 0):
                        ph, pc = (0, c) if h == 1 else (1, c - 1)
                        ctx.wait(t, f"ve:LT{ph}_{pc}")
                    mi = t.transpose(p_trB[b0:b0 + 64, :], hs(L_sb[h]), hs(eyeb))
                    ctx.inc(mi, "mm", 1)
                    ctx.mark(f"mm:LT{h}_{c}")
                    for j in range(1, 6):
                        pp = hs(P_sb[h][(j - 1) % 2]); qq = hs(Q_sb[h][(j - 1) % 2])
                        pn = hs(P_sb[h][j % 2])
                        if j == 1:
                            pp, qq = hs(L_sb[h]), hs(LT_sb[h])
                        ctx.wait(t, f"ve:MT{h}_{c}_{j - 1}" if j > 1 else f"ve:LT{h}_{c}")
                        mi = t.matmul(w0[b0:b0 + 64, 0:64], qq, pp, start=True, stop=True)
                        ctx.inc(mi, "mm", 1)
                        ctx.mark(f"mm:P{h}_{c}_{j}")
                        ctx.wait(t, f"ve:P{h}_{c}_{j}")
                        mi = t.matmul(w0[b0:b0 + 64, 0:64], pp, qq, start=True, stop=True)
                        ctx.inc(mi, "mm", 1)
                        ctx.mark(f"mm:Q{h}_{c}_{j}")
                        ctx.wait(t, f"ve:Q{h}_{c}_{j}")
                        mi = t.matmul(w0[b0:b0 + 64, 0:64], pn, hs(MT_sb[h]), start=True, stop=True)
                        ctx.inc(mi, "mm", 1)
                        ctx.mark(f"mm:M{h}_{c}_{j}")
                        ctx.wait(t, f"ve:MT{h}_{c}_{j}")
                    mi = t.matmul(w0[b0:b0 + 64, 0:256], hs(MT_sb[h]), cseq(vseq[h], c), start=True, stop=False)
                    ctx.inc(mi, "mm", 1)
                    ctx.mark(f"mm:u{h}_{c}")
                    ctx.wait(t, f"ve:kbc{h}_{c}")
                    t.matmul(w1[0:128, 0:64], kbc[h][b0:b0 + 64, 0:128], hs(MT_sb[h]), start=True, stop=True)
                    mi = t.matmul(w1[0:128, 64:128], kbc[h][b0:b0 + 64, 128:256], hs(MT_sb[h]), start=True, stop=True)
                    ctx.inc(mi, "mm", 1)
                    ctx.mark(f"mm:wT{h}_{c}")
                    ctx.wait(t, f"se:wtn{h}_{c}")
                    t.matmul(w0[b0:b0 + 64, 0:256], wtn[h][:, 0:64], S_sb[h][:, 0:256], start=False, stop=False)
                    mi = t.matmul(w0[b0:b0 + 64, 0:256], wtn[h][:, 64:128], S_sb[h][:, 256:512], start=False, stop=True)
                    ctx.inc(mi, "mm", 1)
                    ctx.mark(f"mm:ui{h}_{c}")
                    ctx.wait(t, f"ve:ui{h}_{c}")
                    mi = accum_mm(t, w1[b0:b0 + 64, 0:64], [(ksl(tA), qsl(tA)), (ksl(tB), qsl(tB))])
                    ctx.inc(mi, "mm", 1)
                    ctx.mark(f"mm:at{h}_{c}")
                    ctx.wait(t, f"ve:at{h}_{c}")
                    t.matmul(w1[b0:b0 + 64, 0:256], qsl(tA), S_sb[h][:, 0:256], start=True, stop=False)
                    t.matmul(w1[b0:b0 + 64, 0:256], qsl(tB), S_sb[h][:, 256:512], start=False, stop=False)
                    mi = t.matmul(w1[b0:b0 + 64, 0:256], hs(atn[h]), ui_sb[h][b0:b0 + 64, :], start=False, stop=True)
                    ctx.inc(mi, "mm", 1)
                    ctx.mark(f"mm:oi{h}_{c}")
                    t.matmul(S_ps[h][:, 0:256], cseq(kseq[h], c)[:, 0:128], ui_sb[h][b0:b0 + 64, :],
                             start=(c == 0), stop=False, skip_group_check=True)
                    mi = t.matmul(S_ps[h][:, 256:512], cseq(kseq[h], c)[:, 128:256], ui_sb[h][b0:b0 + 64, :],
                                  start=False, stop=False, skip_group_check=True)
                    ctx.inc(mi, "mm", 1)
                    ctx.mark(f"mm:sup{h}_{c}")
                    ctx.wait(t, f"ve:sref{h}_{c}")

        def sca(s):
            ctx.wait(s, "lg:init")
            for ch in range(NT8):
                if ch >= 1:
                    ctx.wait(s, f"mm:st{ch - 1}")
                mi = None
                for kt in range(KT):
                    mi = s.activation(sq0[:, kt * NC:(kt + 1) * NC], fsl(x1bf, kt, ch * NC), AF.Square)
                ctx.inc(mi, "se", 1)
                ctx.mark(f"se:xsq{ch}")
            ctx.wait(s, "ve:ln0var")
            ctx.inc(s.activation(s128[:, 32:64], s128[:, 32:64], AF.Sqrt, bias=cols[:, 62:63]), "se", 1)
            ctx.mark("se:ln0sqrt")
            for nm, buf in (("q", qfm), ("k", kfm), ("v", vfm)):
                ctx.wait(s, f"ve:conv{nm}")
                mi = None
                for mt in range(4):
                    mi = s.activation(fsl(buf, mt, 0, S, S), fsl(buf, mt, 0, S, S), AF.Silu)
                ctx.inc(mi, "se", 1)
                ctx.mark(f"se:{nm}silu")
            ctx.wait(s, "lh:bcol")
            ctx.inc(s.activation(bcolf[:, 0:64], bcol[:, 0:64], AF.Sigmoid), "se", 1)
            ctx.mark("se:bsig")
            for i, (buf, t0) in enumerate(((qfm, 0), (kfm, 0), (qfm, 2), (kfm, 2))):
                for ch in range(NT8):
                    if i == 0 and ch == 0:
                        ctx.wait(s, f"mm:b1_{NT8 - 1}")
                    else:
                        pi, pch = (i, ch - 1) if ch >= 1 else (i - 1, NT8 - 1)
                        ctx.wait(s, f"mm:l2{pi}_{pch}")
                    s.activation(sq1[:, 0:NC], fsl(buf, t0, ch * NC), AF.Square)
                    s.activation(sq1[:, NC:2 * NC], fsl(buf, t0 + 1, ch * NC), AF.Square)
                    mi = s.activation(cols[:, 60:61], cols[:, 62:63], AF.Copy)
                    ctx.inc(mi, "se", 1)
                    ctx.mark(f"se:l2q{i}_{ch}")
                ctx.wait(s, f"lh:l2c{i}")
                if i >= 1:
                    ctx.wait(s, f"ve:l2r{i - 1}")
                ctx.inc(s.activation(s128[:, 32:64], s128[:, 0:32], AF.Sqrt, bias=cols[:, 63:64]), "se", 1)
                ctx.mark(f"se:l2sq{i}")
            for c in range(CH_N):
                for h in range(2):
                    ctx.wait(s, f"mm:wT{h}_{c}")
                    ctx.inc(s.activation(wtn[h][:, :], bk[2 * h + 1][:, 0:128], AF.Copy, scale=-1.0), "se", 1)
                    ctx.mark(f"se:wtn{h}_{c}")

        def vec(v):
            v.memset(cols[:, 62:63], LN_EPS)
            v.memset(cols[:, 63:64], 1e-12)
            v.memset(praw[:, :], 0.0)
            ctx.inc(v.memset(onec[:], 1.0), "ve", 1)
            ctx.mark("ve:ones")
            for ch in range(NT8):
                ctx.wait(v, f"mm:sta{ch}")
                if ch >= 1:
                    ctx.wait(v, f"lh:sta{ch - 1}")
                ctx.inc(v.tensor_copy(rowstage[0:1, 0:NC], p_row[0:1, :]), "ve", 1)
                ctx.mark(f"ve:sta{ch}")
                ctx.wait(v, f"mm:st{ch}")
                if ch >= 1:
                    ctx.wait(v, f"lh:st{ch - 1}")
                ctx.inc(v.tensor_copy(rowstage[0:1, NC:2 * NC], p_row[0:1, :]), "ve", 1)
                ctx.mark(f"ve:stevac{ch}")
            ctx.wait(v, "lh:ln0cols")
            v.tensor_scalar(s128[:, 0:32], s128[:, 0:32], 1.0 / D, None, ALU.mult)
            v.tensor_scalar(s128[:, 32:64], s128[:, 32:64], 1.0 / D, None, ALU.mult)
            v.tensor_mul(s128[:, 64:96], s128[:, 0:32], s128[:, 0:32])
            v.tensor_copy(cols[:, 61:62], cols[:, 62:63])
            ctx.inc(v.tensor_sub(s128[:, 32:64], s128[:, 32:64], s128[:, 64:96]), "ve", 1)
            ctx.mark("ve:ln0var")
            ctx.wait(v, "se:ln0sqrt")
            v.reciprocal(s128[:, 32:64], s128[:, 32:64])
            v.tensor_copy(cols[:, 61:62], cols[:, 62:63])
            ctx.inc(v.tensor_mul(s128[:, 64:96], s128[:, 0:32], s128[:, 32:64]), "ve", 1)
            ctx.mark("ve:ln0rs")
            ctx.wait(v, "lg:rs")
            mi = None
            for kt in range(KT):
                for cc in range(NT8):
                    mo = bk[3][:, :]
                    c0 = cc * NC
                    v.tensor_mul(mo, fsl(x1bf, kt, c0), rt0[:, c0:c0 + NC])
                    v.tensor_sub(mo, mo, st0[:, c0:c0 + NC])
                    mi = v.tensor_copy(fsl(xn1, kt, c0), mo)
            ctx.inc(mi, "ve", 1)
            ctx.mark("ve:xn_done")
            for wi, (nm, buf) in enumerate((("q", qfm), ("k", kfm), ("v", vfm))):
                ccol = [0, 16, 32][wi]
                for mt in range(4):
                    for ntc in range(NT8):
                        idx = wi * 4 * NT8 + mt * NT8 + ntc
                        ctx.wait(v, f"mm:p{idx}")
                        mi = v.tensor_copy(praw[:, 8 + ntc * NC: 8 + (ntc + 1) * NC], bk[idx % 2][:, :])
                        ctx.inc(mi, "ve", 1)
                        ctx.mark(f"ve:pev{idx}")
                    v.tensor_scalar(fsl(buf, mt, 0, S, S), praw[:, 5: 5 + S],
                                    cols[:, ccol + mt * 4: ccol + mt * 4 + 1], None, ALU.mult)
                    for tau in range(1, 4):
                        mi = v.scalar_tensor_tensor(fsl(buf, mt, 0, S, S), praw[:, 5 + tau: 5 + tau + S],
                                                    cols[:, ccol + mt * 4 + tau: ccol + mt * 4 + tau + 1],
                                                    fsl(buf, mt, 0, S, S), ALU.mult, ALU.add)
                ctx.inc(mi, "ve", 1)
                ctx.mark(f"ve:conv{nm}")
            for h in range(2):
                for ch in range(NT8):
                    ctx.wait(v, f"mm:b{h}_{ch}")
                    if h == 0 and ch <= 1:
                        ctx.wait(v, "lh:sta7" if ch == 0 else "lh:st7")
                    else:
                        ph, pch = (h, ch - 2) if ch >= 2 else (h - 1, NT8 - 2 + ch)
                        ctx.wait(v, f"lh:bev{ph}_{pch}")
                    mi = v.tensor_copy(rowstage[0:1, (ch % 2) * NC:(ch % 2 + 1) * NC], p_row[0:1, :])
                    ctx.inc(mi, "ve", 1)
                    ctx.mark(f"ve:bev{h}_{ch}")
            for i in range(4):
                for ch in range(NT8):
                    ctx.wait(v, f"mm:l2{i}_{ch}")
                    if i == 0 and ch <= 1:
                        ctx.wait(v, f"lh:bev1_{NT8 - 2 + ch}")
                    else:
                        pi, pch = (i, ch - 2) if ch >= 2 else (i - 1, NT8 - 2 + ch)
                        ctx.wait(v, f"lh:l2e{pi}_{pch}")
                    mi = v.tensor_copy(rowstage[0:1, (ch % 2) * NC:(ch % 2 + 1) * NC], p_row[0:1, :])
                    ctx.inc(mi, "ve", 1)
                    ctx.mark(f"ve:l2e{i}_{ch}")
                ctx.wait(v, f"se:l2sq{i}")
                mi = v.reciprocal(s128[:, 64:96], s128[:, 32:64])
                ctx.inc(mi, "ve", 1)
                ctx.mark(f"ve:l2r{i}")
                buf, t0 = ((qfm, 0), (kfm, 0), (qfm, 2), (kfm, 2))[i]
                ctx.wait(v, f"lg:nrm{i}")
                ctx.wait(v, "se:qsilu")
                ctx.wait(v, "se:ksilu")
                v.tensor_mul(fsl(buf, t0, 0, S, S), fsl(buf, t0, 0, S, S), nrmt[:, :])
                mi = v.tensor_mul(fsl(buf, t0 + 1, 0, S, S), fsl(buf, t0 + 1, 0, S, S), nrmt[:, :])
                ctx.inc(mi, "ve", 1)
                ctx.mark(f"ve:l2s{i}")
            for h in range(2):
                ctx.wait(v, f"lh:vseq{h}")
                ctx.wait(v, "se:bsig")
                mi = None
                for tj in range(32):
                    mi = v.tensor_scalar(vseq[h][:, tj * 256:(tj + 1) * 256], vseq[h][:, tj * 256:(tj + 1) * 256],
                                         bcolf[:, h * 32 + tj: h * 32 + tj + 1], None, ALU.mult)
                ctx.inc(mi, "ve", 1)
                ctx.mark(f"ve:vb{h}")
            for h in range(2):
                ctx.inc(v.memset(S_sb[h][:, :], 0.0), "ve", 1)
                ctx.mark(f"ve:s0{h}")
            for c in range(CH_N):
                b0 = (c % 2) * 64
                hs = lambda buf: buf[b0:b0 + 64, :]
                for h in range(2):
                    w0, w1 = bk[2 * h], bk[2 * h + 1]
                    ctx.wait(v, f"mm:L{h}_{c}")
                    mi = v.scalar_tensor_tensor(hs(L_sb[h]), w0[b0:b0 + 64, 0:64], bslice(h, c),
                                                masks[b0:b0 + 64, 0:64], ALU.mult, ALU.mult)
                    ctx.inc(mi, "ve", 1)
                    ctx.mark(f"ve:L{h}_{c}")
                    ctx.wait(v, f"mm:LT{h}_{c}")
                    v.tensor_copy(hs(LT_sb[h]), p_trB[b0:b0 + 64, :])
                    v.tensor_copy(cols[:, 61:62], cols[:, 62:63])
                    ctx.inc(v.tensor_add(hs(MT_sb[h]), hs(LT_sb[h]), hs(eyeb)), "ve", 1)
                    ctx.mark(f"ve:LT{h}_{c}")
                    ctx.inc(v.tensor_scalar(hs(kbc[h]), cseq(kseq[h], c), bslice(h, c), None, ALU.mult), "ve", 1)
                    ctx.mark(f"ve:kbc{h}_{c}")
                    for j in range(1, 6):
                        pn = hs(P_sb[h][j % 2]); qn = hs(Q_sb[h][j % 2])
                        ctx.wait(v, f"mm:P{h}_{c}_{j}")
                        ctx.inc(v.tensor_copy(pn, w0[b0:b0 + 64, 0:64]), "ve", 1)
                        ctx.mark(f"ve:P{h}_{c}_{j}")
                        ctx.wait(v, f"mm:Q{h}_{c}_{j}")
                        ctx.inc(v.tensor_copy(qn, w0[b0:b0 + 64, 0:64]), "ve", 1)
                        ctx.mark(f"ve:Q{h}_{c}_{j}")
                        ctx.wait(v, f"mm:M{h}_{c}_{j}")
                        ctx.inc(v.tensor_add(hs(MT_sb[h]), hs(MT_sb[h]), w0[b0:b0 + 64, 0:64]), "ve", 1)
                        ctx.mark(f"ve:MT{h}_{c}_{j}")
                    ctx.wait(v, f"mm:ui{h}_{c}")
                    ctx.inc(v.tensor_copy(ui_sb[h][b0:b0 + 64, :], w0[b0:b0 + 64, 0:256]), "ve", 1)
                    ctx.mark(f"ve:ui{h}_{c}")
                    ctx.wait(v, f"mm:at{h}_{c}")
                    ctx.inc(v.tensor_mul(hs(atn[h]), w1[b0:b0 + 64, 0:64], masks[b0:b0 + 64, 64:128]), "ve", 1)
                    ctx.mark(f"ve:at{h}_{c}")
                    ctx.wait(v, f"mm:oi{h}_{c}")
                    if c >= 1:
                        ctx.wait_cnt(v, "so", 16 * (2 * (c - 1) + h + 1))
                    ctx.inc(v.tensor_copy(dstg[h][b0:b0 + 64, :], w1[b0:b0 + 64, 0:256]), "ve", 1)
                    ctx.mark(f"ve:oev{h}_{c}")
                    ctx.wait(v, f"mm:sup{h}_{c}")
                    ctx.inc(v.tensor_copy(S_sb[h][:, :], S_ps[h][:, :]), "ve", 1)
                    ctx.mark(f"ve:sref{h}_{c}")

        for name, fn in (("gpsimd", gps), ("sync", syn), ("scalar", sca),
                         ("tensor", ten), ("vector", vec)):
            if real:
                getattr(block, name)(fn)
            else:
                fn(_Mock())
        es.close()

    emit(False)
    emit(False)
    nc = bass.Bass("TRN2", target_bir_lowering=False, debug=False)
    emit(True, nc)
    return nc


def host_inputs_b(inp, core, x1_full):
    b, p = core // 2, core % 2
    qw = np.asarray(inp["L1_attn_q_proj_w"])[p * 512:(p + 1) * 512]
    kw = np.asarray(inp["L1_attn_k_proj_w"])[p * 512:(p + 1) * 512]
    vw = np.asarray(inp["L1_attn_v_proj_w"])[p * 512:(p + 1) * 512]
    bw = np.asarray(inp["L1_attn_b_proj_w"])[2 * p:2 * p + 2]
    def conv_cols(w):
        w = np.asarray(w)[p * 512:(p + 1) * 512]
        return _np(w.reshape(4, NP, 4).transpose(1, 0, 2).reshape(NP, 16))
    m1 = np.concatenate([
        -np.tril(np.ones((CH, CH), np.float32), -1),
        np.triu(np.ones((CH, CH), np.float32)),
    ], axis=1)
    masks = np.concatenate([m1, m1], axis=0).astype(ml_dtypes.bfloat16)
    eyeb = np.concatenate([np.eye(CH, dtype=np.float32)] * 2, axis=0).astype(ml_dtypes.bfloat16)
    return {
        "x1t": _np(x1_full[b].T),
        "wqp": _np(qw.T), "wkp": _np(kw.T), "wvp": _np(vw.T), "wbp": _np(bw.T),
        "cq_c": conv_cols(inp["L1_attn_q_conv_w"]),
        "ck_c": conv_cols(inp["L1_attn_k_conv_w"]),
        "cv_c": conv_cols(inp["L1_attn_v_conv_w"]),
        "eyeb_in": eyeb,
        "masks_in": masks,
    }


P_BR = 6
FIRK = (3, 7, 15, 31)
FIRO = (0, 24, 80, 200)
HALO = 32
AW = 4096 * 1024


def build_launch_c1(debug=()):
    ctx = Ctx(["lg", "lh", "mm", "ve", "se", "st"])

    def emit(real, nc=None):
        es = ExitStack()
        if real:
            def di(name, shape, dt=F32):
                return nc.dram_tensor(name, shape, dt, kind="ExternalInput").ap()
            sbuf = lambda name, shape, dt: es.enter_context(nc.sbuf_tensor(name, shape, dt))
            psum = lambda name, shape, dt=F32: es.enter_context(nc.psum_tensor(name, shape, dt))
        else:
            di = lambda *a, **k: _Mock()
            sbuf = lambda *a, **k: _Mock()
            psum = lambda *a, **k: _Mock()

        x1t = di("x1t", [D, TH])
        vh_t = di("vh_t", [D, TH + HALO], BF16)
        dl_t = di("dl_t", [D, TH], BF16)
        w1g_in = di("w1g_in", [D, 2048])
        w1s_in = di("w1s_in", [24, 2048])
        gb1_in = di("gb1_in", [NP, 16])
        w2g_in = di("w2g_in", [2048, 24])
        c24_in = di("c24_in", [24, 2])
        ltemp_in = di("ltemp_in", [1, 4])
        onw_in = di("onw_in", [NP, 2])
        wop_in = di("wop_in", [D, D])
        fwc_in = di("fwc_in", [NP, 448])
        ident_in = di("ident_in", [NP, NP], BF16)
        ones_in = di("ones_in", [NP, NP], BF16)
        wua_in = di("wua_in", [4096, D])
        wda_in = di("wda_in", [D, 4096])
        if real:
            s1_o = nc.dram_tensor("s1_o", [D, TH], BF16, kind="ExternalOutput").ap()
            xn_o = nc.dram_tensor("xn_o", [D, TH], BF16, kind="ExternalOutput").ap()
            alp_o = nc.dram_tensor("alp_o", [1, 2], F32, kind="ExternalOutput").ap()
            scr = nc.dram_tensor("scr", [64, TH], F32, kind="Internal").ap()
            scr2 = nc.dram_tensor("scr2", [24, TH], BF16, kind="Internal").ap()
            dbg = {}
            for nm in debug:
                shape = {"d_xn1": [D, TH], "d_stats": [24, TH], "d_probs": [24, TH],
                         "d_fir31": [D, TH], "d_o": [D, TH], "d_scr": [64, TH],
                         "d_xnend": [D, TH]}[nm]
                dt_ = F32 if nm in ("d_stats", "d_scr") else BF16
                dbg[nm] = nc.dram_tensor(nm, shape, dt_, kind="ExternalOutput").ap()
        else:
            s1_o = xn_o = alp_o = scr = scr2 = _Mock()
            dbg = {nm: _Mock() for nm in debug}

        cols = sbuf("cols", [NP, 64], F32)
        onec = sbuf("onec", [NP, 1], BF16)
        ident = sbuf("ident", [NP, NP], BF16)
        onecB = sbuf("onecB", [NP, NP], BF16)
        fwc = sbuf("fwc", [NP, 448], F32)
        rowstage = sbuf("rowstage", [1, 2 * NC], F32)
        rw2 = sbuf("rw2", [1, NC], F32)
        c24 = sbuf("c24", [24, 8], F32)
        w1s = sbuf("w1s", [24, 2048], BF16)
        stats_sb = sbuf("stats_sb", [24, NC], BF16)
        lgf32 = sbuf("lgf32", [24, NC], F32)
        lg_hi = sbuf("lg_hi", [24, NC], BF16)
        lg_lo = sbuf("lg_lo", [24, NC], BF16)
        pfm = sbuf("pfm", [24, NC], BF16)
        arow = sbuf("arow", [1, 256], F32)
        smx = sbuf("smx", [NP, 72], F32)
        acol = sbuf("acol", [NP, 4], F32)
        ast = sbuf("ast", [NP, 2 * NC], F32)
        asq = sbuf("asq", [NP, NC], BF16)
        w1g_sb = sbuf("w1g_sb", [NP, KT * 2048], BF16)
        w2g_sb = sbuf("w2g_sb", [NP, 16 * 24], BF16)
        wop_sb = sbuf("wop_sb", [NP, KT * D], BF16)
        xn = sbuf("xn", [NP, KT * TH], BF16)
        def scoped(es_, side="left"):
            return (lambda name, shape, dt: es_.enter_context(nc.sbuf_tensor(name, shape, dt, side=side))) if real else (lambda *a: _Mock())
        es_x = ExitStack(); sb_x = scoped(es_x, "right")
        x1bf = sb_x("x1bf", [NP, KT * TH], BF16)
        rt0 = sb_x("rt0", [NP, TH], BF16)
        st0 = sb_x("st0", [NP, TH], BF16)
        sq0 = sb_x("sq0", [NP, KT * NC], BF16)
        if real:
            es_x.close()
        es_c = ExitStack(); sb_c = scoped(es_c, "right")
        vhb = sb_c("vhb", [NP, KT * (NC + HALO)], BF16)
        dlb = sb_c("dlb", [NP, KT * NC], BF16)
        fir = [sbuf(f"fir{p}", [NP, KT * NC], BF16) for p in range(4)]
        xnorm_b = fir[0]
        onrm = fir[1]
        o_acc = sbuf("o_acc", [NP, KT * NC], F32)
        sabs = sbuf("sabs", [NP, 4 * NC], BF16)
        h1 = sbuf("h1", [NP, 2 * NC], BF16)
        tmp = sbuf("tmp", [NP, NC], F32)
        pbt2 = sbuf("pbt2", [NP, 2 * NC], BF16)
        s1stg = sbuf("s1stg", [NP, 2 * NC], BF16)
        nro = sbuf("nro", [NP, 2 * NC], BF16)
        pbt = sbuf("pbt", [NP, 2 * 24], BF16)
        nrx = sbuf("nrx", [NP, NC], F32)

        bk = [psum(f"bk{i}", [NP, NC]) for i in range(4)]
        p_row = psum("p_row", [1, NC])
        lgps = psum("lgps", [24, NC])
        p_tr = psum("p_tr", [NP, 256], BF16)

        ctx.start(real, nc, es)
        block = es.enter_context(nc.Block()) if real else _Mock()
        fsl = lambda buf, kt, c0, w=NC, Tlen=TH: buf[:, kt * Tlen + c0: kt * Tlen + c0 + w]
        csl = lambda buf, i, w=NC: buf[:, i * w:(i + 1) * w]
        vsl = lambda i, off, w=NC: vhb[:, i * (NC + HALO) + HALO + off: i * (NC + HALO) + HALO + off + w]

        def br_tile(p, i):
            if p < 4:
                return csl(fir[p], i)
            if p == 4:
                return vsl(i, 0)
            return csl(dlb, i)

        def astrip(k):
            if k < 64:
                return wua_in[(k // 2) * NP:(k // 2 + 1) * NP, (k % 2) * NC:(k % 2 + 1) * NC]
            k -= 64
            return wda_in[(k // 8) * NP:(k // 8 + 1) * NP, (k % 8) * NC:(k % 8 + 1) * NC]

        def gps(g):
            for kt in range(KT):
                ctx.inc(g.dma_start(out=fsl(x1bf, kt, 0, TH), in_=x1t[kt * NP:(kt + 1) * NP, :]), "lg", 16)
            for kt in range(KT):
                ctx.inc(g.dma_start(out=w1g_sb[:, kt * 2048:(kt + 1) * 2048],
                                    in_=w1g_in[kt * NP:(kt + 1) * NP, :]), "lg", 16)
            ctx.inc(g.dma_start(out=w1s[:, :], in_=w1s_in), "lg", 16)
            for kt in range(16):
                ctx.inc(g.dma_start(out=w2g_sb[:, kt * 24:(kt + 1) * 24],
                                    in_=w2g_in[kt * NP:(kt + 1) * NP, :]), "lg", 16)
            for kt in range(KT):
                ctx.inc(g.dma_start(out=wop_sb[:, kt * D:(kt + 1) * D],
                                    in_=wop_in[kt * NP:(kt + 1) * NP, :]), "lg", 16)
            ctx.mark("lg:init")
            ctx.wait(g, "lh:lnrows")
            ctx.inc(g.dma_start(out=rt0[:], in_=scr[31:32, :].partition_broadcast(NP)), "lg", 16)
            ctx.inc(g.dma_start(out=st0[:], in_=scr[32:33, :].partition_broadcast(NP)), "lg", 16)
            ctx.mark("lg:rs")
            for ntc in range(NT4):
                c0 = ntc * NC
                ctx.wait(g, f"lh:stat{ntc}")
                if ntc >= 1:
                    ctx.wait(g, f"mm:g{ntc - 1}_15")
                    ctx.wait(g, f"ve:lgf{ntc - 1}")
                ctx.inc(g.dma_start(out=stats_sb[:, :], in_=scr[0:24, c0:c0 + NC]), "lg", 16)
                ctx.mark(f"lg:ssb{ntc}")
                for p in range(P_BR):
                    for hh in range(4):
                        idx = p * 4 + hh
                        r = hh * 6 + p
                        ctx.wait(g, f"lh:pst{ntc}")
                        if idx >= 2 or ntc >= 1:
                            pi = (ntc, idx - 2) if idx >= 2 else (ntc - 1, 22 + idx)
                            ctx.wait(g, f"ve:mixt{pi[0]}_{pi[1]}")
                        ctx.inc(g.dma_start(out=csl(pbt2, idx % 2),
                                            in_=scr2[r:r + 1, c0:c0 + NC].partition_broadcast(NP)), "lg", 16)
                        ctx.mark(f"lg:pb{ntc}_{idx}")
                for hh in range(4):
                    ctx.wait(g, f"lh:onr{ntc}_{hh}")
                    if hh >= 2 or ntc >= 1:
                        pi = (ntc, hh - 2) if hh >= 2 else (ntc - 1, 2 + hh)
                        ctx.wait(g, f"ve:onrm{pi[0]}_{pi[1]}")
                    ctx.inc(g.dma_start(out=csl(nro, hh % 2),
                                        in_=scr[24 + hh:25 + hh, c0:c0 + NC].partition_broadcast(NP)), "lg", 16)
                    ctx.mark(f"lg:nro{ntc}_{hh}")
                for kk in range(32):
                    k = ntc * 32 + kk
                    if k >= 2:
                        ctx.wait(g, f"se:aa{k - 2}")
                    ctx.inc(g.dma_start(out=csl(ast, k % 2), in_=astrip(k)), "lg", 16)
                    ctx.mark(f"lg:as{k}")

        def syn(sy):
            ctx.inc(sy.dma_start(out=cols[:, 0:16], in_=gb1_in), "lh", 16)
            ctx.inc(sy.dma_start(out=cols[:, 16:18], in_=onw_in), "lh", 16)
            ctx.inc(sy.dma_start(out=fwc[:, :], in_=fwc_in), "lh", 16)
            ctx.inc(sy.dma_start(out=c24[:, 0:2], in_=c24_in), "lh", 16)
            ctx.inc(sy.dma_start(out=rw2[0:1, 0:4], in_=ltemp_in), "lh", 16)
            ctx.inc(sy.dma_start(out=ident[:, :], in_=ident_in), "lh", 16)
            ctx.inc(sy.dma_start(out=onecB[:, :], in_=ones_in), "lh", 16)
            ctx.mark("lh:cinit")
            ctx.wait(sy, "ve:trow")
            ctx.inc(sy.dma_start(out=scr[33:34, 0:4], in_=rw2[0:1, 4:8]), "lh", 16)
            ctx.mark("lh:trow")
            ctx.wait(sy, "lh:trow")
            for hh in range(4):
                ctx.inc(sy.dma_start(out=c24[hh * 6:(hh + 1) * 6, 2:3],
                                     in_=scr[33:34, hh:hh + 1].partition_broadcast(6)), "lh", 16)
            ctx.mark("lh:trcol")
            for ch in range(NT4):
                ctx.wait(sy, f"ve:lnr{ch}")
                ctx.inc(sy.dma_start(out=scr[31:32, ch * NC:(ch + 1) * NC], in_=rowstage[0:1, NC:2 * NC]), "lh", 16)
                ctx.inc(sy.dma_start(out=scr[32:33, ch * NC:(ch + 1) * NC], in_=rw2[0:1, :]), "lh", 16)
                ctx.mark(f"lh:lnr{ch}")
            ctx.mark("lh:lnrows")
            if "d_xn1" in dbg:
                ctx.wait(sy, "ve:xn_done")
                for kt in range(KT):
                    ctx.inc(sy.dma_start(out=dbg["d_xn1"][kt * NP:(kt + 1) * NP, :], in_=fsl(xn, kt, 0, TH)), "st", 16)
            for ntc in range(NT4):
                c0 = ntc * NC
                if ntc >= 1:
                    ctx.wait(sy, f"ve:mixd{ntc - 1}")
                else:
                    ctx.wait(sy, "ve:xn_done")
                for kt in range(KT):
                    ctx.inc(sy.dma_start(out=vhb[:, kt * (NC + HALO):(kt + 1) * (NC + HALO)],
                                         in_=vh_t[kt * NP:(kt + 1) * NP, c0:c0 + NC + HALO]), "lh", 16)
                ctx.mark(f"lh:vh{ntc}")
                for kt in range(KT):
                    ctx.inc(sy.dma_start(out=csl(dlb, kt), in_=dl_t[kt * NP:(kt + 1) * NP, c0:c0 + NC]), "lh", 16)
                ctx.mark(f"lh:dl{ntc}")
                for p in range(P_BR):
                    for hh in range(4):
                        idx = p * 4 + hh
                        r = hh * 6 + p
                        ctx.wait(sy, f"ve:sev{ntc}_{idx}")
                        ctx.inc(sy.dma_start(out=scr[r:r + 1, c0:c0 + NC],
                                             in_=rowstage[0:1, (idx % 2) * NC:(idx % 2 + 1) * NC]), "lh", 16)
                        ctx.mark(f"lh:sev{ntc}_{idx}")
                ctx.mark(f"lh:stat{ntc}")
                ctx.wait(sy, f"ve:pfm{ntc}_3")
                ctx.inc(sy.dma_start(out=scr2[:, c0:c0 + NC], in_=pfm[:, :]), "lh", 16)
                ctx.mark(f"lh:pst{ntc}")
                if "d_probs" in dbg:
                    ctx.inc(sy.dma_start(out=dbg["d_probs"][:, c0:c0 + NC], in_=pfm[:, :]), "st", 16)
                for hh in range(4):
                    ctx.wait(sy, f"ve:onr{ntc}_{hh}")
                    ctx.inc(sy.dma_start(out=scr[24 + hh:25 + hh, c0:c0 + NC],
                                         in_=rowstage[0:1, (hh % 2) * NC:(hh % 2 + 1) * NC]), "lh", 16)
                    ctx.mark(f"lh:onr{ntc}_{hh}")
                for mt in range(KT):
                    ctx.wait(sy, f"ve:s1e{ntc}_{mt}")
                    ctx.inc(sy.dma_start(out=s1_o[mt * NP:(mt + 1) * NP, c0:c0 + NC],
                                         in_=csl(s1stg, mt % 2)), "st", 16)
                    ctx.mark(f"st:s1{ntc}_{mt}")
                ctx.wait(sy, f"ve:xnormd{ntc}")
                for kt in range(KT):
                    ctx.inc(sy.dma_start(out=xn_o[kt * NP:(kt + 1) * NP, c0:c0 + NC],
                                         in_=fsl(xn, kt, c0)), "st", 16)
                ctx.mark(f"st:xn{ntc}")
                if "d_stats" in dbg:
                    ctx.wait(sy, f"lh:stat{ntc}")
                    ctx.inc(sy.dma_start(out=dbg["d_stats"][:, c0:c0 + NC], in_=scr[0:24, c0:c0 + NC]), "st", 16)
                if "d_fir31" in dbg:
                    ctx.wait(sy, f"ve:fir{ntc}_3")
                    for kt in range(KT):
                        ctx.inc(sy.dma_start(out=dbg["d_fir31"][kt * NP:(kt + 1) * NP, c0:c0 + NC],
                                             in_=csl(fir[3], kt)), "st", 16)
                if "d_o" in dbg:
                    ctx.wait(sy, f"ve:onrm{ntc}_3")
                    for kt in range(KT):
                        ctx.inc(sy.dma_start(out=dbg["d_o"][kt * NP:(kt + 1) * NP, c0:c0 + NC],
                                             in_=csl(onrm, kt)), "st", 16)
            ctx.wait(sy, "ve:acold")
            ctx.inc(sy.dma_start(out=scr[34:35, 0:NP].rearrange("o (p c) -> (o p) c", p=NP),
                                 in_=acol[:, 0:1], allow_slow_non_contiguous=True), "lh", 16)
            ctx.inc(sy.dma_start(out=scr[35:36, 0:NP].rearrange("o (p c) -> (o p) c", p=NP),
                                 in_=acol[:, 1:2], allow_slow_non_contiguous=True), "lh", 16)
            ctx.mark("lh:acol")
            ctx.wait(sy, "lh:acol")
            ctx.inc(sy.dma_start(out=arow[0:1, 0:NP], in_=scr[34:35, 0:NP]), "lh", 16)
            ctx.inc(sy.dma_start(out=arow[0:1, NP:2 * NP], in_=scr[35:36, 0:NP]), "lh", 16)
            ctx.mark("lh:arow")
            ctx.wait(sy, "ve:alp")
            ctx.inc(sy.dma_start(out=alp_o, in_=rw2[0:1, 6:8]), "st", 16)
            if "d_scr" in dbg:
                ctx.wait_cnt(sy, "lh", ctx.cnt["lh"])
                ctx.inc(sy.dma_start(out=dbg["d_scr"], in_=scr[:, :]), "st", 16)
            if "d_xnend" in dbg:
                ctx.wait_cnt(sy, "ve", ctx.cnt["ve"])
                for kt in range(KT):
                    ctx.inc(sy.dma_start(out=dbg["d_xnend"][kt * NP:(kt + 1) * NP, :],
                                         in_=xn[:, kt * TH:(kt + 1) * TH]), "st", 16)
            if real:
                sy.wait_ge(ctx.sem["st"], ctx.cnt["st"])

        def ten(t):
            ctx.wait(t, "lg:init")
            ctx.wait(t, "ve:ones")
            for ch in range(NT4):
                mi = accum_mm(t, p_row[0:1, :], [(onec[:, :], fsl(x1bf, kt, ch * NC)) for kt in range(KT)])
                ctx.inc(mi, "mm", 1)
                ctx.mark(f"mm:sta{ch}")
                ctx.wait(t, f"ve:sta{ch}")
                ctx.wait(t, f"se:xsq{ch}")
                mi = accum_mm(t, p_row[0:1, :], [(onec[:, :], sq0[:, kt * NC:(kt + 1) * NC]) for kt in range(KT)])
                ctx.inc(mi, "mm", 1)
                ctx.mark(f"mm:st{ch}")
                ctx.wait(t, f"ve:ste{ch}")
            for ntc in range(NT4):
                c0 = ntc * NC
                for p in range(P_BR):
                    for hh in range(4):
                        idx = p * 4 + hh
                        ctx.wait(t, f"se:ab{ntc}_{idx}")
                        mi = accum_mm(t, p_row[0:1, :], [(onec[:, :], csl(sabs, 2 * (idx % 2) + j))
                                                         for j in range(2)])
                        ctx.inc(mi, "mm", 1)
                        ctx.mark(f"mm:sb{ntc}_{idx}")
                        ctx.wait(t, f"ve:sev{ntc}_{idx}")
                ctx.wait(t, f"lg:ssb{ntc}")
                if ntc >= 1:
                    ctx.wait(t, f"ve:lgf{ntc - 1}")
                for mt in range(16):
                    pairs = [(w1g_sb[:, kt * 2048 + mt * NP: kt * 2048 + (mt + 1) * NP], fsl(xn, kt, c0))
                             for kt in range(KT)]
                    pairs.append((w1s[:, mt * NP:(mt + 1) * NP], stats_sb[:, :]))
                    ctx.inc(accum_mm(t, bk[mt % 2][:, :], pairs), "mm", 1)
                    ctx.mark(f"mm:g{ntc}_{mt}")
                    if mt >= 1:
                        ctx.wait(t, f"se:h1e{ntc}_{mt - 1}")
                        mi = t.matmul(lgps[:, :], w2g_sb[:, (mt - 1) * 24:mt * 24], csl(h1, (mt - 1) % 2),
                                      start=(mt == 1), stop=False)
                        ctx.inc(mi, "mm", 1)
                        ctx.mark(f"mm:gl{ntc}_{mt - 1}")
                ctx.wait(t, f"se:h1e{ntc}_15")
                mi = t.matmul(lgps[:, :], w2g_sb[:, 15 * 24:16 * 24], csl(h1, 1),
                              start=False, stop=True)
                ctx.inc(mi, "mm", 1)
                ctx.mark(f"mm:gl{ntc}_15")
                ctx.wait(t, f"ve:lgsplit{ntc}")
                if ntc >= 1:
                    ctx.wait(t, f"ve:pfm{ntc - 1}_3")
                for blk in range(4):
                    q = (blk % 2) * 48
                    if blk >= 2:
                        ctx.wait(t, f"ve:pfm{ntc}_{blk - 2}")
                    elif blk == 1:
                        ctx.wait(t, f"ve:lt{ntc}_0")
                    t.transpose(p_tr[:, q:q + 24], lg_hi[:, blk * NP:(blk + 1) * NP], ident[0:24, 0:24])
                    mi = t.transpose(p_tr[:, q + 24:q + 48], lg_lo[:, blk * NP:(blk + 1) * NP], ident[0:24, 0:24])
                    ctx.inc(mi, "mm", 1)
                    ctx.mark(f"mm:tr{ntc}_{blk}")
                    ctx.wait(t, f"ve:pb{ntc}_{blk}")
                    mi = t.transpose(p_tr[0:24, 128:256], pbt[:, (blk % 2) * 24:(blk % 2 + 1) * 24],
                                     ident[:, :])
                    ctx.inc(mi, "mm", 1)
                    ctx.mark(f"mm:ptr{ntc}_{blk}")
                for hh in range(4):
                    ctx.wait(t, f"se:osq{ntc}_{hh}")
                    mi = accum_mm(t, p_row[0:1, :], [(onec[:, :], csl(sabs, 2 * (hh % 2) + j))
                                                     for j in range(2)])
                    ctx.inc(mi, "mm", 1)
                    ctx.mark(f"mm:on{ntc}_{hh}")
                    ctx.wait(t, f"ve:onre{ntc}_{hh}")
                for mt in range(KT):
                    if mt == 0:
                        ctx.wait(t, f"ve:onrm{ntc}_3")
                    pairs = [(wop_sb[:, kt * D + mt * NP: kt * D + (mt + 1) * NP], csl(onrm, kt))
                             for kt in range(KT)]
                    ctx.inc(accum_mm(t, bk[mt % 2][:, :], pairs), "mm", 1)
                    ctx.mark(f"mm:op{ntc}_{mt}")
                    if mt >= 1:
                        ctx.wait(t, f"ve:s1e{ntc}_{mt - 1}")
                ctx.wait(t, f"ve:s1e{ntc}_7")
                for gq in range(4):
                    ctx.wait(t, f"se:xq{ntc}_{gq}")
                    for j in range(2):
                        mi = t.matmul(bk[2][:, :], onecB[:, :], csl(sabs, 2 * (gq % 2) + j),
                                      start=(gq == 0 and j == 0), stop=(gq == 3 and j == 1))
                    ctx.inc(mi, "mm", 1)
                    ctx.mark(f"mm:xr{ntc}_{gq}")
                ctx.wait(t, f"se:xns{ntc}")


        def sca(s):
            ctx.wait(s, "lg:init")
            ctx.wait(s, "lh:cinit")
            ctx.wait(s, "ve:spl0")
            ctx.inc(s.activation(rw2[0:1, 4:8], rw2[0:1, 4:8], AF.Exp), "se", 1)
            ctx.mark("se:spexp")
            ctx.wait(s, "ve:spl1")
            ctx.inc(s.activation(rw2[0:1, 4:8], rw2[0:1, 4:8], AF.Ln), "se", 1)
            ctx.mark("se:spln")
            for ch in range(NT4):
                if ch >= 1:
                    ctx.wait(s, f"mm:st{ch - 1}")
                mi = None
                for kt in range(KT):
                    mi = s.activation(sq0[:, kt * NC:(kt + 1) * NC], fsl(x1bf, kt, ch * NC), AF.Square)
                ctx.inc(mi, "se", 1)
                ctx.mark(f"se:xsq{ch}")
                if ch >= 1:
                    ctx.wait(s, f"ve:lnv{ch - 1}")
                    ctx.inc(s.activation(rowstage[0:1, NC:2 * NC], rowstage[0:1, NC:2 * NC], AF.Sqrt,
                                         bias=cols[0:1, 62:63]), "se", 1)
                    ctx.mark(f"se:lns{ch - 1}")
            ctx.wait(s, f"ve:lnv{NT4 - 1}")
            ctx.inc(s.activation(rowstage[0:1, NC:2 * NC], rowstage[0:1, NC:2 * NC], AF.Sqrt,
                                 bias=cols[0:1, 62:63]), "se", 1)
            ctx.mark(f"se:lns{NT4 - 1}")
            for ntc in range(NT4):
                for p in range(P_BR):
                    for hh in range(4):
                        idx = p * 4 + hh
                        ctx.wait(s, f"ve:fir{ntc}_{p}" if p < 4 else (f"lh:vh{ntc}" if p == 4 else f"lh:dl{ntc}"))
                        if idx >= 2:
                            ctx.wait(s, f"mm:sb{ntc}_{idx - 2}")
                        elif ntc >= 1:
                            ctx.wait(s, f"mm:xr{ntc - 1}_{2 + idx}")
                        for j in range(2):
                            s.activation(csl(sabs, 2 * (idx % 2) + j), br_tile(p, 2 * hh + j), AF.Abs)
                        mi = s.activation(cols[:, 59:60], cols[:, 62:63], AF.Copy)
                        ctx.inc(mi, "se", 1)
                        ctx.mark(f"se:ab{ntc}_{idx}")
                for mt in range(16):
                    ctx.wait(s, f"mm:g{ntc}_{mt}")
                    if mt >= 2:
                        ctx.wait(s, f"mm:gl{ntc}_{mt - 2}")
                    mi = s.activation(csl(h1, mt % 2), bk[mt % 2][:, :], AF.Gelu,
                                      bias=cols[:, mt:mt + 1])
                    ctx.inc(mi, "se", 1)
                    ctx.mark(f"se:h1e{ntc}_{mt}")
                for blk in range(4):
                    ctx.wait(s, f"ve:mxn{ntc}_{blk}")
                    for hh in range(4):
                        mi = s.activation(smx[:, 24 + hh * 6:24 + (hh + 1) * 6],
                                          smx[:, 48 + hh * 6:48 + (hh + 1) * 6], AF.Exp,
                                          bias=smx[:, 8 + hh:9 + hh],
                                          accum_out=smx[:, 16 + hh:17 + hh])
                    ctx.inc(mi, "se", 1)
                    ctx.mark(f"se:exp{ntc}_{blk}")
                for hh in range(4):
                    if hh == 0:
                        ctx.wait(s, f"ve:mixd{ntc}")
                    if hh >= 2:
                        ctx.wait(s, f"mm:on{ntc}_{hh - 2}")
                    else:
                        ctx.wait(s, f"mm:sb{ntc}_{22 + hh}")
                    for j in range(2):
                        s.activation(csl(sabs, 2 * (hh % 2) + j), csl(o_acc, 2 * hh + j), AF.Square)
                    mi = s.activation(cols[:, 59:60], cols[:, 62:63], AF.Copy)
                    ctx.inc(mi, "se", 1)
                    ctx.mark(f"se:osq{ntc}_{hh}")
                for hh in range(4):
                    ctx.wait(s, f"ve:onre{ntc}_{hh}")
                    mi = s.activation(rowstage[0:1, (hh % 2) * NC:(hh % 2 + 1) * NC],
                                      rowstage[0:1, (hh % 2) * NC:(hh % 2 + 1) * NC], AF.Sqrt,
                                      scale=1.0 / 256, bias=cols[0:1, 62:63])
                    ctx.inc(mi, "se", 1)
                    ctx.mark(f"se:ons{ntc}_{hh}")
                for gq in range(4):
                    if gq >= 2:
                        ctx.wait(s, f"mm:xr{ntc}_{gq - 2}")
                    else:
                        ctx.wait(s, f"mm:on{ntc}_{2 + gq}")
                    for j in range(2):
                        s.activation(csl(sabs, 2 * (gq % 2) + j), fsl(xn, 2 * gq + j, ntc * NC), AF.Square)
                    mi = s.activation(cols[:, 59:60], cols[:, 62:63], AF.Copy)
                    ctx.inc(mi, "se", 1)
                    ctx.mark(f"se:xq{ntc}_{gq}")
                ctx.wait(s, f"mm:xr{ntc}_3")
                if ntc >= 1:
                    ctx.wait(s, f"ve:xnormd{ntc - 1}")
                s.activation(nrx[:, :], bk[2][:, :], AF.Sqrt, scale=1.0 / D)
                mi = s.activation(cols[:, 59:60], cols[:, 62:63], AF.Copy)
                ctx.inc(mi, "se", 1)
                ctx.mark(f"se:xns{ntc}")
                for kk in range(32):
                    k = ntc * 32 + kk
                    ctx.wait(s, f"lg:as{k}")
                    if k >= 2:
                        ctx.wait(s, f"ve:ac{k - 2}")
                    mi = s.activation(asq[:, :], csl(ast, k % 2), AF.Abs,
                                      accum_out=acol[:, 2 + k % 2:3 + k % 2])
                    ctx.inc(mi, "se", 1)
                    ctx.mark(f"se:aa{k}")
            ctx.wait(s, "lh:arow")
            s.activation(asq[0:1, 0:NP], arow[0:1, 0:NP], AF.Copy, accum_out=rw2[0:1, 6:7])
            mi = s.activation(asq[0:1, 0:NP], arow[0:1, NP:2 * NP], AF.Copy, accum_out=rw2[0:1, 7:8])
            ctx.inc(mi, "se", 1)
            ctx.mark("se:asum")

        def vec(v):
            v.memset(cols[:, 62:63], LN_EPS)
            v.memset(cols[:, 60:61], 1e-8)
            v.memset(acol[:, :], 0.0)
            ctx.wait(v, "lh:cinit")
            ctx.inc(v.memset(onec[:], 1.0), "ve", 1)
            ctx.mark("ve:ones")
            spc = lambda: v.tensor_copy(cols[:, 61:62], cols[:, 62:63])
            ctx.wait(v, "lh:cinit")
            ctx.inc(v.tensor_scalar(rw2[0:1, 4:8], rw2[0:1, 0:4], 1.0, None, ALU.mult), "ve", 1)
            ctx.mark("ve:spl0")
            ctx.wait(v, "se:spexp")
            ctx.inc(v.tensor_scalar(rw2[0:1, 4:8], rw2[0:1, 4:8], 1.0, None, ALU.add), "ve", 1)
            ctx.mark("ve:spl1")
            ctx.wait(v, "se:spln")
            v.tensor_scalar(rw2[0:1, 4:8], rw2[0:1, 4:8], 1e-4, None, ALU.add)
            spc()
            ctx.inc(v.reciprocal(rw2[0:1, 4:8], rw2[0:1, 4:8]), "ve", 1)
            ctx.mark("ve:trow")
            for ch in range(NT4):
                ctx.wait(v, f"mm:sta{ch}")
                ctx.inc(v.tensor_copy(rowstage[0:1, 0:NC], p_row[0:1, :]), "ve", 1)
                ctx.mark(f"ve:sta{ch}")
                ctx.wait(v, f"mm:st{ch}")
                if ch >= 1:
                    ctx.wait(v, f"lh:lnr{ch - 1}")
                ctx.inc(v.tensor_copy(rowstage[0:1, NC:2 * NC], p_row[0:1, :]), "ve", 1)
                ctx.mark(f"ve:ste{ch}")
                v.tensor_scalar(rowstage[0:1, 0:NC], rowstage[0:1, 0:NC], 1.0 / D, None, ALU.mult)
                v.tensor_scalar(rowstage[0:1, NC:2 * NC], rowstage[0:1, NC:2 * NC], 1.0 / D, None, ALU.mult)
                v.tensor_mul(rw2[0:1, :], rowstage[0:1, 0:NC], rowstage[0:1, 0:NC])
                spc()
                ctx.inc(v.tensor_sub(rowstage[0:1, NC:2 * NC], rowstage[0:1, NC:2 * NC], rw2[0:1, :]), "ve", 1)
                ctx.mark(f"ve:lnv{ch}")
                ctx.wait(v, f"se:lns{ch}")
                v.reciprocal(rowstage[0:1, NC:2 * NC], rowstage[0:1, NC:2 * NC])
                spc()
                ctx.inc(v.tensor_mul(rw2[0:1, :], rowstage[0:1, 0:NC], rowstage[0:1, NC:2 * NC]), "ve", 1)
                ctx.mark(f"ve:lnr{ch}")
            ctx.wait(v, "lg:rs")
            mi = None
            for kt in range(KT):
                for cc in range(NT4):
                    mo = bk[3][:, :]
                    c0 = cc * NC
                    v.tensor_mul(mo, fsl(x1bf, kt, c0), rt0[:, c0:c0 + NC])
                    v.tensor_sub(mo, mo, st0[:, c0:c0 + NC])
                    mi = v.tensor_copy(fsl(xn, kt, c0), mo)
            ctx.inc(mi, "ve", 1)
            ctx.mark("ve:xn_done")
            for ntc in range(NT4):
                c0 = ntc * NC
                ctx.wait(v, f"lh:vh{ntc}")
                if ntc >= 1:
                    ctx.wait(v, f"mm:op{ntc - 1}_7")
                    ctx.wait(v, f"st:xn{ntc - 1}")
                    ctx.wait(v, f"se:ab{ntc - 1}_23")
                for p in range(4):
                    K = FIRK[p]
                    for i in range(KT):
                        v.tensor_scalar(csl(fir[p], i), vsl(i, -(K - 1)),
                                        fwc[:, FIRO[p] + i * K: FIRO[p] + i * K + 1], None, ALU.mult)
                        for tau in range(1, K):
                            mi = v.scalar_tensor_tensor(csl(fir[p], i), vsl(i, -(K - 1) + tau),
                                                        fwc[:, FIRO[p] + i * K + tau: FIRO[p] + i * K + tau + 1],
                                                        csl(fir[p], i), ALU.mult, ALU.add)
                    ctx.inc(mi, "ve", 1)
                    ctx.mark(f"ve:fir{ntc}_{p}")
                for p in range(P_BR):
                    for hh in range(4):
                        idx = p * 4 + hh
                        ctx.wait(v, f"mm:sb{ntc}_{idx}")
                        if idx >= 2:
                            ctx.wait(v, f"lh:sev{ntc}_{idx - 2}")
                        elif ntc >= 1:
                            ctx.wait(v, f"lh:onr{ntc - 1}_{2 + idx}")
                        mi = v.tensor_scalar(rowstage[0:1, (idx % 2) * NC:(idx % 2 + 1) * NC],
                                             p_row[0:1, :], 1.0 / 256, None, ALU.mult)
                        ctx.inc(mi, "ve", 1)
                        ctx.mark(f"ve:sev{ntc}_{idx}")
                ctx.wait(v, f"mm:gl{ntc}_15")
                if ntc >= 1:
                    ctx.wait(v, f"mm:tr{ntc - 1}_3")
                v.scalar_tensor_tensor(lgf32[:, :], stats_sb[:, :], c24[:, 0:1], lgps[:, :],
                                       ALU.mult, ALU.add)
                v.tensor_scalar(lgf32[:, :], lgf32[:, :], c24[:, 1:2], None, ALU.add)
                ctx.inc(v.tensor_scalar(lgf32[:, :], lgf32[:, :], c24[:, 2:3], None, ALU.mult), "ve", 1)
                ctx.mark(f"ve:lgf{ntc}")
                v.tensor_copy(lg_hi[:, :], lgf32[:, :])
                spc()
                ctx.inc(v.scalar_tensor_tensor(lg_lo[:, :], lg_hi[:, :], -1.0, lgf32[:, :],
                                               ALU.mult, ALU.add), "ve", 1)
                ctx.mark(f"ve:lgsplit{ntc}")
                for blk in range(4):
                    ctx.wait(v, f"mm:tr{ntc}_{blk}")
                    q = (blk % 2) * 48
                    v.tensor_copy(smx[:, 48:72], p_tr[:, q:q + 24])
                    spc()
                    ctx.inc(v.tensor_add(smx[:, 48:72], smx[:, 48:72], p_tr[:, q + 24:q + 48]), "ve", 1)
                    ctx.mark(f"ve:lt{ntc}_{blk}")
                    spc()
                    for hh in range(4):
                        v.reduce_max(smx[:, hh:hh + 1], smx[:, 48 + hh * 6:48 + (hh + 1) * 6], axis=AX.X)
                    for hh in range(4):
                        mi = v.tensor_scalar(smx[:, 8 + hh:9 + hh], smx[:, hh:hh + 1], -1.0, None, ALU.mult)
                    ctx.inc(mi, "ve", 1)
                    ctx.mark(f"ve:mxn{ntc}_{blk}")
                    ctx.wait(v, f"se:exp{ntc}_{blk}")
                    for hh in range(4):
                        v.reciprocal(smx[:, 16 + hh:17 + hh], smx[:, 16 + hh:17 + hh])
                    for hh in range(4):
                        v.tensor_scalar(smx[:, 24 + hh * 6:24 + (hh + 1) * 6],
                                        smx[:, 24 + hh * 6:24 + (hh + 1) * 6],
                                        smx[:, 16 + hh:17 + hh], None, ALU.mult)
                    spc()
                    v.tensor_scalar(smx[:, 24:48], smx[:, 24:48], 1.0 - 0.05, None, ALU.mult)
                    spc()
                    for hh in range(4):
                        v.tensor_scalar(smx[:, 24 + hh * 6 + 5:24 + hh * 6 + 6],
                                        smx[:, 24 + hh * 6 + 5:24 + hh * 6 + 6], 0.05, None, ALU.add)
                    spc()
                    ctx.inc(v.tensor_copy(pbt[:, (blk % 2) * 24:(blk % 2 + 1) * 24], smx[:, 24:48]), "ve", 1)
                    ctx.mark(f"ve:pb{ntc}_{blk}")
                    ctx.wait(v, f"mm:ptr{ntc}_{blk}")
                    ctx.inc(v.tensor_copy(pfm[:, blk * NP:(blk + 1) * NP], p_tr[0:24, 128:256]), "ve", 1)
                    ctx.mark(f"ve:pfm{ntc}_{blk}")
                for p in range(P_BR):
                    for hh in range(4):
                        idx = p * 4 + hh
                        ctx.wait(v, f"lg:pb{ntc}_{idx}")
                        if p == 0 and hh == 0 and ntc >= 1:
                            ctx.wait(v, f"se:osq{ntc - 1}_3")
                        for j in range(2):
                            i = 2 * hh + j
                            if p == 0:
                                mi = v.tensor_mul(csl(o_acc, i), br_tile(p, i), csl(pbt2, idx % 2))
                            else:
                                v.tensor_mul(tmp[:, :], br_tile(p, i), csl(pbt2, idx % 2))
                                mi = v.tensor_add(csl(o_acc, i), csl(o_acc, i), tmp[:, :])
                        ctx.inc(mi, "ve", 1)
                        ctx.mark(f"ve:mixt{ntc}_{idx}")
                ctx.mark(f"ve:mixd{ntc}")
                for hh in range(4):
                    ctx.wait(v, f"mm:on{ntc}_{hh}")
                    if hh >= 2:
                        ctx.wait(v, f"lh:onr{ntc}_{hh - 2}")
                    else:
                        ctx.wait(v, f"lh:sev{ntc}_{22 + hh}")
                    ctx.inc(v.tensor_copy(rowstage[0:1, (hh % 2) * NC:(hh % 2 + 1) * NC], p_row[0:1, :]), "ve", 1)
                    ctx.mark(f"ve:onre{ntc}_{hh}")
                    ctx.wait(v, f"se:ons{ntc}_{hh}")
                    spc()
                    ctx.inc(v.reciprocal(rowstage[0:1, (hh % 2) * NC:(hh % 2 + 1) * NC],
                                         rowstage[0:1, (hh % 2) * NC:(hh % 2 + 1) * NC]), "ve", 1)
                    ctx.mark(f"ve:onr{ntc}_{hh}")
                for hh in range(4):
                    ctx.wait(v, f"lg:nro{ntc}_{hh}")
                    for j in range(2):
                        i = 2 * hh + j
                        mi = v.scalar_tensor_tensor(csl(onrm, i), csl(o_acc, i), cols[:, 16 + j:17 + j],
                                                    csl(nro, hh % 2), ALU.mult, ALU.mult)
                    ctx.inc(mi, "ve", 1)
                    ctx.mark(f"ve:onrm{ntc}_{hh}")
                for mt in range(KT):
                    ctx.wait(v, f"mm:op{ntc}_{mt}")
                    if mt >= 2:
                        ctx.wait(v, f"st:s1{ntc}_{mt - 2}")
                    elif ntc >= 1:
                        ctx.wait(v, f"st:s1{ntc - 1}_{6 + mt}")
                    ctx.inc(v.tensor_copy(csl(s1stg, mt % 2), bk[mt % 2][:, :]), "ve", 1)
                    ctx.mark(f"ve:s1e{ntc}_{mt}")
                ctx.wait(v, f"se:xns{ntc}")
                v.tensor_scalar(nrx[:, :], nrx[:, :], 1e-8, None, ALU.add)
                spc()
                v.reciprocal(nrx[:, :], nrx[:, :])
                spc()
                ctx.wait(v, f"mm:g{ntc}_15")
                mi = None
                for kt in range(KT):
                    mi = v.tensor_mul(fsl(xn, kt, c0), fsl(xn, kt, c0), nrx[:, :])
                ctx.inc(mi, "ve", 1)
                ctx.mark(f"ve:xnormd{ntc}")
                for kk in range(32):
                    k = ntc * 32 + kk
                    ctx.wait(v, f"se:aa{k}")
                    dst = 0 if k < 64 else 1
                    spc()
                    ctx.inc(v.tensor_add(acol[:, dst:dst + 1], acol[:, dst:dst + 1],
                                         acol[:, 2 + k % 2:3 + k % 2]), "ve", 1)
                    ctx.mark(f"ve:ac{k}")
            ctx.mark("ve:acold")
            ctx.wait(v, "se:asum")
            spc()
            ctx.inc(v.tensor_scalar(rw2[0:1, 6:8], rw2[0:1, 6:8], 1.0 / AW, None, ALU.mult), "ve", 1)
            ctx.mark("ve:alp")

        for name, fn in (("gpsimd", gps), ("sync", syn), ("scalar", sca),
                         ("tensor", ten), ("vector", vec)):
            if real:
                getattr(block, name)(fn)
            else:
                fn(_Mock())
        es.close()

    emit(False)
    emit(False)
    nc = bass.Bass("TRN2", target_bir_lowering=False, debug=False)
    emit(True, nc)
    return nc


def host_inputs_c1(inp, core, x1_full, v_full, dl_full):
    b, hf = core // 2, core % 2
    t0 = hf * TH
    x1 = np.asarray(x1_full[b])
    vt = np.asarray(v_full[b]).T
    dlt = np.asarray(dl_full[b]).T
    vh = np.zeros((D, TH + HALO), ml_dtypes.bfloat16)
    lo = max(0, t0 - HALO)
    vh[:, HALO - (t0 - lo):] = vt[:, lo:t0 + TH][:, :]
    gb1 = np.asarray(inp["L1_attn_gate_b1"])
    gb2 = np.asarray(inp["L1_attn_gate_b2"])
    astat = np.asarray(inp["L1_attn_alpha_stat"])
    w1 = np.asarray(inp["L1_attn_gate_w1"])
    fw = []
    for key, K in (("L1_attn_fir_w3", 3), ("L1_attn_fir_w7", 7),
                   ("L1_attn_fir_w15", 15), ("L1_attn_fir_w31", 31)):
        w = np.asarray(inp[key])
        fw.append(w.reshape(KT, NP, K).transpose(1, 0, 2).reshape(NP, KT * K))
    return {
        "x1t": _np(x1[t0:t0 + TH].T),
        "vh_t": vh,
        "dl_t": np.ascontiguousarray(dlt[:, t0:t0 + TH]),
        "w1g_in": _np(w1[:, 0:D].T),
        "w1s_in": _np(w1[:, D:D + 24].T),
        "gb1_in": _np(gb1.reshape(16, NP).T),
        "w2g_in": _np(np.asarray(inp["L1_attn_gate_w2"]).T),
        "c24_in": _np(np.stack([np.tile(np.asarray(astat), 4),
                                gb2], axis=1)),
        "ltemp_in": _np(np.asarray(inp["L1_attn_gate_log_temp"]).reshape(1, 4)),
        "onw_in": _np(np.asarray(inp["L1_attn_o_norm_w"]).reshape(2, NP).T),
        "wop_in": _np(np.asarray(inp["L1_attn_o_proj_w"]).T),
        "fwc_in": _np(np.concatenate(fw, axis=1)),
        "ident_in": np.eye(NP, dtype=ml_dtypes.bfloat16),
        "ones_in": np.ones((NP, NP), dtype=ml_dtypes.bfloat16),
        "wua_in": _np(inp["L1_ffn_w_up"]),
        "wda_in": _np(inp["L1_ffn_w_down"]),
    }


def build_launch_c2(debug=()):
    ctx = Ctx(["lg", "lh", "mm", "ve", "se", "st"])
    NSTR = 64

    def emit(real, nc=None):
        es = ExitStack()
        if real:
            def di(name, shape, dt=F32):
                return nc.dram_tensor(name, shape, dt, kind="ExternalInput").ap()
            sbuf = lambda name, shape, dt: es.enter_context(nc.sbuf_tensor(name, shape, dt))
            psum = lambda name, shape, dt=F32: es.enter_context(nc.psum_tensor(name, shape, dt))
        else:
            di = lambda *a, **k: _Mock()
            sbuf = lambda *a, **k: _Mock()
            psum = lambda *a, **k: _Mock()

        x1t = di("x1t", [D, TH])
        s1t = di("s1t", [D, TH], BF16)
        xnt = di("xnt", [D, TH], BF16)
        wut_in = di("wut_in", [D, 4096])
        wdt_in = di("wdt_in", [4096, D])
        alp_in = di("alp_in", [1, 2])
        if real:
            out_o = nc.dram_tensor("out_o", [D, TH], F32, kind="ExternalOutput").ap()
            scr = nc.dram_tensor("scr", [4, TH], F32, kind="Internal").ap()
            dbg = {}
            for nm in debug:
                shape = {"d_h": [NP, 32 * NC], "d_g": [D, TH], "d_wtu": [NP, 4096]}[nm]
                dbg[nm] = nc.dram_tensor(nm, shape, BF16, kind="ExternalOutput").ap()
        else:
            out_o = scr = _Mock()
            dbg = {nm: _Mock() for nm in debug}

        cols = sbuf("cols", [NP, 8], F32)
        onec = sbuf("onec", [NP, 1], BF16)
        rw2 = sbuf("rw2", [1, NC], F32)
        wtu = sbuf("wtu", [NP, KT * 4096], BF16)
        wtd = sbuf("wtd", [NP, 32 * D], BF16)
        h = sbuf("h", [NP, 32 * NC], BF16)
        xnb = sbuf("xnb", [NP, 2 * KT * NC], BF16)
        hsq = sbuf("hsq", [NP, 2 * NC], BF16)
        hnr = sbuf("hnr", [NP, NC], BF16)
        def scoped(es_, side="left"):
            return (lambda name, shape, dt: es_.enter_context(nc.sbuf_tensor(name, shape, dt, side=side))) if real else (lambda *a: _Mock())
        es_t = ExitStack(); sb_t = scoped(es_t, "right")
        tstg = sb_t("tstg", [NP, 2 * 1024], F32)
        absb = sb_t("absb", [NP, 2 * 1024], F32)
        sgn = sb_t("sgn", [NP, 2 * 1024], BF16)
        msk = sb_t("msk", [NP, 1024], BF16)
        if real:
            es_t.close()
        es_c = ExitStack(); sb_c = scoped(es_c, "right")
        g_sb = sb_c("g_sb", [NP, 2 * NC], BF16)
        x1f = sb_c("x1f", [NP, 2 * NC], F32)
        s1b = sb_c("s1b", [NP, 2 * NC], BF16)
        outb = sb_c("outb", [NP, 2 * NC], F32)
        tmp2 = sb_c("tmp2", [NP, NC], F32)

        bk = [psum(f"bk{i}", [NP, NC]) for i in range(4)]
        p_row = psum("p_row", [1, NC])

        ctx.start(real, nc, es)
        block = es.enter_context(nc.Block()) if real else _Mock()
        csl = lambda buf, i, w=NC: buf[:, i * w:(i + 1) * w]

        def tsrc(k):
            if k < 32:
                kt, s5 = k // 4, k % 4
                return (wut_in[kt * NP:(kt + 1) * NP, s5 * 1024:(s5 + 1) * 1024],
                        wtu[:, kt * 4096 + s5 * 1024: kt * 4096 + (s5 + 1) * 1024], 0)
            k -= 32
            return (wdt_in[k * NP:(k + 1) * NP, :], wtd[:, k * D:(k + 1) * D], 1)

        def gps(g):
            ctx.inc(g.dma_start(out=rw2[0:1, 8:10], in_=alp_in), "lg", 16)
            ctx.mark("lg:alp")
            ctx.wait(g, "lh:thr")
            ctx.inc(g.dma_start(out=cols[:, 0:2], in_=scr[1:2, 0:2].partition_broadcast(NP)), "lg", 16)
            ctx.mark("lg:thrc")
            for k in range(NSTR):
                if k >= 2:
                    ctx.wait(g, f"se:tsa{k - 2}")
                ctx.inc(g.dma_start(out=csl(tstg, k % 2, 1024), in_=tsrc(k)[0]), "lg", 16)
                ctx.mark(f"lg:ts{k}")
            for ntc in range(NT4):
                ctx.wait(g, f"lh:hr{ntc}")
                if ntc >= 1:
                    ctx.wait(g, f"ve:hn{ntc - 1}")
                ctx.inc(g.dma_start(out=hnr[:, :], in_=scr[2:3, ntc * NC:(ntc + 1) * NC].partition_broadcast(NP)), "lg", 16)
                ctx.mark(f"lg:hnr{ntc}")

        def syn(sy):
            ctx.wait(sy, "ve:thr")
            ctx.inc(sy.dma_start(out=scr[1:2, 0:2], in_=rw2[0:1, 10:12]), "lh", 16)
            ctx.mark("lh:thr")
            for ntc in range(NT4):
                c0 = ntc * NC
                if ntc >= 2:
                    ctx.wait(sy, f"mm:u{ntc - 2}_31")
                for kt in range(KT):
                    ctx.inc(sy.dma_start(out=xnb[:, ((ntc % 2) * KT + kt) * NC:((ntc % 2) * KT + kt + 1) * NC],
                                         in_=xnt[kt * NP:(kt + 1) * NP, c0:c0 + NC]), "lh", 16)
                ctx.mark(f"lh:xn{ntc}")
                ctx.wait(sy, f"ve:hvr{ntc}")
                ctx.inc(sy.dma_start(out=scr[2:3, c0:c0 + NC], in_=rw2[0:1, :]), "lh", 16)
                ctx.mark(f"lh:hr{ntc}")
                for mt2 in range(KT):
                    if ntc == 0 and mt2 == 0:
                        ctx.wait(sy, "ve:wt63")
                    if mt2 >= 2 or ntc >= 1:
                        pi = (ntc, mt2 - 2) if mt2 >= 2 else (ntc - 1, 6 + mt2)
                        ctx.wait(sy, f"ve:out{pi[0]}_{pi[1]}")
                    ctx.inc(sy.dma_start(out=csl(x1f, mt2 % 2), in_=x1t[mt2 * NP:(mt2 + 1) * NP, c0:c0 + NC]), "lh", 16)
                    ctx.inc(sy.dma_start(out=csl(s1b, mt2 % 2), in_=s1t[mt2 * NP:(mt2 + 1) * NP, c0:c0 + NC]), "lh", 16)
                    ctx.mark(f"lh:ep{ntc}_{mt2}")
                    ctx.wait(sy, f"ve:out{ntc}_{mt2}")
                    ctx.inc(sy.dma_start(out=out_o[mt2 * NP:(mt2 + 1) * NP, c0:c0 + NC],
                                         in_=csl(outb, mt2 % 2)), "st", 16)
                    ctx.mark(f"st:o{ntc}_{mt2}")
                if "d_h" in dbg and ntc == 0:
                    ctx.wait(sy, f"ve:hn{ntc}")
                    ctx.inc(sy.dma_start(out=dbg["d_h"], in_=h[:, :]), "st", 16)
            if "d_wtu" in dbg:
                ctx.wait(sy, "ve:wt7")
                ctx.inc(sy.dma_start(out=dbg["d_wtu"], in_=wtu[:, 0:4096]), "st", 16)
            if real:
                sy.wait_ge(ctx.sem["st"], ctx.cnt["st"])

        def ten(t):
            ctx.wait(t, "ve:ones")
            for ntc in range(NT4):
                ctx.wait(t, "ve:wt31")
                ctx.wait(t, f"lh:xn{ntc}")
                for mt in range(32):
                    pairs = [(wtu[:, kt * 4096 + mt * NP: kt * 4096 + (mt + 1) * NP],
                              csl(xnb, (ntc % 2) * KT + kt)) for kt in range(KT)]
                    ctx.inc(accum_mm(t, bk[mt % 2][:, :], pairs), "mm", 1)
                    ctx.mark(f"mm:u{ntc}_{mt}")
                    if mt >= 1:
                        ctx.wait(t, f"se:ug{ntc}_{mt - 1}")
                        mi = t.matmul(p_row[0:1, :], onec[:, :], csl(hsq, (mt - 1) % 2),
                                      start=(mt == 1), stop=False)
                        ctx.inc(mi, "mm", 1)
                        ctx.mark(f"mm:hs{ntc}_{mt - 1}")
                ctx.wait(t, f"se:ug{ntc}_31")
                mi = t.matmul(p_row[0:1, :], onec[:, :], csl(hsq, 31 % 2), start=False, stop=True)
                ctx.inc(mi, "mm", 1)
                ctx.mark(f"mm:hs{ntc}_31")
                ctx.wait(t, f"ve:hv{ntc}")
                ctx.wait(t, "ve:wt63")
                ctx.wait(t, f"ve:hn{ntc}")
                for mt2 in range(KT):
                    pairs = [(wtd[:, kt2 * D + mt2 * NP: kt2 * D + (mt2 + 1) * NP], csl(h, kt2))
                             for kt2 in range(32)]
                    ctx.inc(accum_mm(t, bk[2 + mt2 % 2][:, :], pairs), "mm", 1)
                    ctx.mark(f"mm:d{ntc}_{mt2}")
                    if mt2 >= 1:
                        ctx.wait(t, f"se:dg{ntc}_{mt2 - 1}")
                ctx.wait(t, f"se:dg{ntc}_7")

        def sca(s):
            ctx.wait(s, "lg:thrc")
            for k in range(NSTR):
                ctx.wait(s, f"lg:ts{k}")
                if k >= 2:
                    ctx.wait(s, f"ve:wt{k - 2}")
                s.activation(csl(sgn, k % 2, 1024), csl(tstg, k % 2, 1024), AF.Sign)
                s.activation(csl(absb, k % 2, 1024), csl(tstg, k % 2, 1024), AF.Abs)
                mi = s.activation(cols[:, 6:7], cols[:, 0:1], AF.Copy)
                ctx.inc(mi, "se", 1)
                ctx.mark(f"se:tsa{k}")
            for ntc in range(NT4):
                for mt in range(32):
                    ctx.wait(s, f"mm:u{ntc}_{mt}")
                    if mt >= 2:
                        ctx.wait(s, f"mm:hs{ntc}_{mt - 2}")
                    s.activation(csl(h, mt), bk[mt % 2][:, :], AF.Gelu)
                    mi = s.activation(csl(hsq, mt % 2), csl(h, mt), AF.Square)
                    ctx.inc(mi, "se", 1)
                    ctx.mark(f"se:ug{ntc}_{mt}")
                ctx.wait(s, f"ve:hv{ntc}")
                ctx.inc(s.activation(rw2[0:1, :], rw2[0:1, :], AF.Sqrt, scale=1.0 / 4096), "se", 1)
                ctx.mark(f"se:hrs{ntc}")
                for mt2 in range(KT):
                    ctx.wait(s, f"mm:d{ntc}_{mt2}")
                    if mt2 >= 2:
                        ctx.wait(s, f"ve:g{ntc}_{mt2 - 2}")
                    elif ntc >= 1:
                        ctx.wait(s, f"ve:g{ntc - 1}_{6 + mt2}")
                    mi = s.activation(csl(g_sb, mt2 % 2), bk[2 + mt2 % 2][:, :], AF.Silu)
                    ctx.inc(mi, "se", 1)
                    ctx.mark(f"se:dg{ntc}_{mt2}")

        def vec(v):
            v.memset(cols[:, 7:8], 0.0)
            ctx.inc(v.memset(onec[:], 1.0), "ve", 1)
            ctx.mark("ve:ones")
            spc = lambda: v.tensor_copy(cols[:, 6:7], cols[:, 7:8])
            ctx.wait(v, "lg:alp")
            ctx.inc(v.tensor_scalar(rw2[0:1, 10:12], rw2[0:1, 8:10], 0.5, None, ALU.mult), "ve", 1)
            ctx.mark("ve:thr")
            for k in range(NSTR):
                ctx.wait(v, f"se:tsa{k}")
                dst = tsrc(k)[1]
                v.tensor_scalar(msk[:, :], csl(absb, k % 2, 1024), cols[:, tsrc(k)[2]:tsrc(k)[2] + 1],
                                None, ALU.is_gt)
                spc()
                mi = v.tensor_mul(dst, csl(sgn, k % 2, 1024), msk[:, :])
                ctx.inc(mi, "ve", 1)
                ctx.mark(f"ve:wt{k}")
            for ntc in range(NT4):
                ctx.wait(v, f"mm:hs{ntc}_31")
                if ntc >= 1:
                    ctx.wait(v, f"lh:hr{ntc - 1}")
                ctx.inc(v.tensor_copy(rw2[0:1, :], p_row[0:1, :]), "ve", 1)
                ctx.mark(f"ve:hv{ntc}")
                ctx.wait(v, f"se:hrs{ntc}")
                v.tensor_scalar(rw2[0:1, :], rw2[0:1, :], 1e-8, None, ALU.add)
                spc()
                ctx.inc(v.reciprocal(rw2[0:1, :], rw2[0:1, :]), "ve", 1)
                ctx.mark(f"ve:hvr{ntc}")
                ctx.wait(v, f"lg:hnr{ntc}")
                mi = None
                for mt in range(32):
                    mi = v.tensor_mul(csl(h, mt), csl(h, mt), hnr[:, :])
                ctx.inc(mi, "ve", 1)
                ctx.mark(f"ve:hn{ntc}")
                for mt2 in range(KT):
                    ctx.wait(v, f"se:dg{ntc}_{mt2}")
                    ctx.wait(v, f"lh:ep{ntc}_{mt2}")
                    if mt2 >= 2 or ntc >= 1:
                        pi = (ntc, mt2 - 2) if mt2 >= 2 else (ntc - 1, 6 + mt2)
                        ctx.wait(v, f"st:o{pi[0]}_{pi[1]}")
                    v.tensor_mul(tmp2[:, :], csl(s1b, mt2 % 2), csl(g_sb, mt2 % 2))
                    mi = v.tensor_add(csl(outb, mt2 % 2), tmp2[:, :], csl(x1f, mt2 % 2))
                    ctx.inc(mi, "ve", 1)
                    ctx.mark(f"ve:out{ntc}_{mt2}")
                if "d_g" in dbg:
                    pass

        for name, fn in (("gpsimd", gps), ("sync", syn), ("scalar", sca),
                         ("tensor", ten), ("vector", vec)):
            if real:
                getattr(block, name)(fn)
            else:
                fn(_Mock())
        es.close()

    emit(False)
    emit(False)
    nc = bass.Bass("TRN2", target_bir_lowering=False, debug=False)
    emit(True, nc)
    return nc


def host_inputs_c2(inp, core, x1_full, s1_full, xn_full, alp):
    b, hf = core // 2, core % 2
    t0 = hf * TH
    return {
        "x1t": _np(np.asarray(x1_full[b])[t0:t0 + TH].T),
        "s1t": np.ascontiguousarray(np.asarray(s1_full[b]).T[:, t0:t0 + TH]),
        "xnt": np.ascontiguousarray(np.asarray(xn_full[b]).T[:, t0:t0 + TH]),
        "wut_in": _np(np.asarray(inp["L1_ffn_w_up"]).T),
        "wdt_in": _np(np.asarray(inp["L1_ffn_w_down"]).T),
        "alp_in": _np(np.asarray(alp).reshape(1, 2)),
    }


_NC_CACHE = {}


def _bf(a):
    a = np.asarray(a)
    if a.dtype == np.dtype("V2"):
        a = a.view(ml_dtypes.bfloat16)
    return a


def kernel(**inputs):
    for key, bld in (("A", build_launch_a), ("B", build_launch_b),
                     ("C1", build_launch_c1), ("C2", build_launch_c2)):
        if key not in _NC_CACHE:
            _NC_CACHE[key] = bld()
    cores = list(range(8))
    res = run_bass_kernel_spmd(_NC_CACHE["A"], [host_inputs_a(inputs, c) for c in cores],
                               core_ids=cores)
    x1 = np.zeros((B, S, D), np.float32)
    for c in cores:
        b, h = c // 2, c % 2
        x1[b, h * TH:(h + 1) * TH] = np.asarray(res.results[c]["x1t"], np.float32).T
    res = run_bass_kernel_spmd(_NC_CACHE["B"], [host_inputs_b(inputs, c, x1) for c in cores],
                               core_ids=cores)
    v_full = np.zeros((B, S, D), ml_dtypes.bfloat16)
    dl_full = np.zeros((B, S, D), ml_dtypes.bfloat16)
    for c in cores:
        b, p = c // 2, c % 2
        r = res.results[c]
        v_full[b, :, p * 512:(p + 1) * 512] = _bf(r["vfm_out"]).T
        dl_full[b, :, (2 * p) * 256:(2 * p + 1) * 256] = _bf(r["dseqA"])
        dl_full[b, :, (2 * p + 1) * 256:(2 * p + 2) * 256] = _bf(r["dseqB"])
    res = run_bass_kernel_spmd(_NC_CACHE["C1"],
                               [host_inputs_c1(inputs, c, x1, v_full, dl_full) for c in cores],
                               core_ids=cores)
    s1_full = np.zeros((B, S, D), ml_dtypes.bfloat16)
    xn_full = np.zeros((B, S, D), ml_dtypes.bfloat16)
    for c in cores:
        b, hf = c // 2, c % 2
        r = res.results[c]
        s1_full[b, hf * TH:(hf + 1) * TH] = _bf(r["s1_o"]).T
        xn_full[b, hf * TH:(hf + 1) * TH] = _bf(r["xn_o"]).T
    alp = np.asarray(res.results[0]["alp_o"], np.float32)
    res = run_bass_kernel_spmd(_NC_CACHE["C2"],
                               [host_inputs_c2(inputs, c, x1, s1_full, xn_full, alp) for c in cores],
                               core_ids=cores)
    out = np.zeros((B, S, D), np.float32)
    for c in cores:
        b, hf = c // 2, c % 2
        out[b, hf * TH:(hf + 1) * TH] = np.asarray(res.results[c]["out_o"], np.float32).T
    return out


# version 05
# speedup vs baseline: 1.0091x; 1.0091x over previous
import numpy as np
import ml_dtypes
import concourse.bass as bass
import concourse.mybir as mybir
from concourse.bass_utils import run_bass_kernel_spmd
from contextlib import ExitStack

F32 = mybir.dt.float32
BF16 = mybir.dt.bfloat16
AF = mybir.ActivationFunctionType
ALU = mybir.AluOpType
AX = mybir.AxisListType

NP, NC = 128, 512
B, S, D = 4, 4096, 1024
TH = S // 2
KT = D // NP
RSQRT2 = 2.0 ** (-0.5)
E, I_DIM, PASS_I = 8, 2688, 384
H0 = 8
LN_EPS = 1e-5
NT8, NT4 = S // NC, TH // NC
PASSES = I_DIM // PASS_I
TOTAL_P = E * PASSES
PI_MT = (PASS_I + NP - 1) // NP


class _Mock:
    def __getattr__(self, k):
        return self
    def __call__(self, *a, **kw):
        return self
    def __getitem__(self, k):
        return self
    def __enter__(self):
        return self
    def __exit__(self, *a):
        return False


class Ctx:
    def __init__(self, sem_names):
        self.marks = {}
        self.sem_names = sem_names

    def start(self, real, nc=None, es=None):
        self.real = real
        self.cnt = {n: 0 for n in self.sem_names}
        self.sem = ({n: es.enter_context(nc.semaphore(n)) for n in self.sem_names}
                    if real else {n: None for n in self.sem_names})

    def inc(self, inst, name, k=1):
        self.cnt[name] += k
        if self.real:
            inst.then_inc(self.sem[name], k)
        return inst

    def mark(self, label, offset=0):
        self.marks[label] = self.cnt[label.split(":")[0]] + offset

    def wait(self, eng, label, offset=0):
        val = self.marks.get(label, 0) + offset
        if self.real and val > 0:
            eng.wait_ge(self.sem[label.split(":")[0]], val)

    def wait_cnt(self, eng, name, val):
        if self.real and val > 0:
            eng.wait_ge(self.sem[name], val)


class Ctx2:
    def __init__(self, base, pref):
        self._b = base
        self._p = pref
    def _t(self, label):
        a, b = label.split(":", 1)
        return f"{a}:{self._p}{b}"
    @property
    def real(self):
        return self._b.real
    @property
    def sem(self):
        return self._b.sem
    @property
    def cnt(self):
        return self._b.cnt
    def inc(self, inst, name, k=1):
        return self._b.inc(inst, name, k)
    def mark(self, label, offset=0):
        self._b.marks[self._t(label)] = self._b.cnt[label.split(":")[0]] + offset
    def wait(self, eng, label, offset=0):
        self._b.wait(eng, self._t(label), offset)
    def wait_cnt(self, eng, name, val):
        self._b.wait_cnt(eng, name, val)


def accum_mm(t, out_ap, pairs):
    mi = None
    n = len(pairs)
    for i, (l, r) in enumerate(pairs):
        mi = t.matmul(out_ap, l, r, start=(i == 0), stop=(i == n - 1))
    return mi


def rows_of(it):
    return NP if (it + 1) * NP <= PASS_I else PASS_I - it * NP


A_SEMS = ["lg", "lh", "lw", "lsc", "mm", "mmy", "ve", "se", "st", "vi", "vo", "silu"]


def _phase_a(ctx, real, nc, es, debug=()):
    if True:
        if real:
            def di(name, shape, dt=F32):
                return nc.dram_tensor(name, shape, dt, kind="ExternalInput").ap()
            sbuf = lambda name, shape, dt: es.enter_context(nc.sbuf_tensor(name, shape, dt))
            psum = lambda name, shape, dt=F32: es.enter_context(nc.psum_tensor(name, shape, dt))
        else:
            di = lambda *a, **k: _Mock()
            sbuf = lambda *a, **k: _Mock()
            psum = lambda *a, **k: _Mock()

        xt = di("xt", [D, S])
        wq_d = di("wq", [D, 512]); wk_d = di("wk", [D, 512]); wv_d = di("wv", [D, 512])
        wout_d = di("wout", [512, D])
        bq_c = di("bq_c", [NP, 4]); bk_c = di("bk_c", [NP, 4])
        bv_row = di("bv_row", [1, 512]); bout_c = di("bout_c", [NP, 8])
        gam_c = di("gam_c", [NP, 8]); bet_c = di("bet_c", [NP, 8])
        rw_t = di("rw_t", [D, 8]); a1_t = di("a1_t", [D, 64]); a2_t = di("a2_t", [64, 8])
        b1_c = di("b1_c", [64, 1]); b2_c = di("b2_c", [8, 1])
        gw = di("gw", [E, D, I_DIM]); uw = di("uw", [E, D, I_DIM]); dw = di("dw", [E, I_DIM, D])
        onec_in = di("onec_in", [NP, 1], BF16)
        argw_in = di("argw_in", [1, 8])
        ident_in = di("ident_in", [NP, NP], BF16)
        if real:
            x1t = nc.dram_tensor("x1t", [D, TH], F32, kind="ExternalOutput").ap()
            scr = nc.dram_tensor("scr", [16, S], F32, kind="Internal").ap()
            scores_dram = nc.dram_tensor("scores_dram", [E, TH], F32, kind="Internal").ap()
            s_dram = nc.dram_tensor("s_dram", [D, TH], BF16, kind="Internal").ap()
            dbg = {}
            for nm in debug:
                shape = {"d_xn": [D, S], "d_scores": [E, TH], "d_s": [D, TH],
                         "d_o": [512, TH], "d_k": [512, S], "d_q": [512, TH],
                         "d_lg": [40, TH], "d_moe": [D, TH], "d_scr": [8, S],
                         "d_ah": [64, TH], "d_tm": [NP, 16 * 40],
                         "d_spl": [NP, KT * 16], "d_xnlo": [D, TH]}[nm]
                dt_ = F32 if nm in ("d_scores", "d_lg", "d_moe", "d_scr", "d_ah", "d_tm") else BF16
                dbg[nm] = nc.dram_tensor(nm, shape, dt_, kind="ExternalOutput").ap()
        else:
            x1t = scr = scores_dram = s_dram = _Mock()
            dbg = {nm: _Mock() for nm in debug}

        cols = sbuf("cols", [NP, 64], F32)
        onec = sbuf("onec", [NP, 1], BF16)
        ident = sbuf("ident", [NP, NP], BF16)
        onecB = sbuf("onecB", [NP, NP], BF16)
        s128 = sbuf("s128", [NP, 96], F32)
        rowstage = sbuf("rowstage", [1, 2 * NC], F32)
        scs = sbuf("scs", [NP, NC], F32)

        def scoped(es_, side="left"):
            return (lambda name, shape, dt: es_.enter_context(nc.sbuf_tensor(name, shape, dt, side=side))) if real else (lambda *a: _Mock())
        xn_own = sbuf("xn_own", [NP, KT * TH], BF16)
        es_o = ExitStack(); sb_o = scoped(es_o, "right")
        xn_oth = sb_o("xn_oth", [NP, KT * TH], BF16)
        es_x = ExitStack(); sb_x = scoped(es_x)
        xbf = sb_x("xbf", [NP, KT * S], BF16)
        sq0 = sb_x("sq0", [NP, 8 * NC], BF16)
        rt0 = sb_x("rt0", [NP, S], BF16)
        st0 = sb_x("st0", [NP, S], BF16)
        es_l = ExitStack(); sb_l = scoped(es_l, "right")
        xnlo = sb_l("xnlo", [NP, KT * TH], BF16)
        if real:
            es_x.close()
        es_w = ExitStack(); sb_w = scoped(es_w)
        wq_sb = sb_w("wq_sb", [NP, KT * 512], BF16)
        wk_sb = sb_w("wk_sb", [NP, KT * 512], BF16)
        es_d = ExitStack(); sb_d = scoped(es_d)
        ah = sb_d("ah", [64, TH], F32)
        ah2 = sb_d("ah2", [64, TH], F32)
        ahhi = sb_d("ahhi", [64, TH], BF16); ahlo = sb_d("ahlo", [64, TH], BF16)
        lgfm = sb_d("lgfm", [40, TH], F32)
        lg_hi = sb_d("lg_hi", [40, TH], BF16)
        lg_lo = sb_d("lg_lo", [40, TH], BF16)
        wf32 = sb_d("wf32", [NP, 80], F32)
        rwhi = sb_d("rwhi", [NP, KT * 8], BF16); rwlo = sb_d("rwlo", [NP, KT * 8], BF16)
        a1hi = sb_d("a1hi", [NP, KT * 64], BF16); a1lo = sb_d("a1lo", [NP, KT * 64], BF16)
        a2hi = sb_d("a2hi", [64, 8], BF16); a2lo = sb_d("a2lo", [64, 8], BF16)
        tm = sb_d("tm", [NP, 16 * 40], F32)
        tmw = sb_d("tmw", [NP, 16 * 16], F32)
        tmc = sb_d("tmc", [NP, 16 * 8], F32)
        tmx = sb_d("tmx", [NP, 16], F32)
        scb16 = sb_d("scb16", [NP, 16 * 8], BF16)
        argw = sb_d("argw", [NP, 8], F32)
        if real:
            es_l.close()
        es_k = ExitStack(); sb_k = scoped(es_k, "right")
        kfm = sb_k("kfm", [NP, 4 * S], BF16)
        qfm = sb_k("qfm", [NP, 4 * TH], BF16)
        if real:
            es_d.close()
            es_w.close()
        es_f = ExitStack(); sb_f = scoped(es_f)
        wv_sb = sb_f("wv_sb", [NP, KT * 512], BF16)
        wo_sb = sb_f("wo_sb", [NP, 4 * D], BF16)
        vtok = sb_f("vtok", [NP, 32 * 520], BF16)
        ofm = sb_f("ofm", [NP, 4 * TH], BF16)
        ychunk = sb_f("ychunk", [NP, 8 * NC], BF16)
        expb = [sb_f(f"expb{i}", [NP, NC], BF16) for i in range(2)]
        nrm = sb_f("nrm", [NP, TH], BF16)
        bvt = sb_f("bvt", [NP, 512], BF16)
        rt2 = sb_f("rt2", [NP, TH], BF16)
        st2 = sb_f("st2", [NP, TH], BF16)
        if real:
            es_f.close()
            es_k.close()
            es_o.close()
        wg_b = [sbuf(f"wg_b{i}", [NP, KT * PASS_I], BF16) for i in range(2)]
        wu_b = [sbuf(f"wu_b{i}", [NP, KT * PASS_I], BF16) for i in range(2)]
        wd_b = [sbuf(f"wd_b{i}", [NP, PI_MT * D], BF16) for i in range(2)]
        moe_out = sbuf("moe_out", [NP, KT * TH], F32)
        inter = sbuf("inter", [NP, PI_MT * TH], BF16)
        sc_b = sbuf("sc_b", [NP, TH], F32)
        xf32 = sbuf("xf32", [NP, TH], F32)
        sbb = sbuf("sbb", [NP, TH], BF16)
        gtmp = sbuf("gtmp", [NP, TH], BF16)

        bk = [psum(f"bk{i}", [NP, NC]) for i in range(6)]
        p_row = psum("p_row", [1, NC])
        p_tr = psum("p_tr", [NP, 416], BF16)


        fsl = lambda buf, kt, c0, w=NC, Tlen=S: buf[:, kt * Tlen + c0: kt * Tlen + c0 + w]
        def xnsl(kt, c0, w=NC):
            if c0 < TH:
                return xn_own[:, kt * TH + c0: kt * TH + c0 + w]
            return xn_oth[:, kt * TH + (c0 - TH): kt * TH + (c0 - TH) + w]
        hb = lambda hh: (hh % 2) * 64
        QL = 4 * NT4 - 1

        def gps(g):
            for kt in range(KT):
                ctx.inc(g.dma_start(out=fsl(xbf, kt, 0, S), in_=xt[kt * NP:(kt + 1) * NP, :]), "lg", 16)
            ctx.inc(g.dma_start(out=onec[:], in_=onec_in), "lg", 16)
            ctx.inc(g.dma_start(out=ident[:], in_=ident_in), "lg", 16)
            ctx.inc(g.dma_start(out=cols[:, 0:4], in_=bq_c), "lg", 16)
            ctx.inc(g.dma_start(out=cols[:, 4:8], in_=bk_c), "lg", 16)
            ctx.inc(g.dma_start(out=cols[:, 8:16], in_=bout_c), "lg", 16)
            ctx.inc(g.dma_start(out=cols[:, 16:24], in_=gam_c), "lg", 16)
            ctx.inc(g.dma_start(out=cols[:, 24:32], in_=bet_c), "lg", 16)
            ctx.inc(g.dma_start(out=cols[0:64, 32:33], in_=b1_c), "lg", 16)
            ctx.inc(g.dma_start(out=cols[0:8, 33:34], in_=b2_c), "lg", 16)
            ctx.mark("lg:init")
            ctx.wait(g, "lh:ln0rows")
            ctx.inc(g.dma_start(out=rt0[:], in_=scr[2:3, :].partition_broadcast(NP)), "lg", 16)
            ctx.inc(g.dma_start(out=st0[:], in_=scr[3:4, :].partition_broadcast(NP)), "lg", 16)
            ctx.mark("lg:rs")
            ctx.wait(g, "ve:xn_done")
            for w_d, w_s in ((wq_d, wq_sb), (wk_d, wk_sb)):
                for kt in range(KT):
                    ctx.inc(g.dma_start(out=w_s[:, kt * 512:(kt + 1) * 512],
                                        in_=w_d[kt * NP:(kt + 1) * NP, :]), "lg", 16)
            ctx.inc(g.dma_start(out=argw[:], in_=argw_in.partition_broadcast(NP)), "lg", 16)
            ctx.mark("lg:wqk")
            for kt in range(KT):
                if kt > 0:
                    ctx.wait(g, f"ve:split{kt - 1}")
                ctx.inc(g.dma_start(out=wf32[:, 0:8], in_=rw_t[kt * NP:(kt + 1) * NP, :]), "lg", 16)
                ctx.inc(g.dma_start(out=wf32[:, 8:72], in_=a1_t[kt * NP:(kt + 1) * NP, :]), "lg", 16)
                ctx.mark(f"lg:wf{kt}")
            ctx.wait(g, f"ve:split{KT - 1}")
            ctx.inc(g.dma_start(out=wf32[0:64, 0:8], in_=a2_t), "lg", 16)
            ctx.mark("lg:a2")
            ctx.wait(g, f"mm:q{QL}")
            ctx.wait(g, "lh:scores")
            for kt in range(KT):
                ctx.inc(g.dma_start(out=wv_sb[:, kt * 512:(kt + 1) * 512],
                                    in_=wv_d[kt * NP:(kt + 1) * NP, :]), "lg", 16)
            for kt in range(4):
                ctx.inc(g.dma_start(out=wo_sb[:, kt * D:(kt + 1) * D],
                                    in_=wout_d[kt * NP:(kt + 1) * NP, :]), "lg", 16)
            ctx.inc(g.dma_start(out=bvt[:, :], in_=bv_row.partition_broadcast(NP)), "lg", 16)
            ctx.mark("lg:wv")
            ctx.wait(g, "lh:boundrow")
            ctx.inc(g.dma_start(out=cols[:, 34:38], in_=scr[1:2, 0:4].partition_broadcast(NP)), "lg", 16)
            ctx.mark("lg:bound")
            for hh in range(H0):
                ctx.wait(g, f"lh:recip{hh}")
                if hh >= 1:
                    ctx.wait(g, f"ve:onorm{hh - 1}")
                ctx.inc(g.dma_start(out=nrm[:, :], in_=scr[8 + hh:9 + hh, 0:TH].partition_broadcast(NP)), "lg", 16)
                ctx.mark(f"lg:nrm{hh}")
            ctx.wait(g, "lh:ln1rows")
            ctx.inc(g.dma_start(out=rt2[:, :], in_=scr[6:7, 0:TH].partition_broadcast(NP)), "lg", 16)
            ctx.inc(g.dma_start(out=st2[:, :], in_=scr[7:8, 0:TH].partition_broadcast(NP)), "lg", 16)
            ctx.mark("lg:rs2")
            for gi in range(TOTAL_P):
                e, p = divmod(gi, PASSES)
                slot = gi % 2
                if p == 0:
                    if e >= 1:
                        ctx.wait_cnt(g, "vo", e * PASSES * KT * NT4)
                    else:
                        ctx.wait(g, "ve:sdone")
                    ctx.inc(g.dma_start(out=sc_b[:, :],
                                        in_=scores_dram[e:e + 1, :].partition_broadcast(NP)), "lsc", 16)
                if gi == 0:
                    ctx.wait(g, "ve:sdone")
                if gi >= 2:
                    ctx.wait_cnt(g, "mmy", (gi - 1) * KT * NT4)
                i0 = p * PASS_I
                for kt in range(KT):
                    ctx.inc(g.dma_start(out=wg_b[slot][:, kt * PASS_I:(kt + 1) * PASS_I],
                                        in_=gw[e][kt * NP:(kt + 1) * NP, i0:i0 + PASS_I]), "lw", 16)
                    ctx.inc(g.dma_start(out=wu_b[slot][:, kt * PASS_I:(kt + 1) * PASS_I],
                                        in_=uw[e][kt * NP:(kt + 1) * NP, i0:i0 + PASS_I]), "lw", 16)
                for it in range(PI_MT):
                    ctx.inc(g.dma_start(out=wd_b[slot][:, it * D:(it + 1) * D],
                                        in_=dw[e][i0 + it * NP:i0 + (it + 1) * NP, :]), "lw", 16)

        def syn(sy):
            for ch in range(NT8):
                ctx.wait(sy, f"ve:sta{ch}")
                ctx.inc(sy.dma_start(out=scr[0:1, ch * NC:(ch + 1) * NC], in_=rowstage[0:1, 0:NC]), "lh", 16)
                ctx.mark(f"lh:sta{ch}")
                ctx.wait(sy, f"ve:stevac{ch}")
                ctx.inc(sy.dma_start(out=scr[4:5, ch * NC:(ch + 1) * NC], in_=rowstage[0:1, NC:2 * NC]), "lh", 16)
                ctx.mark(f"lh:st{ch}")
            ctx.inc(sy.dma_start(out=s128[:, 0:32], in_=scr[0:1, :].rearrange("o (p c) -> (o p) c", p=NP)), "lh", 16)
            ctx.inc(sy.dma_start(out=s128[:, 32:64], in_=scr[4:5, :].rearrange("o (p c) -> (o p) c", p=NP)), "lh", 16)
            ctx.mark("lh:ln0cols")
            ctx.wait(sy, "ve:ln0rs")
            ctx.inc(sy.dma_start(out=scr[2:3, :].rearrange("o (p c) -> (o p) c", p=NP), in_=s128[:, 32:64]), "lh", 16)
            ctx.inc(sy.dma_start(out=scr[3:4, :].rearrange("o (p c) -> (o p) c", p=NP), in_=s128[:, 64:96]), "lh", 16)
            ctx.mark("lh:ln0rows")
            if "d_xn" in dbg:
                ctx.wait(sy, "ve:xn_done")
                for kt in range(KT):
                    ctx.inc(sy.dma_start(out=dbg["d_xn"][kt * NP:(kt + 1) * NP, 0:TH],
                                         in_=xn_own[:, kt * TH:(kt + 1) * TH]), "st", 16)
                    ctx.inc(sy.dma_start(out=dbg["d_xn"][kt * NP:(kt + 1) * NP, TH:S],
                                         in_=xn_oth[:, kt * TH:(kt + 1) * TH]), "st", 16)
            if "d_spl" in dbg:
                ctx.wait(sy, f"ve:split{KT - 1}")
                ctx.inc(sy.dma_start(out=dbg["d_spl"][:, 0:KT * 8], in_=rwhi[:, :]), "st", 16)
                ctx.inc(sy.dma_start(out=dbg["d_spl"][:, KT * 8:KT * 16], in_=rwlo[:, :]), "st", 16)
            if "d_xnlo" in dbg:
                ctx.wait(sy, "ve:xn_done")
                for kt in range(KT):
                    ctx.inc(sy.dma_start(out=dbg["d_xnlo"][kt * NP:(kt + 1) * NP, :],
                                         in_=xnlo[:, kt * TH:(kt + 1) * TH]), "st", 16)
            if "d_ah" in dbg:
                ctx.wait(sy, "ve:ah_split")
                ctx.inc(sy.dma_start(out=dbg["d_ah"][:, :], in_=ah[:, :]), "st", 16)
            if "d_tm" in dbg:
                ctx.wait(sy, f"ve:tr{15}")
                ctx.inc(sy.dma_start(out=dbg["d_tm"][:, :], in_=tm[:, :]), "st", 16)
            if "d_scr" in dbg:
                ctx.wait(sy, "lh:ln0rows")
                ctx.inc(sy.dma_start(out=dbg["d_scr"][:, :], in_=scr[0:8, :]), "st", 16)
            ctx.wait(sy, "ve:scores")
            ctx.inc(sy.dma_start(out=scores_dram[:, :], in_=lgfm[0:8, :]), "lh", 16)
            ctx.mark("lh:scores")
            if "d_scores" in dbg:
                ctx.inc(sy.dma_start(out=dbg["d_scores"][:, :], in_=lgfm[0:8, :]), "st", 16)
            if "d_lg" in dbg:
                ctx.wait(sy, "ve:lgdone")
                ctx.inc(sy.dma_start(out=dbg["d_lg"][:, :], in_=lgfm[:, :]), "st", 16)
            if "d_k" in dbg:
                ctx.wait(sy, f"ve:kev{4 * NT8 - 1}")
                for mt in range(4):
                    ctx.inc(sy.dma_start(out=dbg["d_k"][mt * NP:(mt + 1) * NP, :], in_=fsl(kfm, mt, 0, S)), "st", 16)
            if "d_q" in dbg:
                ctx.wait(sy, f"ve:qev{QL}")
                for mt in range(4):
                    ctx.inc(sy.dma_start(out=dbg["d_q"][mt * NP:(mt + 1) * NP, :], in_=fsl(qfm, mt, 0, TH, TH)), "st", 16)
            ctx.wait(sy, "ve:bound")
            ctx.inc(sy.dma_start(out=scr[1:2, 0:4], in_=rowstage[0:1, 8:12]), "lh", 16)
            ctx.mark("lh:boundrow")
            for hh in range(H0):
                for qc in range(NT4):
                    ctx.wait(sy, f"ve:den{hh}_{qc}")
                    ctx.inc(sy.dma_start(out=scr[8 + hh:9 + hh, qc * NC:(qc + 1) * NC],
                                         in_=rowstage[0:1, (qc % 2) * NC:(qc % 2 + 1) * NC]), "lh", 16)
                    ctx.mark(f"lh:den{hh}_{qc}")
                ctx.inc(sy.dma_start(out=s128[:, 48:64],
                                     in_=scr[8 + hh:9 + hh, 0:TH].rearrange("o (p c) -> (o p) c", p=NP)), "lh", 16)
                ctx.mark(f"lh:dload{hh}")
                ctx.wait(sy, f"ve:recip{hh}")
                ctx.inc(sy.dma_start(out=scr[8 + hh:9 + hh, 0:TH].rearrange("o (p c) -> (o p) c", p=NP),
                                     in_=s128[:, 48:64]), "lh", 16)
                ctx.mark(f"lh:recip{hh}")
            if "d_o" in dbg:
                ctx.wait(sy, "ve:onorm7")
                for mt in range(4):
                    ctx.inc(sy.dma_start(out=dbg["d_o"][mt * NP:(mt + 1) * NP, :], in_=fsl(ofm, mt, 0, TH, TH)), "st", 16)
            for ch in range(NT4):
                ctx.wait(sy, f"ve:ysta{ch}")
                ctx.inc(sy.dma_start(out=scr[5:6, ch * NC:(ch + 1) * NC], in_=rowstage[0:1, 0:NC]), "lh", 16)
                ctx.mark(f"lh:ysta{ch}")
                ctx.wait(sy, f"ve:ystevac{ch}")
                ctx.inc(sy.dma_start(out=scr[5:6, TH + ch * NC:TH + (ch + 1) * NC], in_=rowstage[0:1, NC:2 * NC]), "lh", 16)
                ctx.mark(f"lh:yst{ch}")
            ctx.inc(sy.dma_start(out=s128[:, 0:16], in_=scr[5:6, 0:TH].rearrange("o (p c) -> (o p) c", p=NP)), "lh", 16)
            ctx.inc(sy.dma_start(out=s128[:, 16:32], in_=scr[5:6, TH:2 * TH].rearrange("o (p c) -> (o p) c", p=NP)), "lh", 16)
            ctx.mark("lh:ln1cols")
            ctx.wait(sy, "ve:ln1rs")
            ctx.inc(sy.dma_start(out=scr[6:7, 0:TH].rearrange("o (p c) -> (o p) c", p=NP), in_=s128[:, 16:32]), "lh", 16)
            ctx.inc(sy.dma_start(out=scr[7:8, 0:TH].rearrange("o (p c) -> (o p) c", p=NP), in_=s128[:, 32:48]), "lh", 16)
            ctx.mark("lh:ln1rows")
            for ntc in range(NT4):
                for mt in range(KT):
                    ctx.wait(sy, f"ve:s{ntc}_{mt}")
                    ctx.inc(sy.dma_start(out=s_dram[mt * NP:(mt + 1) * NP, ntc * NC:(ntc + 1) * NC],
                                         in_=ychunk[:, mt * NC:(mt + 1) * NC]), "lh", 16)
                    ctx.mark(f"lh:s{ntc}_{mt}")
            ctx.mark("lh:sspill")
            if "d_s" in dbg:
                for kt in range(KT):
                    ctx.inc(sy.dma_start(out=dbg["d_s"][kt * NP:(kt + 1) * NP, :],
                                         in_=s_dram[kt * NP:(kt + 1) * NP, :]), "st", 16)
            for kt in range(KT):
                if kt >= 1:
                    ctx.wait(sy, f"ve:x1_{kt - 1}")
                ctx.inc(sy.dma_start(out=xf32[:, :], in_=xt[kt * NP:(kt + 1) * NP, 0:TH]), "lh", 16)
                ctx.inc(sy.dma_start(out=sbb[:, :], in_=s_dram[kt * NP:(kt + 1) * NP, :]), "lh", 16)
                ctx.mark(f"lh:ep{kt}")
            if "d_moe" in dbg:
                ctx.wait_cnt(sy, "vo", TOTAL_P * KT * NT4)
                for kt in range(KT):
                    ctx.inc(sy.dma_start(out=dbg["d_moe"][kt * NP:(kt + 1) * NP, :], in_=fsl(moe_out, kt, 0, TH, TH)), "st", 16)
            for kt in range(KT):
                ctx.wait(sy, f"ve:x1_{kt}")
                ctx.inc(sy.dma_start(out=x1t[kt * NP:(kt + 1) * NP, :], in_=fsl(moe_out, kt, 0, TH, TH)), "st", 16)
            if real:
                sy.wait_ge(ctx.sem["st"], ctx.cnt["st"])

        def ten(t):
            ctx.wait(t, "lg:init")
            for ch in range(NT8):
                mi = accum_mm(t, p_row[0:1, :], [(onec[:, :], fsl(xbf, kt, ch * NC)) for kt in range(KT)])
                ctx.inc(mi, "mm", 1)
                ctx.mark(f"mm:sta{ch}")
                ctx.wait(t, f"ve:sta{ch}")
                ctx.wait(t, f"se:xsq{ch}")
                mi = accum_mm(t, p_row[0:1, :], [(onec[:, :], sq0[:, kt * NC:(kt + 1) * NC]) for kt in range(KT)])
                ctx.inc(mi, "mm", 1)
                ctx.mark(f"mm:st{ch}")
                ctx.wait(t, f"ve:stevac{ch}")
            ctx.wait(t, "ve:xn_done")
            ctx.wait(t, f"ve:split{KT - 1}")
            for ntc in range(NT4):
                pairs = []
                for kt in range(KT):
                    pairs += [(rwhi[:, kt * 8:(kt + 1) * 8], xnsl(kt, ntc * NC)),
                              (rwlo[:, kt * 8:(kt + 1) * 8], xnsl(kt, ntc * NC)),
                              (rwhi[:, kt * 8:(kt + 1) * 8], fsl(xnlo, kt, ntc * NC, NC, TH))]
                ctx.inc(accum_mm(t, bk[0][0:8, :], pairs), "mm", 1)
                ctx.mark(f"mm:rl{ntc}")
                pairs = []
                for kt in range(KT):
                    pairs += [(a1hi[:, kt * 64:(kt + 1) * 64], xnsl(kt, ntc * NC)),
                              (a1lo[:, kt * 64:(kt + 1) * 64], xnsl(kt, ntc * NC)),
                              (a1hi[:, kt * 64:(kt + 1) * 64], fsl(xnlo, kt, ntc * NC, NC, TH))]
                ctx.inc(accum_mm(t, bk[1][0:64, :], pairs), "mm", 1)
                ctx.mark(f"mm:a1_{ntc}")
                ctx.wait(t, f"ve:rlevac{ntc}")
            ctx.wait(t, "ve:ah_split")
            ctx.wait(t, "ve:a2split")
            for ntc in range(NT4):
                pairs = [(a2hi[:, :], ahhi[:, ntc * NC:(ntc + 1) * NC]),
                         (a2lo[:, :], ahhi[:, ntc * NC:(ntc + 1) * NC]),
                         (a2hi[:, :], ahlo[:, ntc * NC:(ntc + 1) * NC])]
                ctx.inc(accum_mm(t, bk[0][0:8, :], pairs), "mm", 1)
                ctx.mark(f"mm:a2_{ntc}")
                ctx.wait(t, f"ve:a2evac{ntc}")
            ctx.wait(t, "ve:lg_split")
            for blk in range(16):
                q0 = (blk % 2) * 80
                if blk % 2 == 0 and blk >= 2:
                    ctx.wait(t, f"ve:tr{blk - 1}")
                t.transpose(p_tr[:, q0:q0 + 40], lg_hi[:, blk * 128:(blk + 1) * 128], ident[0:40, 0:40])
                mi = t.transpose(p_tr[:, q0 + 40:q0 + 80], lg_lo[:, blk * 128:(blk + 1) * 128], ident[0:40, 0:40])
                ctx.inc(mi, "mm", 1)
                ctx.mark(f"mm:tr{blk}")
            for blk in range(16):
                ctx.wait(t, f"ve:scb{blk}")
                if blk >= 1:
                    ctx.wait(t, f"ve:sctr{blk - 1}")
                mi = t.transpose(p_tr[0:8, 160 + (blk % 2) * 128: 160 + (blk % 2) * 128 + 128],
                                 scb16[:, blk * 8:(blk + 1) * 8], ident[:, :])
                ctx.inc(mi, "mm", 1)
                ctx.mark(f"mm:sctr{blk}")
            ctx.wait(t, "lg:wqk")
            for mt in range(4):
                for ntc in range(NT8):
                    idx = mt * NT8 + ntc
                    pairs = [(wk_sb[:, kt * 512 + mt * NP: kt * 512 + (mt + 1) * NP], xnsl(kt, ntc * NC))
                             for kt in range(KT)]
                    ctx.inc(accum_mm(t, bk[idx % 2][:, :], pairs), "mm", 1)
                    ctx.mark(f"mm:k{idx}")
                    if idx >= 1:
                        ctx.wait(t, f"ve:kev{idx - 1}")
            for mt in range(4):
                for ntc in range(NT4):
                    idx = mt * NT4 + ntc
                    pairs = [(wq_sb[:, kt * 512 + mt * NP: kt * 512 + (mt + 1) * NP], xnsl(kt, ntc * NC))
                             for kt in range(KT)]
                    ctx.inc(accum_mm(t, bk[idx % 2][:, :], pairs), "mm", 1)
                    ctx.mark(f"mm:q{idx}")
                    if idx >= 1:
                        ctx.wait(t, f"ve:qev{idx - 1}")
            ctx.wait(t, "lg:wv")
            for jt in range(32):
                pairs = [(xnsl(kt, jt * NP, NP), wv_sb[:, kt * 512:(kt + 1) * 512])
                         for kt in range(KT)]
                ctx.inc(accum_mm(t, bk[2 + jt % 2][:, :], pairs), "mm", 1)
                ctx.mark(f"mm:v{jt}")
                if jt >= 1:
                    ctx.wait(t, f"ve:vev{jt - 1}")
            bnd = [("k", mt, ch) for mt in range(4) for ch in range(NT8)] + \
                  [("q", mt, ch) for mt in range(4) for ch in range(NT4)]
            for gidx, (tag, mt, ch) in enumerate(bnd):
                ctx.wait(t, f"se:{tag}sq{mt}_{ch}")
                mi = t.matmul(p_row[0:1, :], onec[:, :],
                              expb[gidx % 2][:, :], start=True, stop=True)
                ctx.inc(mi, "mm", 1)
                ctx.mark(f"mm:{tag}sq{mt}_{ch}")
                ctx.wait(t, f"ve:{tag}sqm{mt}_{ch}")
            ctx.wait(t, "lg:bound")
            ctx.wait(t, "ve:vones")
            for hh in range(H0):
                for qc in range(NT4):
                    for jt in range(32):
                        if jt >= 2:
                            ctx.wait(t, f"se:exp{hh}_{qc}_{jt - 2}")
                        mi = t.matmul(bk[jt % 2][:, :],
                                      kfm[hb(hh):hb(hh) + 64, (hh // 2) * S + jt * NP:(hh // 2) * S + (jt + 1) * NP],
                                      qfm[hb(hh):hb(hh) + 64, (hh // 2) * TH + qc * NC:(hh // 2) * TH + (qc + 1) * NC],
                                      start=True, stop=True)
                        ctx.inc(mi, "mm", 1)
                        ctx.mark(f"mm:sim{hh}_{qc}_{jt}")
                        ctx.wait(t, f"se:exp{hh}_{qc}_{jt}")
                        mi = t.matmul(bk[2][0:65, :], vtok[:, jt * 520 + hh * 65: jt * 520 + (hh + 1) * 65],
                                      expb[jt % 2][:, :], start=(jt == 0), stop=(jt == 31))
                        ctx.inc(mi, "mm", 1)
                        ctx.mark(f"mm:av{hh}_{qc}_{jt}")
                    ctx.wait(t, f"ve:oevac{hh}_{qc}")
            ctx.wait(t, "ve:onorm7")
            for ntc in range(NT4):
                for mt in range(KT):
                    pairs = [(wo_sb[:, kt * D + mt * NP: kt * D + (mt + 1) * NP],
                              fsl(ofm, kt, ntc * NC, NC, TH)) for kt in range(4)]
                    ctx.inc(accum_mm(t, bk[mt % 2][:, :], pairs), "mm", 1)
                    ctx.mark(f"mm:y1_{ntc}_{mt}")
                    if mt >= 1:
                        ctx.wait(t, f"ve:yev1_{ntc}_{mt - 1}")
                ctx.wait(t, f"ve:yev1_{ntc}_{KT - 1}")
                mi = accum_mm(t, p_row[0:1, :], [(onec[:, :], ychunk[:, mt2 * NC:(mt2 + 1) * NC]) for mt2 in range(KT)])
                ctx.inc(mi, "mm", 1)
                ctx.mark(f"mm:ysta{ntc}")
                ctx.wait(t, f"ve:ysta{ntc}")
                for mt2 in range(KT):
                    ctx.wait(t, f"se:ysq{ntc}_{mt2}")
                    mi = t.matmul(p_row[0:1, :], onec[:, :], expb[mt2 % 2][:, :],
                                  start=(mt2 == 0), stop=(mt2 == KT - 1))
                    ctx.inc(mi, "mm", 1)
                    ctx.mark(f"mm:ysqc{ntc}_{mt2}")
                ctx.mark(f"mm:yst{ntc}")
                ctx.wait(t, f"ve:ystevac{ntc}")
            ctx.wait(t, "lg:rs2")
            for ntc in range(NT4):
                for mt in range(KT):
                    pairs = [(wo_sb[:, kt * D + mt * NP: kt * D + (mt + 1) * NP],
                              fsl(ofm, kt, ntc * NC, NC, TH)) for kt in range(4)]
                    ctx.inc(accum_mm(t, bk[mt % 2][:, :], pairs), "mm", 1)
                    ctx.mark(f"mm:y2_{ntc}_{mt}")
                    if mt >= 1:
                        ctx.wait(t, f"ve:s{ntc}_{mt - 1}")
                ctx.wait(t, f"ve:s{ntc}_{KT - 1}")
            n_gu, n_y = 0, 0
            for gi in range(TOTAL_P):
                slot = gi % 2
                ctx.wait_cnt(t, "lw", 16 * (2 * KT + PI_MT) * (gi + 1))
                for it in range(PI_MT):
                    for ntc in range(NT4):
                        par = (it * NT4 + ntc) % 2
                        if n_gu >= 2:
                            ctx.wait_cnt(t, "vi", n_gu - 1)
                        accum_mm(t, bk[par][:, :],
                                 [(wg_b[slot][:, kt * PASS_I + it * NP: kt * PASS_I + (it + 1) * NP],
                                   xnsl(kt, ntc * NC)) for kt in range(KT)])
                        mi = accum_mm(t, bk[2 + par][:, :],
                                      [(wu_b[slot][:, kt * PASS_I + it * NP: kt * PASS_I + (it + 1) * NP],
                                        xnsl(kt, ntc * NC)) for kt in range(KT)])
                        ctx.inc(mi, "mm", 1)
                        ctx.mark(f"mm:gu{n_gu}")
                        n_gu += 1
                ctx.wait_cnt(t, "vi", n_gu)
                for mt in range(KT):
                    for ntc in range(NT4):
                        par = (mt * NT4 + ntc) % 2
                        if n_y >= 2:
                            ctx.wait_cnt(t, "vo", n_y - 1)
                        mi = accum_mm(t, bk[4 + par][:, :],
                                      [(wd_b[slot][:, it * D + mt * NP: it * D + (mt + 1) * NP],
                                        inter[:, it * TH + ntc * NC: it * TH + (ntc + 1) * NC])
                                       for it in range(PI_MT)])
                        ctx.inc(mi, "mmy", 1)
                        n_y += 1

        def sca(s):
            ctx.wait(s, "lg:init")
            for ch in range(NT8):
                if ch >= 1:
                    ctx.wait(s, f"mm:st{ch - 1}")
                mi = None
                for kt in range(KT):
                    mi = s.activation(sq0[:, kt * NC:(kt + 1) * NC], fsl(xbf, kt, ch * NC), AF.Square)
                ctx.inc(mi, "se", 1)
                ctx.mark(f"se:xsq{ch}")
            ctx.wait(s, "ve:ln0var")
            ctx.inc(s.activation(s128[:, 32:64], s128[:, 32:64], AF.Sqrt, bias=cols[:, 38:39]), "se", 1)
            ctx.mark("se:ln0sqrt")
            ctx.wait(s, "ve:ah_raw")
            ctx.inc(s.activation(ah2[:, :], ah[:, :], AF.Square), "se", 1)
            ctx.mark("se:ah2")
            ctx.wait(s, "ve:ah_u")
            ctx.inc(s.activation(ah2[:, :], ah2[:, :], AF.Tanh, scale=0.7978845608028654), "se", 1)
            ctx.mark("se:ah_tanh")
            for blk in range(16):
                ctx.wait(s, f"ve:tmmax{blk}")
                ctx.inc(s.activation(tmw[:, blk * 16:blk * 16 + 8], tm[:, blk * 40:blk * 40 + 8], AF.Exp,
                                     bias=tmc[:, blk * 8:blk * 8 + 1],
                                     accum_out=tmc[:, blk * 8 + 1:blk * 8 + 2]), "se", 1)
                ctx.mark(f"se:pexp{blk}")
            ctx.wait(s, f"mm:q{QL}")
            bnd = [("k", mt, ch, kfm, S) for mt in range(4) for ch in range(NT8)] + \
                  [("q", mt, ch, qfm, TH) for mt in range(4) for ch in range(NT4)]
            for gidx, (tag, mt, ch, buf, Tlen) in enumerate(bnd):
                nt_ = NT8 if tag == "k" else NT4
                ctx.wait(s, f"ve:{tag}ev{mt * nt_ + ch}")
                if gidx >= 2:
                    pt, pm, pc, _, _ = bnd[gidx - 2]
                    ctx.wait(s, f"mm:{pt}sq{pm}_{pc}")
                ctx.inc(s.activation(expb[gidx % 2][:, :], fsl(buf, mt, ch * NC, NC, Tlen), AF.Square), "se", 1)
                ctx.mark(f"se:{tag}sq{mt}_{ch}")
            ctx.wait(s, "ve:boundprod")
            ctx.inc(s.activation(rowstage[0:1, 28:32], rowstage[0:1, 28:32], AF.Sqrt), "se", 1)
            ctx.mark("se:bsqrt")
            ctx.wait(s, "lg:bound")
            for hh in range(H0):
                for qc in range(NT4):
                    for jt in range(32):
                        ctx.wait(s, f"mm:sim{hh}_{qc}_{jt}")
                        if jt >= 2:
                            ctx.wait(s, f"mm:av{hh}_{qc}_{jt - 2}")
                        ctx.inc(s.activation(expb[jt % 2][:, :], bk[jt % 2][:, :], AF.Exp,
                                             scale=0.125, bias=cols[:, 34 + hh // 2:35 + hh // 2]), "se", 1)
                        ctx.mark(f"se:exp{hh}_{qc}_{jt}")
            for ntc in range(NT4):
                ctx.wait(s, f"mm:ysta{ntc}")
                for mt in range(KT):
                    if mt >= 2:
                        ctx.wait(s, f"mm:ysqc{ntc}_{mt - 2}")
                    elif ntc >= 1 and mt < 2:
                        ctx.wait(s, f"mm:ysqc{ntc - 1}_{KT - 2 + mt}")
                    ctx.inc(s.activation(expb[mt % 2][:, :], ychunk[:, mt * NC:(mt + 1) * NC], AF.Square), "se", 1)
                    ctx.mark(f"se:ysq{ntc}_{mt}")
            ctx.wait(s, "ve:ln1var")
            ctx.inc(s.activation(s128[:, 16:32], s128[:, 16:32], AF.Sqrt, bias=cols[:, 38:39]), "se", 1)
            ctx.mark("se:ln1sqrt")
            n_gu = 0
            for gi in range(TOTAL_P):
                for it in range(PI_MT):
                    for ntc in range(NT4):
                        par = (it * NT4 + ntc) % 2
                        ctx.wait(s, f"mm:gu{n_gu}")
                        ctx.inc(s.activation(inter[:, it * TH + ntc * NC: it * TH + (ntc + 1) * NC],
                                             bk[par][:, :], AF.Silu), "silu", 1)
                        n_gu += 1
            ctx.wait_cnt(s, "vo", TOTAL_P * KT * NT4)
            for kt in range(KT):
                if kt >= 1:
                    ctx.wait(s, f"ve:gused{kt - 1}")
                mi = s.activation(gtmp[:, :], fsl(moe_out, kt, 0, TH, TH), AF.Silu)
                ctx.inc(mi, "se", 1)
                ctx.mark(f"se:g{kt}")

        def vec(v):
            v.memset(cols[:, 38:39], LN_EPS)
            for ch in range(NT8):
                ctx.wait(v, f"mm:sta{ch}")
                if ch >= 1:
                    ctx.wait(v, f"lh:sta{ch - 1}")
                mi = v.tensor_copy(rowstage[0:1, 0:NC], p_row[0:1, :])
                ctx.inc(mi, "ve", 1)
                ctx.mark(f"ve:sta{ch}")
                ctx.wait(v, f"mm:st{ch}")
                if ch >= 1:
                    ctx.wait(v, f"lh:st{ch - 1}")
                mi = v.tensor_copy(rowstage[0:1, NC:2 * NC], p_row[0:1, :])
                ctx.inc(mi, "ve", 1)
                ctx.mark(f"ve:stevac{ch}")
            ctx.wait(v, "lh:ln0cols")
            v.tensor_scalar(s128[:, 0:32], s128[:, 0:32], 1.0 / D, None, ALU.mult)
            v.tensor_scalar(s128[:, 32:64], s128[:, 32:64], 1.0 / D, None, ALU.mult)
            v.tensor_mul(s128[:, 64:96], s128[:, 0:32], s128[:, 0:32])
            v.tensor_copy(cols[:, 39:40], cols[:, 38:39])
            mi = v.tensor_sub(s128[:, 32:64], s128[:, 32:64], s128[:, 64:96])
            ctx.inc(mi, "ve", 1)
            ctx.mark("ve:ln0var")
            ctx.wait(v, "se:ln0sqrt")
            v.reciprocal(s128[:, 32:64], s128[:, 32:64])
            v.tensor_copy(cols[:, 39:40], cols[:, 38:39])
            mi = v.tensor_mul(s128[:, 64:96], s128[:, 0:32], s128[:, 32:64])
            ctx.inc(mi, "ve", 1)
            ctx.mark("ve:ln0rs")
            ctx.wait(v, "lg:rs")
            for kt in range(KT):
                for cc in range(NT4):
                    mo = bk[3][:, :]
                    c0 = cc * NC
                    v.tensor_mul(mo, fsl(xbf, kt, c0), rt0[:, c0:c0 + NC])
                    v.tensor_sub(mo, mo, st0[:, c0:c0 + NC])
                    v.tensor_copy(xn_own[:, kt * TH + c0: kt * TH + c0 + NC], mo)
                    v.tensor_sub(fsl(xnlo, kt, c0, NC, TH), mo, xn_own[:, kt * TH + c0: kt * TH + c0 + NC])
                for cc in range(NT4):
                    c0 = TH + cc * NC
                    dst = xn_oth[:, kt * TH + cc * NC: kt * TH + (cc + 1) * NC]
                    v.tensor_mul(dst, fsl(xbf, kt, c0), rt0[:, c0:c0 + NC])
                    mi = v.tensor_sub(dst, dst, st0[:, c0:c0 + NC])
            ctx.inc(mi, "ve", 1)
            ctx.mark("ve:xn_done")
            v.memset(lgfm[:, :], 0.0)
            for kt in range(KT):
                ctx.wait(v, f"lg:wf{kt}")
                v.tensor_copy(rwhi[:, kt * 8:(kt + 1) * 8], wf32[:, 0:8])
                v.tensor_copy(a1hi[:, kt * 64:(kt + 1) * 64], wf32[:, 8:72])
                v.scalar_tensor_tensor(rwlo[:, kt * 8:(kt + 1) * 8], rwhi[:, kt * 8:(kt + 1) * 8],
                                       -1.0, wf32[:, 0:8], ALU.mult, ALU.add)
                mi = v.scalar_tensor_tensor(a1lo[:, kt * 64:(kt + 1) * 64], a1hi[:, kt * 64:(kt + 1) * 64],
                                            -1.0, wf32[:, 8:72], ALU.mult, ALU.add)
                ctx.inc(mi, "ve", 1)
                ctx.mark(f"ve:split{kt}")
            ctx.wait(v, "lg:a2")
            v.tensor_copy(a2hi[:, :], wf32[0:64, 0:8])
            v.tensor_copy(cols[:, 39:40], cols[:, 38:39])
            mi = v.scalar_tensor_tensor(a2lo[:, :], a2hi[:, :], -1.0, wf32[0:64, 0:8], ALU.mult, ALU.add)
            ctx.inc(mi, "ve", 1)
            ctx.mark("ve:a2split")
            for ntc in range(NT4):
                ctx.wait(v, f"mm:rl{ntc}")
                v.tensor_copy(lgfm[0:8, ntc * NC:(ntc + 1) * NC], bk[0][0:8, :])
                ctx.wait(v, f"mm:a1_{ntc}")
                mi = v.tensor_scalar(ah[:, ntc * NC:(ntc + 1) * NC], bk[1][0:64, :],
                                     cols[0:64, 32:33], None, ALU.add)
                ctx.inc(mi, "ve", 1)
                ctx.mark(f"ve:rlevac{ntc}")
            ctx.mark("ve:ah_raw")
            ctx.wait(v, "se:ah2")
            v.tensor_mul(ah2[:, :], ah2[:, :], ah[:, :])
            mi = v.scalar_tensor_tensor(ah2[:, :], ah2[:, :], 0.044715, ah[:, :], ALU.mult, ALU.add)
            ctx.inc(mi, "ve", 1)
            ctx.mark("ve:ah_u")
            ctx.wait(v, "se:ah_tanh")
            v.tensor_scalar(ah2[:, :], ah2[:, :], 1.0, None, ALU.add)
            v.tensor_mul(ah[:, :], ah[:, :], ah2[:, :])
            v.tensor_scalar(ah[:, :], ah[:, :], 0.5, None, ALU.mult)
            v.tensor_copy(ahhi[:, :], ah[:, :])
            v.tensor_copy(cols[:, 39:40], cols[:, 38:39])
            mi = v.scalar_tensor_tensor(ahlo[:, :], ahhi[:, :], -1.0, ah[:, :], ALU.mult, ALU.add)
            ctx.inc(mi, "ve", 1)
            ctx.mark("ve:ah_split")
            for ntc in range(NT4):
                ctx.wait(v, f"mm:a2_{ntc}")
                v.tensor_scalar(lgfm[32:40, ntc * NC:(ntc + 1) * NC], bk[0][0:8, :],
                                cols[0:8, 33:34], None, ALU.add)
                mi = v.tensor_scalar(lgfm[32:40, ntc * NC:(ntc + 1) * NC],
                                     lgfm[32:40, ntc * NC:(ntc + 1) * NC], 30.0, -30.0, ALU.min, ALU.max)
                ctx.inc(mi, "ve", 1)
                ctx.mark(f"ve:a2evac{ntc}")
            ctx.mark("ve:lgdone")
            v.tensor_copy(lg_hi[:, :], lgfm[:, :])
            mi = v.scalar_tensor_tensor(lg_lo[:, :], lg_hi[:, :], -1.0, lgfm[:, :], ALU.mult, ALU.add)
            ctx.inc(mi, "ve", 1)
            ctx.mark("ve:lg_split")
            for bp in range(8):
                B0, B1 = 2 * bp, 2 * bp + 1
                def _c(blk):
                    return dict(b16=blk * 16, b40=blk * 40, b8=blk * 8, q0=(blk % 2) * 80,
                                tx=(blk % 2) * 8)
                cA, cB = _c(B0), _c(B1)
                ctx.wait(v, f"mm:tr{B1}")
                v.tensor_copy(tm[:, cA["b40"]:cA["b40"] + 40], p_tr[:, cA["q0"]:cA["q0"] + 40])
                v.tensor_copy(tm[:, cB["b40"]:cB["b40"] + 40], p_tr[:, cB["q0"]:cB["q0"] + 40])
                mi = v.tensor_add(tm[:, cA["b40"]:cA["b40"] + 40], tm[:, cA["b40"]:cA["b40"] + 40],
                                  p_tr[:, cA["q0"] + 40:cA["q0"] + 80])
                ctx.inc(mi, "ve", 1)
                ctx.mark(f"ve:tr{B0}")
                mi = v.tensor_add(tm[:, cB["b40"]:cB["b40"] + 40], tm[:, cB["b40"]:cB["b40"] + 40],
                                  p_tr[:, cB["q0"] + 40:cB["q0"] + 80])
                ctx.inc(mi, "ve", 1)
                ctx.mark(f"ve:tr{B1}")
                for c in (cA, cB):
                    v.reduce_max(tmc[:, c["b8"]:c["b8"] + 1], tm[:, c["b40"]:c["b40"] + 8], axis=AX.X)
                for c, Bx in ((cA, B0), (cB, B1)):
                    mi = v.tensor_scalar(tmc[:, c["b8"]:c["b8"] + 1], tmc[:, c["b8"]:c["b8"] + 1], -1.0, None, ALU.mult)
                    ctx.inc(mi, "ve", 1)
                    ctx.mark(f"ve:tmmax{Bx}")
                ctx.wait(v, f"se:pexp{B0}")
                v.reciprocal(tmc[:, cA["b8"] + 2:cA["b8"] + 3], tmc[:, cA["b8"] + 1:cA["b8"] + 2])
                ctx.wait(v, f"se:pexp{B1}")
                v.reciprocal(tmc[:, cB["b8"] + 2:cB["b8"] + 3], tmc[:, cB["b8"] + 1:cB["b8"] + 2])
                for c in (cA, cB):
                    v.tensor_scalar(tmw[:, c["b16"]:c["b16"] + 8], tmw[:, c["b16"]:c["b16"] + 8],
                                    tmc[:, c["b8"] + 2:c["b8"] + 3], None, ALU.mult)
                for c in (cA, cB):
                    v.memset(tmw[:, c["b16"] + 8:c["b16"] + 16], 0.0)
                for ep in range(8):
                    for c in (cA, cB):
                        v.tensor_scalar(tmx[:, c["tx"]:c["tx"] + 8], tm[:, c["b40"]:c["b40"] + 8],
                                        tm[:, c["b40"] + ep:c["b40"] + ep + 1], None, ALU.is_lt)
                    for c in (cA, cB):
                        v.tensor_add(tmw[:, c["b16"] + 8:c["b16"] + 16], tmw[:, c["b16"] + 8:c["b16"] + 16],
                                     tmx[:, c["tx"]:c["tx"] + 8])
                for c in (cA, cB):
                    v.reduce_max(tmc[:, c["b8"] + 3:c["b8"] + 4], tm[:, c["b40"] + 32:c["b40"] + 40], axis=AX.X)
                for c in (cA, cB):
                    v.tensor_scalar(tmx[:, c["tx"]:c["tx"] + 8], tm[:, c["b40"] + 32:c["b40"] + 40],
                                    tmc[:, c["b8"] + 3:c["b8"] + 4], None, ALU.is_ge)
                for c in (cA, cB):
                    v.tensor_mul(tmx[:, c["tx"]:c["tx"] + 8], tmx[:, c["tx"]:c["tx"] + 8], argw[:, :])
                for c in (cA, cB):
                    v.reduce_max(tmc[:, c["b8"] + 4:c["b8"] + 5], tmx[:, c["tx"]:c["tx"] + 8], axis=AX.X)
                for c in (cA, cB):
                    v.tensor_scalar(tmc[:, c["b8"] + 5:c["b8"] + 6], tmc[:, c["b8"] + 4:c["b8"] + 5],
                                    -1.0, 7.0, ALU.mult, ALU.add)
                for c in (cA, cB):
                    v.tensor_scalar(tmx[:, c["tx"]:c["tx"] + 8], tmw[:, c["b16"] + 8:c["b16"] + 16],
                                    tmc[:, c["b8"] + 5:c["b8"] + 6], None, ALU.is_lt)
                for c in (cA, cB):
                    v.tensor_mul(tmw[:, c["b16"]:c["b16"] + 8], tmw[:, c["b16"]:c["b16"] + 8],
                                 tmx[:, c["tx"]:c["tx"] + 8])
                for c, Bx in ((cA, B0), (cB, B1)):
                    mi = v.tensor_copy(scb16[:, c["b8"]:c["b8"] + 8], tmw[:, c["b16"]:c["b16"] + 8])
                    ctx.inc(mi, "ve", 1)
                    ctx.mark(f"ve:scb{Bx}")
            for blk in range(16):
                ctx.wait(v, f"mm:sctr{blk}")
                mi = v.tensor_copy(lgfm[0:8, blk * 128:(blk + 1) * 128],
                                   p_tr[0:8, 160 + (blk % 2) * 128: 160 + (blk % 2) * 128 + 128])
                ctx.inc(mi, "ve", 1)
                ctx.mark(f"ve:sctr{blk}")
            ctx.mark("ve:scores")
            for nt_, tag, buf, Tlen, bcol in ((NT8, "k", kfm, S, 4), (NT4, "q", qfm, TH, 0)):
                for mt in range(4):
                    for ntc in range(nt_):
                        idx = mt * nt_ + ntc
                        ctx.wait(v, f"mm:{tag}{idx}")
                        if tag == "q" and idx == 0:
                            ctx.wait(v, "lh:scores")
                        mi = v.tensor_scalar(fsl(buf, mt, ntc * NC, NC, Tlen), bk[idx % 2][:, :],
                                             cols[:, bcol + mt:bcol + mt + 1], None, ALU.add)
                        ctx.inc(mi, "ve", 1)
                        ctx.mark(f"ve:{tag}ev{idx}")
            ctx.wait(v, "lg:wv")
            v3 = vtok[:].rearrange("p (j h s) -> p j h s", h=8, s=65) if real else _Mock()
            ctx.inc(v.memset(v3[:, :, :, 64:65] if real else v3, 1.0), "ve", 1)
            ctx.mark("ve:vones")
            for jt in range(32):
                ctx.wait(v, f"mm:v{jt}")
                mi = None
                for hh in range(H0):
                    mi = v.tensor_add(vtok[:, jt * 520 + hh * 65: jt * 520 + hh * 65 + 64],
                                      bk[2 + jt % 2][:, hh * 64:(hh + 1) * 64], bvt[:, hh * 64:(hh + 1) * 64])
                ctx.inc(mi, "ve", 1)
                ctx.mark(f"ve:vev{jt}")
            for nt_, tag, base in ((NT8, "k", 12), (NT4, "q", 24)):
                for mt in range(4):
                    for ch in range(nt_):
                        ctx.wait(v, f"mm:{tag}sq{mt}_{ch}")
                        if ch == 0:
                            mi = v.reduce_max(rowstage[0:1, base + mt:base + mt + 1], p_row[0:1, :], axis=AX.X)
                        else:
                            v.reduce_max(rowstage[0:1, 20:21], p_row[0:1, :], axis=AX.X)
                            v.tensor_copy(cols[:, 39:40], cols[:, 38:39])
                            mi = v.tensor_max(rowstage[0:1, base + mt:base + mt + 1],
                                              rowstage[0:1, base + mt:base + mt + 1], rowstage[0:1, 20:21])
                        ctx.inc(mi, "ve", 1)
                        ctx.mark(f"ve:{tag}sqm{mt}_{ch}")
            mi = v.tensor_mul(rowstage[0:1, 28:32], rowstage[0:1, 12:16], rowstage[0:1, 24:28])
            ctx.inc(mi, "ve", 1)
            ctx.mark("ve:boundprod")
            ctx.wait(v, "se:bsqrt")
            mi = v.tensor_scalar(rowstage[0:1, 8:12], rowstage[0:1, 28:32], -0.125, None, ALU.mult)
            ctx.inc(mi, "ve", 1)
            ctx.mark("ve:bound")
            for hh in range(H0):
                for qc in range(NT4):
                    ctx.wait(v, f"mm:av{hh}_{qc}_31")
                    if qc >= 2:
                        ctx.wait(v, f"lh:den{hh}_{qc - 2}")
                    ctx.inc(v.tensor_copy(rowstage[0:1, (qc % 2) * NC:(qc % 2 + 1) * NC], bk[2][64:65, :]), "ve", 1)
                    ctx.mark(f"ve:den{hh}_{qc}")
                    mi = v.tensor_copy(ofm[hb(hh):hb(hh) + 64, (hh // 2) * TH + qc * NC:(hh // 2) * TH + (qc + 1) * NC],
                                       bk[2][0:64, :])
                    ctx.inc(mi, "ve", 1)
                    ctx.mark(f"ve:oevac{hh}_{qc}")
                ctx.wait(v, f"lh:dload{hh}")
                mi = v.reciprocal(s128[:, 48:64], s128[:, 48:64])
                ctx.inc(mi, "ve", 1)
                ctx.mark(f"ve:recip{hh}")
                ctx.wait(v, f"lg:nrm{hh}")
                mi = v.tensor_mul(ofm[hb(hh):hb(hh) + 64, (hh // 2) * TH: (hh // 2) * TH + TH],
                                  ofm[hb(hh):hb(hh) + 64, (hh // 2) * TH: (hh // 2) * TH + TH],
                                  nrm[hb(hh):hb(hh) + 64, :])
                ctx.inc(mi, "ve", 1)
                ctx.mark(f"ve:onorm{hh}")
            for ntc in range(NT4):
                for mt in range(KT):
                    ctx.wait(v, f"mm:y1_{ntc}_{mt}")
                    mi = v.tensor_scalar(ychunk[:, mt * NC:(mt + 1) * NC], bk[mt % 2][:, :],
                                         cols[:, 8 + mt:9 + mt], None, ALU.add)
                    ctx.inc(mi, "ve", 1)
                    ctx.mark(f"ve:yev1_{ntc}_{mt}")
                ctx.wait(v, f"mm:ysta{ntc}")
                if ntc >= 1:
                    ctx.wait(v, f"lh:ysta{ntc - 1}")
                mi = v.tensor_copy(rowstage[0:1, 0:NC], p_row[0:1, :])
                ctx.inc(mi, "ve", 1)
                ctx.mark(f"ve:ysta{ntc}")
                ctx.wait(v, f"mm:yst{ntc}")
                if ntc >= 1:
                    ctx.wait(v, f"lh:yst{ntc - 1}")
                mi = v.tensor_copy(rowstage[0:1, NC:2 * NC], p_row[0:1, :])
                ctx.inc(mi, "ve", 1)
                ctx.mark(f"ve:ystevac{ntc}")
            ctx.wait(v, "lh:ln1cols")
            v.tensor_scalar(s128[:, 0:16], s128[:, 0:16], 1.0 / D, None, ALU.mult)
            v.tensor_scalar(s128[:, 16:32], s128[:, 16:32], 1.0 / D, None, ALU.mult)
            v.tensor_mul(s128[:, 32:48], s128[:, 0:16], s128[:, 0:16])
            v.tensor_copy(cols[:, 39:40], cols[:, 38:39])
            mi = v.tensor_sub(s128[:, 16:32], s128[:, 16:32], s128[:, 32:48])
            ctx.inc(mi, "ve", 1)
            ctx.mark("ve:ln1var")
            ctx.wait(v, "se:ln1sqrt")
            v.reciprocal(s128[:, 16:32], s128[:, 16:32])
            v.tensor_copy(cols[:, 39:40], cols[:, 38:39])
            mi = v.tensor_mul(s128[:, 32:48], s128[:, 0:16], s128[:, 16:32])
            ctx.inc(mi, "ve", 1)
            ctx.mark("ve:ln1rs")
            v.tensor_scalar(cols[:, 40:48], cols[:, 16:24], RSQRT2, None, ALU.mult)
            v.tensor_scalar(cols[:, 48:56], cols[:, 24:32], RSQRT2, None, ALU.mult)
            for ntc in range(NT4):
                for mt in range(KT):
                    ctx.wait(v, f"mm:y2_{ntc}_{mt}")
                    if ntc >= 1:
                        ctx.wait(v, f"lh:s{ntc - 1}_{mt}")
                    ysl = ychunk[:, mt * NC:(mt + 1) * NC]
                    v.tensor_scalar(scs[:, :], bk[mt % 2][:, :], cols[:, 8 + mt:9 + mt], None, ALU.add)
                    v.tensor_mul(scs[:, :], scs[:, :], rt2[:, ntc * NC:(ntc + 1) * NC])
                    v.tensor_sub(scs[:, :], scs[:, :], st2[:, ntc * NC:(ntc + 1) * NC])
                    v.tensor_scalar(scs[:, :], scs[:, :], cols[:, 40 + mt:41 + mt], cols[:, 48 + mt:49 + mt],
                                    ALU.mult, ALU.add)
                    mi = v.scalar_tensor_tensor(ysl, xnsl(mt, ntc * NC), RSQRT2, scs[:, :], ALU.mult, ALU.add)
                    ctx.inc(mi, "ve", 1)
                    ctx.mark(f"ve:s{ntc}_{mt}")
            ctx.mark("ve:sdone")
            ctx.wait(v, "lh:sspill")
            n_gu, n_vo = 0, 0
            for gi in range(TOTAL_P):
                e, p = divmod(gi, PASSES)
                if p == 0:
                    ctx.wait_cnt(v, "lsc", 16 * (e + 1))
                for it in range(PI_MT):
                    for ntc in range(NT4):
                        par = (it * NT4 + ntc) % 2
                        ctx.wait_cnt(v, "silu", n_gu + 1)
                        sl = slice(it * TH + ntc * NC, it * TH + (ntc + 1) * NC)
                        ctx.inc(v.tensor_mul(inter[:, sl], inter[:, sl], bk[2 + par][:, :]), "vi", 1)
                        n_gu += 1
                for mt in range(KT):
                    for ntc in range(NT4):
                        par = (mt * NT4 + ntc) % 2
                        ctx.wait_cnt(v, "mmy", n_vo + 1)
                        osl = slice(mt * TH + ntc * NC, mt * TH + (ntc + 1) * NC)
                        scrow = sc_b[:, ntc * NC:(ntc + 1) * NC]
                        if gi == 0:
                            ctx.inc(v.tensor_mul(moe_out[:, osl], bk[4 + par][:, :], scrow), "vo", 1)
                        else:
                            v.tensor_mul(bk[4 + par][:, :], bk[4 + par][:, :], scrow)
                            ctx.inc(v.tensor_add(moe_out[:, osl], moe_out[:, osl], bk[4 + par][:, :]), "vo", 1)
                        n_vo += 1
            for kt in range(KT):
                ctx.wait(v, f"lh:ep{kt}")
                ctx.wait(v, f"se:g{kt}")
                mo = fsl(moe_out, kt, 0, TH, TH)
                ctx.inc(v.tensor_mul(mo, sbb[:, :], gtmp[:, :]), "ve", 1)
                ctx.mark(f"ve:gused{kt}")
                mi = v.tensor_add(mo, mo, xf32[:, :])
                ctx.inc(mi, "ve", 1)
                ctx.mark(f"ve:x1_{kt}")

        return ({"gpsimd": gps, "sync": syn, "scalar": sca, "tensor": ten,
                 "vector": vec},
                {"x1t": x1t})


def build_launch_a(debug=()):
    ctx = Ctx(A_SEMS)

    def emit(real, nc=None):
        es = ExitStack()
        ctx.start(real, nc, es)
        fns, _ = _phase_a(ctx, real, nc, es, debug)
        block = es.enter_context(nc.Block()) if real else _Mock()
        for name in ("gpsimd", "sync", "scalar", "tensor", "vector"):
            if real:
                getattr(block, name)(fns[name])
            else:
                fns[name](_Mock())
        es.close()

    emit(False)
    emit(False)
    nc = bass.Bass("TRN2", target_bir_lowering=False, debug=False)
    emit(True, nc)
    return nc


def _np(x):
    return np.ascontiguousarray(x, dtype=np.float32)


def host_inputs_a(inp, core):
    b, h = core // 2, core % 2
    x = np.asarray(inp["L0_attn_x"])[b]
    if h == 1:
        x = np.concatenate([x[TH:], x[:TH]], axis=0)
    W_qk = np.asarray(inp["L0_attn_W_qk"])
    b_qk = np.asarray(inp["L0_attn_b_qk"])
    d = {
        "xt": _np(x.T),
        "wq": _np(W_qk[:, 0::2]),
        "wk": _np(W_qk[:, 1::2]),
        "wv": _np(inp["L0_attn_W_v"]),
        "wout": _np(inp["L0_attn_W_out"]),
        "bq_c": _np(b_qk[0::2].reshape(4, NP).T),
        "bk_c": _np(b_qk[1::2].reshape(4, NP).T),
        "bv_row": _np(np.asarray(inp["L0_attn_b_v"]).reshape(1, 512)),
        "bout_c": _np(np.asarray(inp["L0_attn_b_out"]).reshape(8, NP).T),
        "gam_c": _np(np.asarray(inp["L0_attn_ln_gamma"]).reshape(8, NP).T),
        "bet_c": _np(np.asarray(inp["L0_attn_ln_beta"]).reshape(8, NP).T),
        "rw_t": _np(np.asarray(inp["L0_ffn_router_w"]).T),
        "a1_t": _np(np.pad(np.asarray(inp["L0_ffn_actor_w1"]), ((0, 14), (0, 0))).T),
        "a2_t": _np(np.pad(np.asarray(inp["L0_ffn_actor_w2"]), ((0, 2), (0, 14))).T),
        "b1_c": _np(np.pad(np.asarray(inp["L0_ffn_actor_b1"]), (0, 14)).reshape(64, 1)),
        "b2_c": _np(np.concatenate([np.asarray(inp["L0_ffn_actor_b2"]),
                                    np.full(2, -1e30, np.float32)]).reshape(8, 1)),
        "gw": _np(np.asarray(inp["L0_ffn_gate_w"]).transpose(0, 2, 1)),
        "uw": _np(np.asarray(inp["L0_ffn_up_w"]).transpose(0, 2, 1)),
        "dw": _np(np.asarray(inp["L0_ffn_down_w"]).transpose(0, 2, 1)),
        "onec_in": np.ones((NP, 1), ml_dtypes.bfloat16),
        "argw_in": _np(np.array([[6, 5, 4, 3, 2, 1, 0, 0]], np.float32)),
        "ident_in": np.eye(NP, dtype=ml_dtypes.bfloat16),
    }
    return d


CH = 64
CH_N = S // CH


B_SEMS = ["lg", "lh", "mm", "ve", "se", "st", "so"]


def _phase_b(ctx, real, nc, es, debug=(), ext=None, sfx=""):
    if True:
        if ext is None:
            ext = {}
        if real:
            def di(name, shape, dt=F32):
                if name in ext:
                    return ext[name]
                return nc.dram_tensor(name + sfx, shape, dt, kind="ExternalInput").ap()
            sbuf = lambda name, shape, dt: es.enter_context(nc.sbuf_tensor(name + sfx, shape, dt))
            psum = lambda name, shape, dt=F32: es.enter_context(nc.psum_tensor(name + sfx, shape, dt))
        else:
            di = lambda *a, **k: _Mock()
            sbuf = lambda *a, **k: _Mock()
            psum = lambda *a, **k: _Mock()

        x1t = di("x1t", [D, S])
        wqp = di("wqp", [D, 512]); wkp = di("wkp", [D, 512]); wvp = di("wvp", [D, 512])
        wbp = di("wbp", [D, 2])
        cq_c = di("cq_c", [NP, 16]); ck_c = di("ck_c", [NP, 16]); cv_c = di("cv_c", [NP, 16])
        eyeb_in = di("eyeb_in", [NP, 64], BF16)
        masks_in = di("masks_in", [NP, 128], BF16)
        if real:
            dseqo = [nc.dram_tensor(nm + sfx, [S, 256], BF16, kind="ExternalOutput").ap()
                     for nm in ("dseqA", "dseqB")]
            vfm_out = nc.dram_tensor("vfm_out" + sfx, [512, S], BF16, kind="ExternalOutput").ap()
            scr = nc.dram_tensor("scr_b" + sfx, [16, S], F32, kind="Internal").ap()
            dbg = {}
            for nm in debug:
                shape = {"d_xn1": [D, S], "d_q": [512, S], "d_k": [512, S],
                         "d_beta": [NP, 64], "d_s0": [256, 256], "d_scr": [16, S]}[nm]
                dt_ = BF16 if nm in ("d_xn1", "d_q", "d_k", "d_s0") else F32
                dbg[nm] = nc.dram_tensor(nm, shape, dt_, kind="ExternalOutput").ap()
        else:
            dseqo = [_Mock(), _Mock()]
            vfm_out = scr = _Mock()
            dbg = {nm: _Mock() for nm in debug}

        cols = sbuf("cols", [NP, 64], F32)
        onec = sbuf("onec", [NP, 1], BF16)
        eyeb = sbuf("eyeb", [NP, 64], BF16)
        masks = sbuf("masks", [NP, 128], BF16)
        s128 = sbuf("s128", [NP, 96], F32)
        rowstage = sbuf("rowstage", [1, 2 * NC], F32)
        bcol = sbuf("bcol", [NP, 64], BF16)
        bcolf = sbuf("bcolf", [NP, 64], F32)
        bstg = sbuf("bstg", [32, NP], BF16)
        def scoped(es_, side="left"):
            return (lambda name, shape, dt: es_.enter_context(nc.sbuf_tensor(name + sfx, shape, dt, side=side))) if real else (lambda *a: _Mock())
        es_pw = ExitStack(); sb_pw = scoped(es_pw)
        wq_sb = sb_pw("wq_sb", [NP, KT * 512], BF16)
        wk_sb = sb_pw("wk_sb", [NP, KT * 512], BF16)
        wv_sb = sb_pw("wv_sb", [NP, KT * 512], BF16)
        wb_sb = sb_pw("wb_sb", [NP, KT * 2], BF16)
        praw = sb_pw("praw", [NP, S + 8], BF16)
        es_xn = ExitStack(); sb_xn = scoped(es_xn)
        xn1 = sb_xn("xn1", [NP, KT * S], BF16)
        es_x1 = ExitStack(); sb_x1 = scoped(es_x1, "right")
        x1bf = sb_x1("x1bf", [NP, KT * S], BF16)
        rt0 = sb_x1("rt0", [NP, S], BF16)
        st0 = sb_x1("st0", [NP, S], BF16)
        sq0 = sb_x1("sq0", [NP, 8 * NC], BF16)
        if real:
            es_x1.close()
        es_qk = ExitStack(); sb_qk = scoped(es_qk, "right")
        qfm = sb_qk("qfm", [NP, 4 * S], BF16)
        kfm = sb_qk("kfm", [NP, 4 * S], BF16)
        vfm = sb_qk("vfm", [NP, 4 * S], BF16)
        if real:
            es_xn.close()
            es_pw.close()
        nrmt = sbuf("nrmt", [NP, S], BF16)
        sq1 = sbuf("sq1", [NP, 2 * NC], BF16)
        kseq = [sbuf(f"kseq{h}", [NP, 32 * 256], BF16) for h in range(2)]
        vseq = [sbuf(f"vseq{h}", [NP, 32 * 256], BF16) for h in range(2)]
        dstg = [sbuf(f"dstg{h}", [NP, 256], BF16) for h in range(2)]
        S_sb = [sbuf(f"S_sb{h}", [NP, 512], BF16) for h in range(2)]
        kbc = [sbuf(f"kbc{h}", [NP, 256], BF16) for h in range(2)]
        ui_sb = [sbuf(f"ui{h}", [NP, 256], BF16) for h in range(2)]
        wtn = [sbuf(f"wtn{h}", [NP, 128], BF16) for h in range(2)]
        L_sb = [sbuf(f"L_sb{h}", [NP, 64], BF16) for h in range(2)]
        LT_sb = [sbuf(f"LT_sb{h}", [NP, 64], BF16) for h in range(2)]
        P_sb = [[sbuf(f"P{h}_{i}", [NP, 64], BF16) for i in range(2)] for h in range(2)]
        Q_sb = [[sbuf(f"Q{h}_{i}", [NP, 64], BF16) for i in range(2)] for h in range(2)]
        MT_sb = [sbuf(f"MT{h}", [NP, 64], BF16) for h in range(2)]
        atn = [sbuf(f"atn{h}", [NP, 64], BF16) for h in range(2)]

        bk = [psum(f"bk{i}", [NP, NC]) for i in range(4)]
        S_ps = [psum(f"S_ps{i}", [NP, 512]) for i in range(2)]
        p_row = psum("p_row", [1, NC])
        p_trB = psum("p_trB", [NP, 64], BF16)

        fsl = lambda buf, kt, c0, w=NC, Tlen=S: buf[:, kt * Tlen + c0: kt * Tlen + c0 + w]
        def cseq(buf, c, w=256):
            return buf[(c % 2) * 64:(c % 2) * 64 + 64, (c // 2) * w:(c // 2 + 1) * w]
        def bslice(h, c):
            return bcolf[(c % 2) * 64:(c % 2) * 64 + 64, h * 32 + c // 2: h * 32 + c // 2 + 1]
        LQ = 3 * 4 * NT8 - 1

        def gps(g):
            for kt in range(KT):
                ctx.inc(g.dma_start(out=fsl(x1bf, kt, 0, S), in_=x1t[kt * NP:(kt + 1) * NP, :]), "lg", 16)
            ctx.inc(g.dma_start(out=eyeb[:], in_=eyeb_in), "lg", 16)
            ctx.inc(g.dma_start(out=masks[:], in_=masks_in), "lg", 16)
            ctx.inc(g.dma_start(out=cols[:, 0:16], in_=cq_c), "lg", 16)
            ctx.inc(g.dma_start(out=cols[:, 16:32], in_=ck_c), "lg", 16)
            ctx.inc(g.dma_start(out=cols[:, 32:48], in_=cv_c), "lg", 16)
            for w_d, w_s, wd in ((wqp, wq_sb, 512), (wkp, wk_sb, 512), (wvp, wv_sb, 512), (wbp, wb_sb, 2)):
                for kt in range(KT):
                    ctx.inc(g.dma_start(out=w_s[:, kt * wd:(kt + 1) * wd],
                                        in_=w_d[kt * NP:(kt + 1) * NP, :]), "lg", 16)
            ctx.mark("lg:init")
            ctx.wait(g, "lh:ln0rows")
            ctx.inc(g.dma_start(out=rt0[:], in_=scr[2:3, :].partition_broadcast(NP)), "lg", 16)
            ctx.inc(g.dma_start(out=st0[:], in_=scr[3:4, :].partition_broadcast(NP)), "lg", 16)
            ctx.mark("lg:rs")
            for h in range(2):
                ctx.wait(g, f"lh:bev{h}_{NT8 - 1}")
                if h == 1:
                    ctx.wait(g, "lh:btr0")
                ctx.inc(g.dma_start(out=bstg[:, :],
                                    in_=scr[4 + h:5 + h, :].rearrange("o (c p) -> (o c) p", p=NP)), "lg", 16)
                ctx.mark(f"lg:bst{h}")
            for i in range(4):
                ctx.wait(g, f"lh:l2r{i}")
                if i >= 1:
                    ctx.wait(g, f"ve:l2s{i - 1}")
                ctx.inc(g.dma_start(out=nrmt[:, :], in_=scr[10 + i:11 + i, :].partition_broadcast(NP)), "lg", 16)
                ctx.mark(f"lg:nrm{i}")

        def syn(sy):
            for ch in range(NT8):
                ctx.wait(sy, f"ve:sta{ch}")
                ctx.inc(sy.dma_start(out=scr[0:1, ch * NC:(ch + 1) * NC], in_=rowstage[0:1, 0:NC]), "lh", 16)
                ctx.mark(f"lh:sta{ch}")
                ctx.wait(sy, f"ve:stevac{ch}")
                ctx.inc(sy.dma_start(out=scr[1:2, ch * NC:(ch + 1) * NC], in_=rowstage[0:1, NC:2 * NC]), "lh", 16)
                ctx.mark(f"lh:st{ch}")
            ctx.wait(sy, "lh:st7")
            ctx.inc(sy.dma_start(out=s128[:, 0:32], in_=scr[0:1, :].rearrange("o (p c) -> (o p) c", p=NP)), "lh", 16)
            ctx.inc(sy.dma_start(out=s128[:, 32:64], in_=scr[1:2, :].rearrange("o (p c) -> (o p) c", p=NP)), "lh", 16)
            ctx.mark("lh:ln0cols")
            ctx.wait(sy, "ve:ln0rs")
            ctx.inc(sy.dma_start(out=scr[2:3, :].rearrange("o (p c) -> (o p) c", p=NP), in_=s128[:, 32:64]), "lh", 16)
            ctx.inc(sy.dma_start(out=scr[3:4, :].rearrange("o (p c) -> (o p) c", p=NP), in_=s128[:, 64:96]), "lh", 16)
            ctx.mark("lh:ln0rows")
            if "d_xn1" in dbg:
                ctx.wait(sy, "ve:xn_done")
                for kt in range(KT):
                    ctx.inc(sy.dma_start(out=dbg["d_xn1"][kt * NP:(kt + 1) * NP, :], in_=fsl(xn1, kt, 0, S)), "st", 16)
            for h in range(2):
                for ch in range(NT8):
                    ctx.wait(sy, f"ve:bev{h}_{ch}")
                    ctx.inc(sy.dma_start(out=scr[4 + h:5 + h, ch * NC:(ch + 1) * NC],
                                         in_=rowstage[0:1, (ch % 2) * NC:(ch % 2 + 1) * NC]), "lh", 16)
                    ctx.mark(f"lh:bev{h}_{ch}")
            for h in range(2):
                ctx.wait(sy, f"lg:bst{h}")
                ctx.inc(sy.dma_start_transpose(bcol[:, h * 32:(h + 1) * 32], bstg[:, :]), "lh", 16)
                ctx.mark(f"lh:btr{h}")
            ctx.mark("lh:bcol")
            if "d_beta" in dbg:
                ctx.wait(sy, "se:bsig")
                ctx.inc(sy.dma_start(out=dbg["d_beta"], in_=bcolf[:, 0:64]), "st", 16)
            for i in range(4):
                for ch in range(NT8):
                    ctx.wait(sy, f"ve:l2e{i}_{ch}")
                    ctx.inc(sy.dma_start(out=scr[6 + i:7 + i, ch * NC:(ch + 1) * NC],
                                         in_=rowstage[0:1, (ch % 2) * NC:(ch % 2 + 1) * NC]), "lh", 16)
                    ctx.mark(f"lh:l2e{i}_{ch}")
                if i >= 1:
                    ctx.wait(sy, f"se:l2sq{i - 1}")
                ctx.wait(sy, f"lh:l2e{i}_{NT8 - 1}")
                ctx.inc(sy.dma_start(out=s128[:, 0:32],
                                     in_=scr[6 + i:7 + i, :].rearrange("o (p c) -> (o p) c", p=NP)), "lh", 16)
                ctx.mark(f"lh:l2c{i}")
                ctx.wait(sy, f"ve:l2r{i}")
                ctx.inc(sy.dma_start(out=scr[10 + i:11 + i, :].rearrange("o (p c) -> (o p) c", p=NP),
                                     in_=s128[:, 64:96]), "lh", 16)
                ctx.mark(f"lh:l2r{i}")
            ctx.wait(sy, f"mm:b1_{NT8 - 1}")
            ctx.wait(sy, "se:vsilu")
            for h in range(2):
                for tj in range(32):
                    for dt2 in range(2):
                        ctx.inc(sy.dma_start_transpose(
                            vseq[h][:, tj * 256 + dt2 * 128: tj * 256 + (dt2 + 1) * 128],
                            vfm[:, (2 * h + dt2) * S + tj * NP: (2 * h + dt2) * S + (tj + 1) * NP]), "lh", 16)
                ctx.mark(f"lh:vseq{h}")
            for h in range(2):
                ctx.wait(sy, f"ve:l2s{1 + 2 * h}")
                for tj in range(32):
                    for dt2 in range(2):
                        ctx.inc(sy.dma_start_transpose(
                            kseq[h][:, tj * 256 + dt2 * 128: tj * 256 + (dt2 + 1) * 128],
                            kfm[:, (2 * h + dt2) * S + tj * NP: (2 * h + dt2) * S + (tj + 1) * NP]), "lh", 16)
                ctx.mark(f"lh:kseq{h}")
            for mt in range(4):
                ctx.inc(sy.dma_start(out=vfm_out[mt * NP:(mt + 1) * NP, :], in_=fsl(vfm, mt, 0, S)), "st", 16)
            if "d_q" in dbg:
                ctx.wait(sy, "ve:l2s3")
                for mt in range(4):
                    ctx.inc(sy.dma_start(out=dbg["d_q"][mt * NP:(mt + 1) * NP, :], in_=fsl(qfm, mt, 0, S)), "st", 16)
            if "d_k" in dbg:
                ctx.wait(sy, "ve:l2s3")
                for mt in range(4):
                    ctx.inc(sy.dma_start(out=dbg["d_k"][mt * NP:(mt + 1) * NP, :], in_=fsl(kfm, mt, 0, S)), "st", 16)
            for c in range(CH_N):
                b0 = (c % 2) * 64
                for h in range(2):
                    ctx.wait(sy, f"ve:oev{h}_{c}")
                    ctx.inc(sy.dma_start(out=dseqo[h][c * CH:(c + 1) * CH, :],
                                         in_=dstg[h][b0:b0 + 64, :]), "so", 16)
            if "d_scr" in dbg:
                ctx.wait(sy, "ve:l2s3")
                ctx.inc(sy.dma_start(out=dbg["d_scr"], in_=scr[:, :]), "st", 16)
            if "d_s0" in dbg:
                ctx.wait(sy, f"ve:sref0_{CH_N - 1}")
                out_ap = dbg["d_s0"].rearrange("(a p) d -> p (a d)", p=NP) if real else dbg["d_s0"]
                ctx.inc(sy.dma_start(out=out_ap, in_=S_sb[0][:, :]), "st", 16)
            if real:
                sy.wait_ge(ctx.sem["st"], ctx.cnt["st"])
                sy.wait_ge(ctx.sem["so"], ctx.cnt["so"])

        def ten(t):
            ctx.wait(t, "lg:init")
            ctx.wait(t, "ve:ones")
            for ch in range(NT8):
                mi = accum_mm(t, p_row[0:1, :], [(onec[:, :], fsl(x1bf, kt, ch * NC)) for kt in range(KT)])
                ctx.inc(mi, "mm", 1)
                ctx.mark(f"mm:sta{ch}")
                ctx.wait(t, f"ve:sta{ch}")
                ctx.wait(t, f"se:xsq{ch}")
                mi = accum_mm(t, p_row[0:1, :], [(onec[:, :], sq0[:, kt * NC:(kt + 1) * NC]) for kt in range(KT)])
                ctx.inc(mi, "mm", 1)
                ctx.mark(f"mm:st{ch}")
                ctx.wait(t, f"ve:stevac{ch}")
            ctx.wait(t, "ve:xn_done")
            for wi, w_s in enumerate((wq_sb, wk_sb, wv_sb)):
                for mt in range(4):
                    for ntc in range(NT8):
                        idx = wi * 4 * NT8 + mt * NT8 + ntc
                        pairs = [(w_s[:, kt * 512 + mt * NP: kt * 512 + (mt + 1) * NP], fsl(xn1, kt, ntc * NC))
                                 for kt in range(KT)]
                        ctx.inc(accum_mm(t, bk[idx % 2][:, :], pairs), "mm", 1)
                        ctx.mark(f"mm:p{idx}")
                        if idx >= 1:
                            ctx.wait(t, f"ve:pev{idx - 1}")
            ctx.wait(t, f"ve:pev{LQ}")
            for h in range(2):
                for ch in range(NT8):
                    pairs = [(wb_sb[:, kt * 2 + h: kt * 2 + h + 1], fsl(xn1, kt, ch * NC)) for kt in range(KT)]
                    ctx.inc(accum_mm(t, p_row[0:1, :], pairs), "mm", 1)
                    ctx.mark(f"mm:b{h}_{ch}")
                    ctx.wait(t, f"ve:bev{h}_{ch}")
            for i in range(4):
                for ch in range(NT8):
                    ctx.wait(t, f"se:l2q{i}_{ch}")
                    mi = accum_mm(t, p_row[0:1, :], [(onec[:, :], sq1[:, j * NC:(j + 1) * NC]) for j in range(2)])
                    ctx.inc(mi, "mm", 1)
                    ctx.mark(f"mm:l2{i}_{ch}")
                    ctx.wait(t, f"ve:l2e{i}_{ch}")
            for w8 in ("lh:bcol", "se:bsig", "ve:l2s3", "lh:vseq1", "lh:kseq1",
                       "ve:vb0", "ve:vb1", "ve:s00", "ve:s01"):
                ctx.wait(t, w8)
            for c in range(CH_N):
                b0 = (c % 2) * 64
                hs = lambda buf: buf[b0:b0 + 64, :]
                for h in range(2):
                    w0, w1 = bk[2 * h], bk[2 * h + 1]
                    tA, tB = 2 * h, 2 * h + 1
                    ksl = lambda tt: kfm[:, tt * S + c * CH: tt * S + (c + 1) * CH]
                    qsl = lambda tt: qfm[:, tt * S + c * CH: tt * S + (c + 1) * CH]
                    mi = accum_mm(t, w0[b0:b0 + 64, 0:64], [(ksl(tA), ksl(tA)), (ksl(tB), ksl(tB))])
                    ctx.inc(mi, "mm", 1)
                    ctx.mark(f"mm:L{h}_{c}")
                    ctx.wait(t, f"ve:L{h}_{c}")
                    if not (c == 0 and h == 0):
                        ph, pc = (0, c) if h == 1 else (1, c - 1)
                        ctx.wait(t, f"ve:LT{ph}_{pc}")
                    mi = t.transpose(p_trB[b0:b0 + 64, :], hs(L_sb[h]), hs(eyeb))
                    ctx.inc(mi, "mm", 1)
                    ctx.mark(f"mm:LT{h}_{c}")
                    for j in range(1, 6):
                        pp = hs(P_sb[h][(j - 1) % 2]); qq = hs(Q_sb[h][(j - 1) % 2])
                        pn = hs(P_sb[h][j % 2])
                        if j == 1:
                            pp, qq = hs(L_sb[h]), hs(LT_sb[h])
                        ctx.wait(t, f"ve:MT{h}_{c}_{j - 1}" if j > 1 else f"ve:LT{h}_{c}")
                        mi = t.matmul(w0[b0:b0 + 64, 0:64], qq, pp, start=True, stop=True)
                        ctx.inc(mi, "mm", 1)
                        ctx.mark(f"mm:P{h}_{c}_{j}")
                        ctx.wait(t, f"ve:P{h}_{c}_{j}")
                        mi = t.matmul(w0[b0:b0 + 64, 0:64], pp, qq, start=True, stop=True)
                        ctx.inc(mi, "mm", 1)
                        ctx.mark(f"mm:Q{h}_{c}_{j}")
                        ctx.wait(t, f"ve:Q{h}_{c}_{j}")
                        mi = t.matmul(w0[b0:b0 + 64, 0:64], pn, hs(MT_sb[h]), start=True, stop=True)
                        ctx.inc(mi, "mm", 1)
                        ctx.mark(f"mm:M{h}_{c}_{j}")
                        ctx.wait(t, f"ve:MT{h}_{c}_{j}")
                    mi = t.matmul(w0[b0:b0 + 64, 0:256], hs(MT_sb[h]), cseq(vseq[h], c), start=True, stop=False)
                    ctx.inc(mi, "mm", 1)
                    ctx.mark(f"mm:u{h}_{c}")
                    ctx.wait(t, f"ve:kbc{h}_{c}")
                    t.matmul(w1[0:128, 0:64], kbc[h][b0:b0 + 64, 0:128], hs(MT_sb[h]), start=True, stop=True)
                    mi = t.matmul(w1[0:128, 64:128], kbc[h][b0:b0 + 64, 128:256], hs(MT_sb[h]), start=True, stop=True)
                    ctx.inc(mi, "mm", 1)
                    ctx.mark(f"mm:wT{h}_{c}")
                    ctx.wait(t, f"se:wtn{h}_{c}")
                    t.matmul(w0[b0:b0 + 64, 0:256], wtn[h][:, 0:64], S_sb[h][:, 0:256], start=False, stop=False)
                    mi = t.matmul(w0[b0:b0 + 64, 0:256], wtn[h][:, 64:128], S_sb[h][:, 256:512], start=False, stop=True)
                    ctx.inc(mi, "mm", 1)
                    ctx.mark(f"mm:ui{h}_{c}")
                    ctx.wait(t, f"ve:ui{h}_{c}")
                    mi = accum_mm(t, w1[b0:b0 + 64, 0:64], [(ksl(tA), qsl(tA)), (ksl(tB), qsl(tB))])
                    ctx.inc(mi, "mm", 1)
                    ctx.mark(f"mm:at{h}_{c}")
                    ctx.wait(t, f"ve:at{h}_{c}")
                    t.matmul(w1[b0:b0 + 64, 0:256], qsl(tA), S_sb[h][:, 0:256], start=True, stop=False)
                    t.matmul(w1[b0:b0 + 64, 0:256], qsl(tB), S_sb[h][:, 256:512], start=False, stop=False)
                    mi = t.matmul(w1[b0:b0 + 64, 0:256], hs(atn[h]), ui_sb[h][b0:b0 + 64, :], start=False, stop=True)
                    ctx.inc(mi, "mm", 1)
                    ctx.mark(f"mm:oi{h}_{c}")
                    t.matmul(S_ps[h][:, 0:256], cseq(kseq[h], c)[:, 0:128], ui_sb[h][b0:b0 + 64, :],
                             start=(c == 0), stop=False, skip_group_check=True)
                    mi = t.matmul(S_ps[h][:, 256:512], cseq(kseq[h], c)[:, 128:256], ui_sb[h][b0:b0 + 64, :],
                                  start=False, stop=False, skip_group_check=True)
                    ctx.inc(mi, "mm", 1)
                    ctx.mark(f"mm:sup{h}_{c}")
                    ctx.wait(t, f"ve:sref{h}_{c}")

        def sca(s):
            ctx.wait(s, "lg:init")
            for ch in range(NT8):
                if ch >= 1:
                    ctx.wait(s, f"mm:st{ch - 1}")
                mi = None
                for kt in range(KT):
                    mi = s.activation(sq0[:, kt * NC:(kt + 1) * NC], fsl(x1bf, kt, ch * NC), AF.Square)
                ctx.inc(mi, "se", 1)
                ctx.mark(f"se:xsq{ch}")
            ctx.wait(s, "ve:ln0var")
            ctx.inc(s.activation(s128[:, 32:64], s128[:, 32:64], AF.Sqrt, bias=cols[:, 62:63]), "se", 1)
            ctx.mark("se:ln0sqrt")
            for nm, buf in (("q", qfm), ("k", kfm), ("v", vfm)):
                ctx.wait(s, f"ve:conv{nm}")
                mi = None
                for mt in range(4):
                    mi = s.activation(fsl(buf, mt, 0, S, S), fsl(buf, mt, 0, S, S), AF.Silu)
                ctx.inc(mi, "se", 1)
                ctx.mark(f"se:{nm}silu")
            ctx.wait(s, "lh:bcol")
            ctx.inc(s.activation(bcolf[:, 0:64], bcol[:, 0:64], AF.Sigmoid), "se", 1)
            ctx.mark("se:bsig")
            for i, (buf, t0) in enumerate(((qfm, 0), (kfm, 0), (qfm, 2), (kfm, 2))):
                for ch in range(NT8):
                    if i == 0 and ch == 0:
                        ctx.wait(s, f"mm:b1_{NT8 - 1}")
                    else:
                        pi, pch = (i, ch - 1) if ch >= 1 else (i - 1, NT8 - 1)
                        ctx.wait(s, f"mm:l2{pi}_{pch}")
                    s.activation(sq1[:, 0:NC], fsl(buf, t0, ch * NC), AF.Square)
                    s.activation(sq1[:, NC:2 * NC], fsl(buf, t0 + 1, ch * NC), AF.Square)
                    mi = s.activation(cols[:, 60:61], cols[:, 62:63], AF.Copy)
                    ctx.inc(mi, "se", 1)
                    ctx.mark(f"se:l2q{i}_{ch}")
                ctx.wait(s, f"lh:l2c{i}")
                if i >= 1:
                    ctx.wait(s, f"ve:l2r{i - 1}")
                ctx.inc(s.activation(s128[:, 32:64], s128[:, 0:32], AF.Sqrt, bias=cols[:, 63:64]), "se", 1)
                ctx.mark(f"se:l2sq{i}")
            for c in range(CH_N):
                for h in range(2):
                    ctx.wait(s, f"mm:wT{h}_{c}")
                    ctx.inc(s.activation(wtn[h][:, :], bk[2 * h + 1][:, 0:128], AF.Copy, scale=-1.0), "se", 1)
                    ctx.mark(f"se:wtn{h}_{c}")

        def vec(v):
            v.memset(cols[:, 62:63], LN_EPS)
            v.memset(cols[:, 63:64], 1e-12)
            v.memset(praw[:, :], 0.0)
            ctx.inc(v.memset(onec[:], 1.0), "ve", 1)
            ctx.mark("ve:ones")
            for ch in range(NT8):
                ctx.wait(v, f"mm:sta{ch}")
                if ch >= 1:
                    ctx.wait(v, f"lh:sta{ch - 1}")
                ctx.inc(v.tensor_copy(rowstage[0:1, 0:NC], p_row[0:1, :]), "ve", 1)
                ctx.mark(f"ve:sta{ch}")
                ctx.wait(v, f"mm:st{ch}")
                if ch >= 1:
                    ctx.wait(v, f"lh:st{ch - 1}")
                ctx.inc(v.tensor_copy(rowstage[0:1, NC:2 * NC], p_row[0:1, :]), "ve", 1)
                ctx.mark(f"ve:stevac{ch}")
            ctx.wait(v, "lh:ln0cols")
            v.tensor_scalar(s128[:, 0:32], s128[:, 0:32], 1.0 / D, None, ALU.mult)
            v.tensor_scalar(s128[:, 32:64], s128[:, 32:64], 1.0 / D, None, ALU.mult)
            v.tensor_mul(s128[:, 64:96], s128[:, 0:32], s128[:, 0:32])
            v.tensor_copy(cols[:, 61:62], cols[:, 62:63])
            ctx.inc(v.tensor_sub(s128[:, 32:64], s128[:, 32:64], s128[:, 64:96]), "ve", 1)
            ctx.mark("ve:ln0var")
            ctx.wait(v, "se:ln0sqrt")
            v.reciprocal(s128[:, 32:64], s128[:, 32:64])
            v.tensor_copy(cols[:, 61:62], cols[:, 62:63])
            ctx.inc(v.tensor_mul(s128[:, 64:96], s128[:, 0:32], s128[:, 32:64]), "ve", 1)
            ctx.mark("ve:ln0rs")
            ctx.wait(v, "lg:rs")
            mi = None
            for kt in range(KT):
                for cc in range(NT8):
                    mo = bk[3][:, :]
                    c0 = cc * NC
                    v.tensor_mul(mo, fsl(x1bf, kt, c0), rt0[:, c0:c0 + NC])
                    v.tensor_sub(mo, mo, st0[:, c0:c0 + NC])
                    mi = v.tensor_copy(fsl(xn1, kt, c0), mo)
            ctx.inc(mi, "ve", 1)
            ctx.mark("ve:xn_done")
            for wi, (nm, buf) in enumerate((("q", qfm), ("k", kfm), ("v", vfm))):
                ccol = [0, 16, 32][wi]
                for mt in range(4):
                    for ntc in range(NT8):
                        idx = wi * 4 * NT8 + mt * NT8 + ntc
                        ctx.wait(v, f"mm:p{idx}")
                        mi = v.tensor_copy(praw[:, 8 + ntc * NC: 8 + (ntc + 1) * NC], bk[idx % 2][:, :])
                        ctx.inc(mi, "ve", 1)
                        ctx.mark(f"ve:pev{idx}")
                    v.tensor_scalar(fsl(buf, mt, 0, S, S), praw[:, 5: 5 + S],
                                    cols[:, ccol + mt * 4: ccol + mt * 4 + 1], None, ALU.mult)
                    for tau in range(1, 4):
                        mi = v.scalar_tensor_tensor(fsl(buf, mt, 0, S, S), praw[:, 5 + tau: 5 + tau + S],
                                                    cols[:, ccol + mt * 4 + tau: ccol + mt * 4 + tau + 1],
                                                    fsl(buf, mt, 0, S, S), ALU.mult, ALU.add)
                ctx.inc(mi, "ve", 1)
                ctx.mark(f"ve:conv{nm}")
            for h in range(2):
                for ch in range(NT8):
                    ctx.wait(v, f"mm:b{h}_{ch}")
                    if h == 0 and ch <= 1:
                        ctx.wait(v, "lh:sta7" if ch == 0 else "lh:st7")
                    else:
                        ph, pch = (h, ch - 2) if ch >= 2 else (h - 1, NT8 - 2 + ch)
                        ctx.wait(v, f"lh:bev{ph}_{pch}")
                    mi = v.tensor_copy(rowstage[0:1, (ch % 2) * NC:(ch % 2 + 1) * NC], p_row[0:1, :])
                    ctx.inc(mi, "ve", 1)
                    ctx.mark(f"ve:bev{h}_{ch}")
            for i in range(4):
                for ch in range(NT8):
                    ctx.wait(v, f"mm:l2{i}_{ch}")
                    if i == 0 and ch <= 1:
                        ctx.wait(v, f"lh:bev1_{NT8 - 2 + ch}")
                    else:
                        pi, pch = (i, ch - 2) if ch >= 2 else (i - 1, NT8 - 2 + ch)
                        ctx.wait(v, f"lh:l2e{pi}_{pch}")
                    mi = v.tensor_copy(rowstage[0:1, (ch % 2) * NC:(ch % 2 + 1) * NC], p_row[0:1, :])
                    ctx.inc(mi, "ve", 1)
                    ctx.mark(f"ve:l2e{i}_{ch}")
                ctx.wait(v, f"se:l2sq{i}")
                mi = v.reciprocal(s128[:, 64:96], s128[:, 32:64])
                ctx.inc(mi, "ve", 1)
                ctx.mark(f"ve:l2r{i}")
                buf, t0 = ((qfm, 0), (kfm, 0), (qfm, 2), (kfm, 2))[i]
                ctx.wait(v, f"lg:nrm{i}")
                ctx.wait(v, "se:qsilu")
                ctx.wait(v, "se:ksilu")
                v.tensor_mul(fsl(buf, t0, 0, S, S), fsl(buf, t0, 0, S, S), nrmt[:, :])
                mi = v.tensor_mul(fsl(buf, t0 + 1, 0, S, S), fsl(buf, t0 + 1, 0, S, S), nrmt[:, :])
                ctx.inc(mi, "ve", 1)
                ctx.mark(f"ve:l2s{i}")
            for h in range(2):
                ctx.wait(v, f"lh:vseq{h}")
                ctx.wait(v, "se:bsig")
                mi = None
                for tj in range(32):
                    mi = v.tensor_scalar(vseq[h][:, tj * 256:(tj + 1) * 256], vseq[h][:, tj * 256:(tj + 1) * 256],
                                         bcolf[:, h * 32 + tj: h * 32 + tj + 1], None, ALU.mult)
                ctx.inc(mi, "ve", 1)
                ctx.mark(f"ve:vb{h}")
            for h in range(2):
                ctx.inc(v.memset(S_sb[h][:, :], 0.0), "ve", 1)
                ctx.mark(f"ve:s0{h}")
            for c in range(CH_N):
                b0 = (c % 2) * 64
                hs = lambda buf: buf[b0:b0 + 64, :]
                for h in range(2):
                    w0, w1 = bk[2 * h], bk[2 * h + 1]
                    ctx.wait(v, f"mm:L{h}_{c}")
                    mi = v.scalar_tensor_tensor(hs(L_sb[h]), w0[b0:b0 + 64, 0:64], bslice(h, c),
                                                masks[b0:b0 + 64, 0:64], ALU.mult, ALU.mult)
                    ctx.inc(mi, "ve", 1)
                    ctx.mark(f"ve:L{h}_{c}")
                    ctx.wait(v, f"mm:LT{h}_{c}")
                    v.tensor_copy(hs(LT_sb[h]), p_trB[b0:b0 + 64, :])
                    v.tensor_copy(cols[:, 61:62], cols[:, 62:63])
                    ctx.inc(v.tensor_add(hs(MT_sb[h]), hs(LT_sb[h]), hs(eyeb)), "ve", 1)
                    ctx.mark(f"ve:LT{h}_{c}")
                    ctx.inc(v.tensor_scalar(hs(kbc[h]), cseq(kseq[h], c), bslice(h, c), None, ALU.mult), "ve", 1)
                    ctx.mark(f"ve:kbc{h}_{c}")
                    for j in range(1, 6):
                        pn = hs(P_sb[h][j % 2]); qn = hs(Q_sb[h][j % 2])
                        ctx.wait(v, f"mm:P{h}_{c}_{j}")
                        ctx.inc(v.tensor_copy(pn, w0[b0:b0 + 64, 0:64]), "ve", 1)
                        ctx.mark(f"ve:P{h}_{c}_{j}")
                        ctx.wait(v, f"mm:Q{h}_{c}_{j}")
                        ctx.inc(v.tensor_copy(qn, w0[b0:b0 + 64, 0:64]), "ve", 1)
                        ctx.mark(f"ve:Q{h}_{c}_{j}")
                        ctx.wait(v, f"mm:M{h}_{c}_{j}")
                        ctx.inc(v.tensor_add(hs(MT_sb[h]), hs(MT_sb[h]), w0[b0:b0 + 64, 0:64]), "ve", 1)
                        ctx.mark(f"ve:MT{h}_{c}_{j}")
                    ctx.wait(v, f"mm:ui{h}_{c}")
                    ctx.inc(v.tensor_copy(ui_sb[h][b0:b0 + 64, :], w0[b0:b0 + 64, 0:256]), "ve", 1)
                    ctx.mark(f"ve:ui{h}_{c}")
                    ctx.wait(v, f"mm:at{h}_{c}")
                    ctx.inc(v.tensor_mul(hs(atn[h]), w1[b0:b0 + 64, 0:64], masks[b0:b0 + 64, 64:128]), "ve", 1)
                    ctx.mark(f"ve:at{h}_{c}")
                    ctx.wait(v, f"mm:oi{h}_{c}")
                    if c >= 1:
                        ctx.wait_cnt(v, "so", 16 * (2 * (c - 1) + h + 1))
                    ctx.inc(v.tensor_copy(dstg[h][b0:b0 + 64, :], w1[b0:b0 + 64, 0:256]), "ve", 1)
                    ctx.mark(f"ve:oev{h}_{c}")
                    ctx.wait(v, f"mm:sup{h}_{c}")
                    ctx.inc(v.tensor_copy(S_sb[h][:, :], S_ps[h][:, :]), "ve", 1)
                    ctx.mark(f"ve:sref{h}_{c}")

        return ({"gpsimd": gps, "sync": syn, "scalar": sca, "tensor": ten,
                 "vector": vec},
                {"dseqA": dseqo[0], "dseqB": dseqo[1], "vfm_out": vfm_out})


def build_launch_b(debug=()):
    ctx = Ctx(B_SEMS)

    def emit(real, nc=None):
        es = ExitStack()
        ctx.start(real, nc, es)
        fns, _ = _phase_b(ctx, real, nc, es, debug)
        block = es.enter_context(nc.Block()) if real else _Mock()
        for name in ("gpsimd", "sync", "scalar", "tensor", "vector"):
            if real:
                getattr(block, name)(fns[name])
            else:
                fns[name](_Mock())
        es.close()

    emit(False)
    emit(False)
    nc = bass.Bass("TRN2", target_bir_lowering=False, debug=False)
    emit(True, nc)
    return nc


PAIR_GROUPS = [[0, 1], [2, 3], [4, 5], [6, 7]]
PROD_A = {"gpsimd": ("lg", "lsc", "lw"), "sync": ("lh", "st"),
          "tensor": ("mm", "mmy"), "scalar": ("se", "silu"),
          "vector": ("ve", "vi", "vo")}


def build_launch_ab(debug=()):
    ALLS = sorted(set(A_SEMS + B_SEMS))
    ctx = Ctx(ALLS)

    def emit(real, nc=None):
        es = ExitStack()
        ctx.start(real, nc, es)
        esA = ExitStack()
        fnsA, handA = _phase_a(ctx, real, nc, esA, ())
        if real:
            esA.close()
            x1i = nc.dram_tensor("x1i", [D, TH], F32, kind="Internal").ap()
            g1 = nc.dram_tensor("g1", [2 * D, TH], F32, kind="Internal").ap()
            x1full = nc.dram_tensor("x1full", [D, S], F32, kind="Internal").ap()
        else:
            x1i = g1 = x1full = _Mock()
        ctxB = Ctx2(ctx, "b.")
        fnsB, handB = _phase_b(ctxB, real, nc, es, (), ext={"x1t": x1full}, sfx="_b")
        block = es.enter_context(nc.Block()) if real else _Mock()
        for name in ("gpsimd", "sync", "scalar", "tensor", "vector"):
            def combo(eng, name=name, f1=fnsA[name], f2=fnsB[name]):
                f1(eng)
                for s in PROD_A.get(name, ()):
                    ctx.mark(f"{s}:aend")
                for s in ALLS:
                    ctx.wait(eng, f"{s}:aend")
                if name == "gpsimd":
                    ctx.inc(eng.dma_start(out=x1i, in_=handA["x1t"]), "lg", 16)
                    ctx.mark("lg:abx1i")
                    ctx.wait(eng, "lg:abx1i")
                    ctx.inc(eng.collective_compute(
                        "AllGather", ALU.bypass, replica_groups=PAIR_GROUPS,
                        ins=[x1i[:, :]], outs=[g1[:, :]]), "lg", 16)
                    ctx.mark("lg:abag")
                    ctx.wait(eng, "lg:abag")
                    ctx.inc(eng.dma_start(out=x1full[:, 0:TH], in_=g1[0:D, :]), "lg", 16)
                    ctx.inc(eng.dma_start(out=x1full[:, TH:S], in_=g1[D:2 * D, :]), "lg", 16)
                    ctx.mark("lg:abglue")
                ctx.wait(eng, "lg:abglue")
                f2(eng)
            if real:
                getattr(block, name)(combo)
            else:
                combo(_Mock())
        es.close()

    emit(False)
    emit(False)
    nc = bass.Bass("TRN2", target_bir_lowering=False, debug=False)
    emit(True, nc)
    return nc


def host_inputs_ab(inp, core):
    m = host_inputs_a(inp, core)
    mb = host_inputs_b(inp, core, None)
    del mb["x1t"]
    for k, v in mb.items():
        m[k + "_b"] = v
    return m


def host_inputs_b(inp, core, x1_full):
    b, p = core // 2, core % 2
    qw = np.asarray(inp["L1_attn_q_proj_w"])[p * 512:(p + 1) * 512]
    kw = np.asarray(inp["L1_attn_k_proj_w"])[p * 512:(p + 1) * 512]
    vw = np.asarray(inp["L1_attn_v_proj_w"])[p * 512:(p + 1) * 512]
    bw = np.asarray(inp["L1_attn_b_proj_w"])[2 * p:2 * p + 2]
    def conv_cols(w):
        w = np.asarray(w)[p * 512:(p + 1) * 512]
        return _np(w.reshape(4, NP, 4).transpose(1, 0, 2).reshape(NP, 16))
    m1 = np.concatenate([
        -np.tril(np.ones((CH, CH), np.float32), -1),
        np.triu(np.ones((CH, CH), np.float32)),
    ], axis=1)
    masks = np.concatenate([m1, m1], axis=0).astype(ml_dtypes.bfloat16)
    eyeb = np.concatenate([np.eye(CH, dtype=np.float32)] * 2, axis=0).astype(ml_dtypes.bfloat16)
    return {
        "x1t": None if x1_full is None else _np(x1_full[b].T),
        "wqp": _np(qw.T), "wkp": _np(kw.T), "wvp": _np(vw.T), "wbp": _np(bw.T),
        "cq_c": conv_cols(inp["L1_attn_q_conv_w"]),
        "ck_c": conv_cols(inp["L1_attn_k_conv_w"]),
        "cv_c": conv_cols(inp["L1_attn_v_conv_w"]),
        "eyeb_in": eyeb,
        "masks_in": masks,
    }


P_BR = 6
FIRK = (3, 7, 15, 31)
FIRO = (0, 24, 80, 200)
HALO = 32
AW = 4096 * 1024


def _phase_c1(ctx, real, nc, es, debug=(), internal=False):
    if True:
        okind = "Internal" if internal else "ExternalOutput"
        if real:
            def di(name, shape, dt=F32):
                return nc.dram_tensor(name, shape, dt, kind="ExternalInput").ap()
            sbuf = lambda name, shape, dt: es.enter_context(nc.sbuf_tensor(name, shape, dt))
            psum = lambda name, shape, dt=F32: es.enter_context(nc.psum_tensor(name, shape, dt))
        else:
            di = lambda *a, **k: _Mock()
            sbuf = lambda *a, **k: _Mock()
            psum = lambda *a, **k: _Mock()

        x1t = di("x1t", [D, TH])
        vh_t = di("vh_t", [D, TH + HALO], BF16)
        dl_t = di("dl_t", [D, TH], BF16)
        w1g_in = di("w1g_in", [D, 2048])
        w1s_in = di("w1s_in", [24, 2048])
        gb1_in = di("gb1_in", [NP, 16])
        w2g_in = di("w2g_in", [2048, 24])
        c24_in = di("c24_in", [24, 2])
        ltemp_in = di("ltemp_in", [1, 4])
        onw_in = di("onw_in", [NP, 2])
        wop_in = di("wop_in", [D, D])
        fwc_in = di("fwc_in", [NP, 448])
        ident_in = di("ident_in", [NP, NP], BF16)
        ones_in = di("ones_in", [NP, NP], BF16)
        wua_in = di("wua_in", [4096, D])
        wda_in = di("wda_in", [D, 4096])
        if real:
            s1_o = nc.dram_tensor("s1_o", [D, TH], BF16, kind=okind).ap()
            xn_o = nc.dram_tensor("xn_o", [D, TH], BF16, kind=okind).ap()
            alp_o = nc.dram_tensor("alp_o", [1, 2], F32, kind=okind).ap()
            scr = nc.dram_tensor("scr", [64, TH], F32, kind="Internal").ap()
            scr2 = nc.dram_tensor("scr2", [24, TH], BF16, kind="Internal").ap()
            dbg = {}
            for nm in debug:
                shape = {"d_xn1": [D, TH], "d_stats": [24, TH], "d_probs": [24, TH],
                         "d_fir31": [D, TH], "d_o": [D, TH], "d_scr": [64, TH],
                         "d_xnend": [D, TH]}[nm]
                dt_ = F32 if nm in ("d_stats", "d_scr") else BF16
                dbg[nm] = nc.dram_tensor(nm, shape, dt_, kind="ExternalOutput").ap()
        else:
            s1_o = xn_o = alp_o = scr = scr2 = _Mock()
            dbg = {nm: _Mock() for nm in debug}

        cols = sbuf("cols", [NP, 64], F32)
        onec = sbuf("onec", [NP, 1], BF16)
        ident = sbuf("ident", [NP, NP], BF16)
        onecB = sbuf("onecB", [NP, NP], BF16)
        fwc = sbuf("fwc", [NP, 448], F32)
        rowstage = sbuf("rowstage", [1, 2 * NC], F32)
        rw2 = sbuf("rw2", [1, NC], F32)
        c24 = sbuf("c24", [24, 8], F32)
        w1s = sbuf("w1s", [24, 2048], BF16)
        stats_sb = sbuf("stats_sb", [24, NC], BF16)
        lgf32 = sbuf("lgf32", [24, NC], F32)
        lg_hi = sbuf("lg_hi", [24, NC], BF16)
        lg_lo = sbuf("lg_lo", [24, NC], BF16)
        pfm = sbuf("pfm", [24, NC], BF16)
        arow = sbuf("arow", [1, 256], F32)
        smx = sbuf("smx", [NP, 72], F32)
        acol = sbuf("acol", [NP, 4], F32)
        ast = sbuf("ast", [NP, 2 * NC], F32)
        asq = sbuf("asq", [NP, NC], BF16)
        w1g_sb = sbuf("w1g_sb", [NP, KT * 2048], BF16)
        w2g_sb = sbuf("w2g_sb", [NP, 16 * 24], BF16)
        wop_sb = sbuf("wop_sb", [NP, KT * D], BF16)
        xn = sbuf("xn", [NP, KT * TH], BF16)
        def scoped(es_, side="left"):
            return (lambda name, shape, dt: es_.enter_context(nc.sbuf_tensor(name, shape, dt, side=side))) if real else (lambda *a: _Mock())
        es_x = ExitStack(); sb_x = scoped(es_x, "right")
        x1bf = sb_x("x1bf", [NP, KT * TH], BF16)
        rt0 = sb_x("rt0", [NP, TH], BF16)
        st0 = sb_x("st0", [NP, TH], BF16)
        sq0 = sb_x("sq0", [NP, KT * NC], BF16)
        if real:
            es_x.close()
        es_c = ExitStack(); sb_c = scoped(es_c, "right")
        vhb = sb_c("vhb", [NP, KT * (NC + HALO)], BF16)
        dlb = sb_c("dlb", [NP, KT * NC], BF16)
        fir = [sbuf(f"fir{p}", [NP, KT * NC], BF16) for p in range(4)]
        xnorm_b = fir[0]
        onrm = fir[1]
        o_acc = sbuf("o_acc", [NP, KT * NC], F32)
        sabs = sbuf("sabs", [NP, 4 * NC], BF16)
        h1 = sbuf("h1", [NP, 2 * NC], BF16)
        tmp = sbuf("tmp", [NP, NC], F32)
        pbt2 = sbuf("pbt2", [NP, 2 * NC], BF16)
        s1stg = sbuf("s1stg", [NP, 2 * NC], BF16)
        nro = sbuf("nro", [NP, 2 * NC], BF16)
        pbt = sbuf("pbt", [NP, 2 * 24], BF16)
        nrx = sbuf("nrx", [NP, NC], F32)

        bk = [psum(f"bk{i}", [NP, NC]) for i in range(4)]
        p_row = psum("p_row", [1, NC])
        lgps = psum("lgps", [24, NC])
        p_tr = psum("p_tr", [NP, 256], BF16)

        fsl = lambda buf, kt, c0, w=NC, Tlen=TH: buf[:, kt * Tlen + c0: kt * Tlen + c0 + w]
        csl = lambda buf, i, w=NC: buf[:, i * w:(i + 1) * w]
        vsl = lambda i, off, w=NC: vhb[:, i * (NC + HALO) + HALO + off: i * (NC + HALO) + HALO + off + w]

        def br_tile(p, i):
            if p < 4:
                return csl(fir[p], i)
            if p == 4:
                return vsl(i, 0)
            return csl(dlb, i)

        def astrip(k):
            if k < 64:
                return wua_in[(k // 2) * NP:(k // 2 + 1) * NP, (k % 2) * NC:(k % 2 + 1) * NC]
            k -= 64
            return wda_in[(k // 8) * NP:(k // 8 + 1) * NP, (k % 8) * NC:(k % 8 + 1) * NC]

        def gps(g):
            for kt in range(KT):
                ctx.inc(g.dma_start(out=fsl(x1bf, kt, 0, TH), in_=x1t[kt * NP:(kt + 1) * NP, :]), "lg", 16)
            for kt in range(KT):
                ctx.inc(g.dma_start(out=w1g_sb[:, kt * 2048:(kt + 1) * 2048],
                                    in_=w1g_in[kt * NP:(kt + 1) * NP, :]), "lg", 16)
            ctx.inc(g.dma_start(out=w1s[:, :], in_=w1s_in), "lg", 16)
            for kt in range(16):
                ctx.inc(g.dma_start(out=w2g_sb[:, kt * 24:(kt + 1) * 24],
                                    in_=w2g_in[kt * NP:(kt + 1) * NP, :]), "lg", 16)
            for kt in range(KT):
                ctx.inc(g.dma_start(out=wop_sb[:, kt * D:(kt + 1) * D],
                                    in_=wop_in[kt * NP:(kt + 1) * NP, :]), "lg", 16)
            ctx.mark("lg:init")
            ctx.wait(g, "lh:lnrows")
            ctx.inc(g.dma_start(out=rt0[:], in_=scr[31:32, :].partition_broadcast(NP)), "lg", 16)
            ctx.inc(g.dma_start(out=st0[:], in_=scr[32:33, :].partition_broadcast(NP)), "lg", 16)
            ctx.mark("lg:rs")
            for ntc in range(NT4):
                c0 = ntc * NC
                ctx.wait(g, f"lh:stat{ntc}")
                if ntc >= 1:
                    ctx.wait(g, f"mm:g{ntc - 1}_15")
                    ctx.wait(g, f"ve:lgf{ntc - 1}")
                ctx.inc(g.dma_start(out=stats_sb[:, :], in_=scr[0:24, c0:c0 + NC]), "lg", 16)
                ctx.mark(f"lg:ssb{ntc}")
                for p in range(P_BR):
                    for hh in range(4):
                        idx = p * 4 + hh
                        r = hh * 6 + p
                        ctx.wait(g, f"lh:pst{ntc}")
                        if idx >= 2 or ntc >= 1:
                            pi = (ntc, idx - 2) if idx >= 2 else (ntc - 1, 22 + idx)
                            ctx.wait(g, f"ve:mixt{pi[0]}_{pi[1]}")
                        ctx.inc(g.dma_start(out=csl(pbt2, idx % 2),
                                            in_=scr2[r:r + 1, c0:c0 + NC].partition_broadcast(NP)), "lg", 16)
                        ctx.mark(f"lg:pb{ntc}_{idx}")
                for hh in range(4):
                    ctx.wait(g, f"lh:onr{ntc}_{hh}")
                    if hh >= 2 or ntc >= 1:
                        pi = (ntc, hh - 2) if hh >= 2 else (ntc - 1, 2 + hh)
                        ctx.wait(g, f"ve:onrm{pi[0]}_{pi[1]}")
                    ctx.inc(g.dma_start(out=csl(nro, hh % 2),
                                        in_=scr[24 + hh:25 + hh, c0:c0 + NC].partition_broadcast(NP)), "lg", 16)
                    ctx.mark(f"lg:nro{ntc}_{hh}")
                for kk in range(32):
                    k = ntc * 32 + kk
                    if k >= 2:
                        ctx.wait(g, f"se:aa{k - 2}")
                    ctx.inc(g.dma_start(out=csl(ast, k % 2), in_=astrip(k)), "lg", 16)
                    ctx.mark(f"lg:as{k}")

        def syn(sy):
            ctx.inc(sy.dma_start(out=cols[:, 0:16], in_=gb1_in), "lh", 16)
            ctx.inc(sy.dma_start(out=cols[:, 16:18], in_=onw_in), "lh", 16)
            ctx.inc(sy.dma_start(out=fwc[:, :], in_=fwc_in), "lh", 16)
            ctx.inc(sy.dma_start(out=c24[:, 0:2], in_=c24_in), "lh", 16)
            ctx.inc(sy.dma_start(out=rw2[0:1, 0:4], in_=ltemp_in), "lh", 16)
            ctx.inc(sy.dma_start(out=ident[:, :], in_=ident_in), "lh", 16)
            ctx.inc(sy.dma_start(out=onecB[:, :], in_=ones_in), "lh", 16)
            ctx.mark("lh:cinit")
            ctx.wait(sy, "ve:trow")
            ctx.inc(sy.dma_start(out=scr[33:34, 0:4], in_=rw2[0:1, 4:8]), "lh", 16)
            ctx.mark("lh:trow")
            ctx.wait(sy, "lh:trow")
            for hh in range(4):
                ctx.inc(sy.dma_start(out=c24[hh * 6:(hh + 1) * 6, 2:3],
                                     in_=scr[33:34, hh:hh + 1].partition_broadcast(6)), "lh", 16)
            ctx.mark("lh:trcol")
            for ch in range(NT4):
                ctx.wait(sy, f"ve:lnr{ch}")
                ctx.inc(sy.dma_start(out=scr[31:32, ch * NC:(ch + 1) * NC], in_=rowstage[0:1, NC:2 * NC]), "lh", 16)
                ctx.inc(sy.dma_start(out=scr[32:33, ch * NC:(ch + 1) * NC], in_=rw2[0:1, :]), "lh", 16)
                ctx.mark(f"lh:lnr{ch}")
            ctx.mark("lh:lnrows")
            if "d_xn1" in dbg:
                ctx.wait(sy, "ve:xn_done")
                for kt in range(KT):
                    ctx.inc(sy.dma_start(out=dbg["d_xn1"][kt * NP:(kt + 1) * NP, :], in_=fsl(xn, kt, 0, TH)), "st", 16)
            for ntc in range(NT4):
                c0 = ntc * NC
                if ntc >= 1:
                    ctx.wait(sy, f"ve:mixd{ntc - 1}")
                else:
                    ctx.wait(sy, "ve:xn_done")
                for kt in range(KT):
                    ctx.inc(sy.dma_start(out=vhb[:, kt * (NC + HALO):(kt + 1) * (NC + HALO)],
                                         in_=vh_t[kt * NP:(kt + 1) * NP, c0:c0 + NC + HALO]), "lh", 16)
                ctx.mark(f"lh:vh{ntc}")
                for kt in range(KT):
                    ctx.inc(sy.dma_start(out=csl(dlb, kt), in_=dl_t[kt * NP:(kt + 1) * NP, c0:c0 + NC]), "lh", 16)
                ctx.mark(f"lh:dl{ntc}")
                for p in range(P_BR):
                    for hh in range(4):
                        idx = p * 4 + hh
                        r = hh * 6 + p
                        ctx.wait(sy, f"ve:sev{ntc}_{idx}")
                        ctx.inc(sy.dma_start(out=scr[r:r + 1, c0:c0 + NC],
                                             in_=rowstage[0:1, (idx % 2) * NC:(idx % 2 + 1) * NC]), "lh", 16)
                        ctx.mark(f"lh:sev{ntc}_{idx}")
                ctx.mark(f"lh:stat{ntc}")
                ctx.wait(sy, f"ve:pfm{ntc}_3")
                ctx.inc(sy.dma_start(out=scr2[:, c0:c0 + NC], in_=pfm[:, :]), "lh", 16)
                ctx.mark(f"lh:pst{ntc}")
                if "d_probs" in dbg:
                    ctx.inc(sy.dma_start(out=dbg["d_probs"][:, c0:c0 + NC], in_=pfm[:, :]), "st", 16)
                for hh in range(4):
                    ctx.wait(sy, f"ve:onr{ntc}_{hh}")
                    ctx.inc(sy.dma_start(out=scr[24 + hh:25 + hh, c0:c0 + NC],
                                         in_=rowstage[0:1, (hh % 2) * NC:(hh % 2 + 1) * NC]), "lh", 16)
                    ctx.mark(f"lh:onr{ntc}_{hh}")
                for mt in range(KT):
                    ctx.wait(sy, f"ve:s1e{ntc}_{mt}")
                    ctx.inc(sy.dma_start(out=s1_o[mt * NP:(mt + 1) * NP, c0:c0 + NC],
                                         in_=csl(s1stg, mt % 2)), "st", 16)
                    ctx.mark(f"st:s1{ntc}_{mt}")
                ctx.wait(sy, f"ve:xnormd{ntc}")
                for kt in range(KT):
                    ctx.inc(sy.dma_start(out=xn_o[kt * NP:(kt + 1) * NP, c0:c0 + NC],
                                         in_=fsl(xn, kt, c0)), "st", 16)
                ctx.mark(f"st:xn{ntc}")
                if "d_stats" in dbg:
                    ctx.wait(sy, f"lh:stat{ntc}")
                    ctx.inc(sy.dma_start(out=dbg["d_stats"][:, c0:c0 + NC], in_=scr[0:24, c0:c0 + NC]), "st", 16)
                if "d_fir31" in dbg:
                    ctx.wait(sy, f"ve:fir{ntc}_3")
                    for kt in range(KT):
                        ctx.inc(sy.dma_start(out=dbg["d_fir31"][kt * NP:(kt + 1) * NP, c0:c0 + NC],
                                             in_=csl(fir[3], kt)), "st", 16)
                if "d_o" in dbg:
                    ctx.wait(sy, f"ve:onrm{ntc}_3")
                    for kt in range(KT):
                        ctx.inc(sy.dma_start(out=dbg["d_o"][kt * NP:(kt + 1) * NP, c0:c0 + NC],
                                             in_=csl(onrm, kt)), "st", 16)
            ctx.wait(sy, "ve:acold")
            ctx.inc(sy.dma_start(out=scr[34:35, 0:NP].rearrange("o (p c) -> (o p) c", p=NP),
                                 in_=acol[:, 0:1], allow_slow_non_contiguous=True), "lh", 16)
            ctx.inc(sy.dma_start(out=scr[35:36, 0:NP].rearrange("o (p c) -> (o p) c", p=NP),
                                 in_=acol[:, 1:2], allow_slow_non_contiguous=True), "lh", 16)
            ctx.mark("lh:acol")
            ctx.wait(sy, "lh:acol")
            ctx.inc(sy.dma_start(out=arow[0:1, 0:NP], in_=scr[34:35, 0:NP]), "lh", 16)
            ctx.inc(sy.dma_start(out=arow[0:1, NP:2 * NP], in_=scr[35:36, 0:NP]), "lh", 16)
            ctx.mark("lh:arow")
            ctx.wait(sy, "ve:alp")
            ctx.inc(sy.dma_start(out=alp_o, in_=rw2[0:1, 6:8]), "st", 16)
            if "d_scr" in dbg:
                ctx.wait_cnt(sy, "lh", ctx.cnt["lh"])
                ctx.inc(sy.dma_start(out=dbg["d_scr"], in_=scr[:, :]), "st", 16)
            if "d_xnend" in dbg:
                ctx.wait_cnt(sy, "ve", ctx.cnt["ve"])
                for kt in range(KT):
                    ctx.inc(sy.dma_start(out=dbg["d_xnend"][kt * NP:(kt + 1) * NP, :],
                                         in_=xn[:, kt * TH:(kt + 1) * TH]), "st", 16)
            if real:
                sy.wait_ge(ctx.sem["st"], ctx.cnt["st"])

        def ten(t):
            ctx.wait(t, "lg:init")
            ctx.wait(t, "ve:ones")
            for ch in range(NT4):
                mi = accum_mm(t, p_row[0:1, :], [(onec[:, :], fsl(x1bf, kt, ch * NC)) for kt in range(KT)])
                ctx.inc(mi, "mm", 1)
                ctx.mark(f"mm:sta{ch}")
                ctx.wait(t, f"ve:sta{ch}")
                ctx.wait(t, f"se:xsq{ch}")
                mi = accum_mm(t, p_row[0:1, :], [(onec[:, :], sq0[:, kt * NC:(kt + 1) * NC]) for kt in range(KT)])
                ctx.inc(mi, "mm", 1)
                ctx.mark(f"mm:st{ch}")
                ctx.wait(t, f"ve:ste{ch}")
            for ntc in range(NT4):
                c0 = ntc * NC
                for p in range(P_BR):
                    for hh in range(4):
                        idx = p * 4 + hh
                        ctx.wait(t, f"se:ab{ntc}_{idx}")
                        mi = accum_mm(t, p_row[0:1, :], [(onec[:, :], csl(sabs, 2 * (idx % 2) + j))
                                                         for j in range(2)])
                        ctx.inc(mi, "mm", 1)
                        ctx.mark(f"mm:sb{ntc}_{idx}")
                        ctx.wait(t, f"ve:sev{ntc}_{idx}")
                ctx.wait(t, f"lg:ssb{ntc}")
                if ntc >= 1:
                    ctx.wait(t, f"ve:lgf{ntc - 1}")
                for mt in range(16):
                    pairs = [(w1g_sb[:, kt * 2048 + mt * NP: kt * 2048 + (mt + 1) * NP], fsl(xn, kt, c0))
                             for kt in range(KT)]
                    pairs.append((w1s[:, mt * NP:(mt + 1) * NP], stats_sb[:, :]))
                    ctx.inc(accum_mm(t, bk[mt % 2][:, :], pairs), "mm", 1)
                    ctx.mark(f"mm:g{ntc}_{mt}")
                    if mt >= 1:
                        ctx.wait(t, f"se:h1e{ntc}_{mt - 1}")
                        mi = t.matmul(lgps[:, :], w2g_sb[:, (mt - 1) * 24:mt * 24], csl(h1, (mt - 1) % 2),
                                      start=(mt == 1), stop=False)
                        ctx.inc(mi, "mm", 1)
                        ctx.mark(f"mm:gl{ntc}_{mt - 1}")
                ctx.wait(t, f"se:h1e{ntc}_15")
                mi = t.matmul(lgps[:, :], w2g_sb[:, 15 * 24:16 * 24], csl(h1, 1),
                              start=False, stop=True)
                ctx.inc(mi, "mm", 1)
                ctx.mark(f"mm:gl{ntc}_15")
                ctx.wait(t, f"ve:lgsplit{ntc}")
                if ntc >= 1:
                    ctx.wait(t, f"ve:pfm{ntc - 1}_3")
                for blk in range(4):
                    q = (blk % 2) * 48
                    if blk >= 2:
                        ctx.wait(t, f"ve:pfm{ntc}_{blk - 2}")
                    elif blk == 1:
                        ctx.wait(t, f"ve:lt{ntc}_0")
                    t.transpose(p_tr[:, q:q + 24], lg_hi[:, blk * NP:(blk + 1) * NP], ident[0:24, 0:24])
                    mi = t.transpose(p_tr[:, q + 24:q + 48], lg_lo[:, blk * NP:(blk + 1) * NP], ident[0:24, 0:24])
                    ctx.inc(mi, "mm", 1)
                    ctx.mark(f"mm:tr{ntc}_{blk}")
                    ctx.wait(t, f"ve:pb{ntc}_{blk}")
                    mi = t.transpose(p_tr[0:24, 128:256], pbt[:, (blk % 2) * 24:(blk % 2 + 1) * 24],
                                     ident[:, :])
                    ctx.inc(mi, "mm", 1)
                    ctx.mark(f"mm:ptr{ntc}_{blk}")
                for hh in range(4):
                    ctx.wait(t, f"se:osq{ntc}_{hh}")
                    mi = accum_mm(t, p_row[0:1, :], [(onec[:, :], csl(sabs, 2 * (hh % 2) + j))
                                                     for j in range(2)])
                    ctx.inc(mi, "mm", 1)
                    ctx.mark(f"mm:on{ntc}_{hh}")
                    ctx.wait(t, f"ve:onre{ntc}_{hh}")
                for mt in range(KT):
                    if mt == 0:
                        ctx.wait(t, f"ve:onrm{ntc}_3")
                    pairs = [(wop_sb[:, kt * D + mt * NP: kt * D + (mt + 1) * NP], csl(onrm, kt))
                             for kt in range(KT)]
                    ctx.inc(accum_mm(t, bk[mt % 2][:, :], pairs), "mm", 1)
                    ctx.mark(f"mm:op{ntc}_{mt}")
                    if mt >= 1:
                        ctx.wait(t, f"ve:s1e{ntc}_{mt - 1}")
                ctx.wait(t, f"ve:s1e{ntc}_7")
                for gq in range(4):
                    ctx.wait(t, f"se:xq{ntc}_{gq}")
                    for j in range(2):
                        mi = t.matmul(bk[2][:, :], onecB[:, :], csl(sabs, 2 * (gq % 2) + j),
                                      start=(gq == 0 and j == 0), stop=(gq == 3 and j == 1))
                    ctx.inc(mi, "mm", 1)
                    ctx.mark(f"mm:xr{ntc}_{gq}")
                ctx.wait(t, f"se:xns{ntc}")


        def sca(s):
            ctx.wait(s, "lg:init")
            ctx.wait(s, "lh:cinit")
            ctx.wait(s, "ve:spl0")
            ctx.inc(s.activation(rw2[0:1, 4:8], rw2[0:1, 4:8], AF.Exp), "se", 1)
            ctx.mark("se:spexp")
            ctx.wait(s, "ve:spl1")
            ctx.inc(s.activation(rw2[0:1, 4:8], rw2[0:1, 4:8], AF.Ln), "se", 1)
            ctx.mark("se:spln")
            for ch in range(NT4):
                if ch >= 1:
                    ctx.wait(s, f"mm:st{ch - 1}")
                mi = None
                for kt in range(KT):
                    mi = s.activation(sq0[:, kt * NC:(kt + 1) * NC], fsl(x1bf, kt, ch * NC), AF.Square)
                ctx.inc(mi, "se", 1)
                ctx.mark(f"se:xsq{ch}")
                if ch >= 1:
                    ctx.wait(s, f"ve:lnv{ch - 1}")
                    ctx.inc(s.activation(rowstage[0:1, NC:2 * NC], rowstage[0:1, NC:2 * NC], AF.Sqrt,
                                         bias=cols[0:1, 62:63]), "se", 1)
                    ctx.mark(f"se:lns{ch - 1}")
            ctx.wait(s, f"ve:lnv{NT4 - 1}")
            ctx.inc(s.activation(rowstage[0:1, NC:2 * NC], rowstage[0:1, NC:2 * NC], AF.Sqrt,
                                 bias=cols[0:1, 62:63]), "se", 1)
            ctx.mark(f"se:lns{NT4 - 1}")
            for ntc in range(NT4):
                for p in range(P_BR):
                    for hh in range(4):
                        idx = p * 4 + hh
                        ctx.wait(s, f"ve:fir{ntc}_{p}" if p < 4 else (f"lh:vh{ntc}" if p == 4 else f"lh:dl{ntc}"))
                        if idx >= 2:
                            ctx.wait(s, f"mm:sb{ntc}_{idx - 2}")
                        elif ntc >= 1:
                            ctx.wait(s, f"mm:xr{ntc - 1}_{2 + idx}")
                        for j in range(2):
                            s.activation(csl(sabs, 2 * (idx % 2) + j), br_tile(p, 2 * hh + j), AF.Abs)
                        mi = s.activation(cols[:, 59:60], cols[:, 62:63], AF.Copy)
                        ctx.inc(mi, "se", 1)
                        ctx.mark(f"se:ab{ntc}_{idx}")
                for mt in range(16):
                    ctx.wait(s, f"mm:g{ntc}_{mt}")
                    if mt >= 2:
                        ctx.wait(s, f"mm:gl{ntc}_{mt - 2}")
                    mi = s.activation(csl(h1, mt % 2), bk[mt % 2][:, :], AF.Gelu,
                                      bias=cols[:, mt:mt + 1])
                    ctx.inc(mi, "se", 1)
                    ctx.mark(f"se:h1e{ntc}_{mt}")
                for blk in range(4):
                    ctx.wait(s, f"ve:mxn{ntc}_{blk}")
                    for hh in range(4):
                        mi = s.activation(smx[:, 24 + hh * 6:24 + (hh + 1) * 6],
                                          smx[:, 48 + hh * 6:48 + (hh + 1) * 6], AF.Exp,
                                          bias=smx[:, 8 + hh:9 + hh],
                                          accum_out=smx[:, 16 + hh:17 + hh])
                    ctx.inc(mi, "se", 1)
                    ctx.mark(f"se:exp{ntc}_{blk}")
                for hh in range(4):
                    if hh == 0:
                        ctx.wait(s, f"ve:mixd{ntc}")
                    if hh >= 2:
                        ctx.wait(s, f"mm:on{ntc}_{hh - 2}")
                    else:
                        ctx.wait(s, f"mm:sb{ntc}_{22 + hh}")
                    for j in range(2):
                        s.activation(csl(sabs, 2 * (hh % 2) + j), csl(o_acc, 2 * hh + j), AF.Square)
                    mi = s.activation(cols[:, 59:60], cols[:, 62:63], AF.Copy)
                    ctx.inc(mi, "se", 1)
                    ctx.mark(f"se:osq{ntc}_{hh}")
                for hh in range(4):
                    ctx.wait(s, f"ve:onre{ntc}_{hh}")
                    mi = s.activation(rowstage[0:1, (hh % 2) * NC:(hh % 2 + 1) * NC],
                                      rowstage[0:1, (hh % 2) * NC:(hh % 2 + 1) * NC], AF.Sqrt,
                                      scale=1.0 / 256, bias=cols[0:1, 62:63])
                    ctx.inc(mi, "se", 1)
                    ctx.mark(f"se:ons{ntc}_{hh}")
                for gq in range(4):
                    if gq >= 2:
                        ctx.wait(s, f"mm:xr{ntc}_{gq - 2}")
                    else:
                        ctx.wait(s, f"mm:on{ntc}_{2 + gq}")
                    for j in range(2):
                        s.activation(csl(sabs, 2 * (gq % 2) + j), fsl(xn, 2 * gq + j, ntc * NC), AF.Square)
                    mi = s.activation(cols[:, 59:60], cols[:, 62:63], AF.Copy)
                    ctx.inc(mi, "se", 1)
                    ctx.mark(f"se:xq{ntc}_{gq}")
                ctx.wait(s, f"mm:xr{ntc}_3")
                if ntc >= 1:
                    ctx.wait(s, f"ve:xnormd{ntc - 1}")
                s.activation(nrx[:, :], bk[2][:, :], AF.Sqrt, scale=1.0 / D)
                mi = s.activation(cols[:, 59:60], cols[:, 62:63], AF.Copy)
                ctx.inc(mi, "se", 1)
                ctx.mark(f"se:xns{ntc}")
                for kk in range(32):
                    k = ntc * 32 + kk
                    ctx.wait(s, f"lg:as{k}")
                    if k >= 2:
                        ctx.wait(s, f"ve:ac{k - 2}")
                    mi = s.activation(asq[:, :], csl(ast, k % 2), AF.Abs,
                                      accum_out=acol[:, 2 + k % 2:3 + k % 2])
                    ctx.inc(mi, "se", 1)
                    ctx.mark(f"se:aa{k}")
            ctx.wait(s, "lh:arow")
            s.activation(asq[0:1, 0:NP], arow[0:1, 0:NP], AF.Copy, accum_out=rw2[0:1, 6:7])
            mi = s.activation(asq[0:1, 0:NP], arow[0:1, NP:2 * NP], AF.Copy, accum_out=rw2[0:1, 7:8])
            ctx.inc(mi, "se", 1)
            ctx.mark("se:asum")

        def vec(v):
            v.memset(cols[:, 62:63], LN_EPS)
            v.memset(cols[:, 60:61], 1e-8)
            v.memset(acol[:, :], 0.0)
            ctx.wait(v, "lh:cinit")
            ctx.inc(v.memset(onec[:], 1.0), "ve", 1)
            ctx.mark("ve:ones")
            spc = lambda: v.tensor_copy(cols[:, 61:62], cols[:, 62:63])
            ctx.wait(v, "lh:cinit")
            ctx.inc(v.tensor_scalar(rw2[0:1, 4:8], rw2[0:1, 0:4], 1.0, None, ALU.mult), "ve", 1)
            ctx.mark("ve:spl0")
            ctx.wait(v, "se:spexp")
            ctx.inc(v.tensor_scalar(rw2[0:1, 4:8], rw2[0:1, 4:8], 1.0, None, ALU.add), "ve", 1)
            ctx.mark("ve:spl1")
            ctx.wait(v, "se:spln")
            v.tensor_scalar(rw2[0:1, 4:8], rw2[0:1, 4:8], 1e-4, None, ALU.add)
            spc()
            ctx.inc(v.reciprocal(rw2[0:1, 4:8], rw2[0:1, 4:8]), "ve", 1)
            ctx.mark("ve:trow")
            for ch in range(NT4):
                ctx.wait(v, f"mm:sta{ch}")
                ctx.inc(v.tensor_copy(rowstage[0:1, 0:NC], p_row[0:1, :]), "ve", 1)
                ctx.mark(f"ve:sta{ch}")
                ctx.wait(v, f"mm:st{ch}")
                if ch >= 1:
                    ctx.wait(v, f"lh:lnr{ch - 1}")
                ctx.inc(v.tensor_copy(rowstage[0:1, NC:2 * NC], p_row[0:1, :]), "ve", 1)
                ctx.mark(f"ve:ste{ch}")
                v.tensor_scalar(rowstage[0:1, 0:NC], rowstage[0:1, 0:NC], 1.0 / D, None, ALU.mult)
                v.tensor_scalar(rowstage[0:1, NC:2 * NC], rowstage[0:1, NC:2 * NC], 1.0 / D, None, ALU.mult)
                v.tensor_mul(rw2[0:1, :], rowstage[0:1, 0:NC], rowstage[0:1, 0:NC])
                spc()
                ctx.inc(v.tensor_sub(rowstage[0:1, NC:2 * NC], rowstage[0:1, NC:2 * NC], rw2[0:1, :]), "ve", 1)
                ctx.mark(f"ve:lnv{ch}")
                ctx.wait(v, f"se:lns{ch}")
                v.reciprocal(rowstage[0:1, NC:2 * NC], rowstage[0:1, NC:2 * NC])
                spc()
                ctx.inc(v.tensor_mul(rw2[0:1, :], rowstage[0:1, 0:NC], rowstage[0:1, NC:2 * NC]), "ve", 1)
                ctx.mark(f"ve:lnr{ch}")
            ctx.wait(v, "lg:rs")
            mi = None
            for kt in range(KT):
                for cc in range(NT4):
                    mo = bk[3][:, :]
                    c0 = cc * NC
                    v.tensor_mul(mo, fsl(x1bf, kt, c0), rt0[:, c0:c0 + NC])
                    v.tensor_sub(mo, mo, st0[:, c0:c0 + NC])
                    mi = v.tensor_copy(fsl(xn, kt, c0), mo)
            ctx.inc(mi, "ve", 1)
            ctx.mark("ve:xn_done")
            for ntc in range(NT4):
                c0 = ntc * NC
                ctx.wait(v, f"lh:vh{ntc}")
                if ntc >= 1:
                    ctx.wait(v, f"mm:op{ntc - 1}_7")
                    ctx.wait(v, f"st:xn{ntc - 1}")
                    ctx.wait(v, f"se:ab{ntc - 1}_23")
                for p in range(4):
                    K = FIRK[p]
                    for i in range(KT):
                        v.tensor_scalar(csl(fir[p], i), vsl(i, -(K - 1)),
                                        fwc[:, FIRO[p] + i * K: FIRO[p] + i * K + 1], None, ALU.mult)
                        for tau in range(1, K):
                            mi = v.scalar_tensor_tensor(csl(fir[p], i), vsl(i, -(K - 1) + tau),
                                                        fwc[:, FIRO[p] + i * K + tau: FIRO[p] + i * K + tau + 1],
                                                        csl(fir[p], i), ALU.mult, ALU.add)
                    ctx.inc(mi, "ve", 1)
                    ctx.mark(f"ve:fir{ntc}_{p}")
                for p in range(P_BR):
                    for hh in range(4):
                        idx = p * 4 + hh
                        ctx.wait(v, f"mm:sb{ntc}_{idx}")
                        if idx >= 2:
                            ctx.wait(v, f"lh:sev{ntc}_{idx - 2}")
                        elif ntc >= 1:
                            ctx.wait(v, f"lh:onr{ntc - 1}_{2 + idx}")
                        mi = v.tensor_scalar(rowstage[0:1, (idx % 2) * NC:(idx % 2 + 1) * NC],
                                             p_row[0:1, :], 1.0 / 256, None, ALU.mult)
                        ctx.inc(mi, "ve", 1)
                        ctx.mark(f"ve:sev{ntc}_{idx}")
                ctx.wait(v, f"mm:gl{ntc}_15")
                if ntc >= 1:
                    ctx.wait(v, f"mm:tr{ntc - 1}_3")
                v.scalar_tensor_tensor(lgf32[:, :], stats_sb[:, :], c24[:, 0:1], lgps[:, :],
                                       ALU.mult, ALU.add)
                v.tensor_scalar(lgf32[:, :], lgf32[:, :], c24[:, 1:2], None, ALU.add)
                ctx.inc(v.tensor_scalar(lgf32[:, :], lgf32[:, :], c24[:, 2:3], None, ALU.mult), "ve", 1)
                ctx.mark(f"ve:lgf{ntc}")
                v.tensor_copy(lg_hi[:, :], lgf32[:, :])
                spc()
                ctx.inc(v.scalar_tensor_tensor(lg_lo[:, :], lg_hi[:, :], -1.0, lgf32[:, :],
                                               ALU.mult, ALU.add), "ve", 1)
                ctx.mark(f"ve:lgsplit{ntc}")
                for blk in range(4):
                    ctx.wait(v, f"mm:tr{ntc}_{blk}")
                    q = (blk % 2) * 48
                    v.tensor_copy(smx[:, 48:72], p_tr[:, q:q + 24])
                    spc()
                    ctx.inc(v.tensor_add(smx[:, 48:72], smx[:, 48:72], p_tr[:, q + 24:q + 48]), "ve", 1)
                    ctx.mark(f"ve:lt{ntc}_{blk}")
                    spc()
                    for hh in range(4):
                        v.reduce_max(smx[:, hh:hh + 1], smx[:, 48 + hh * 6:48 + (hh + 1) * 6], axis=AX.X)
                    for hh in range(4):
                        mi = v.tensor_scalar(smx[:, 8 + hh:9 + hh], smx[:, hh:hh + 1], -1.0, None, ALU.mult)
                    ctx.inc(mi, "ve", 1)
                    ctx.mark(f"ve:mxn{ntc}_{blk}")
                    ctx.wait(v, f"se:exp{ntc}_{blk}")
                    for hh in range(4):
                        v.reciprocal(smx[:, 16 + hh:17 + hh], smx[:, 16 + hh:17 + hh])
                    for hh in range(4):
                        v.tensor_scalar(smx[:, 24 + hh * 6:24 + (hh + 1) * 6],
                                        smx[:, 24 + hh * 6:24 + (hh + 1) * 6],
                                        smx[:, 16 + hh:17 + hh], None, ALU.mult)
                    spc()
                    v.tensor_scalar(smx[:, 24:48], smx[:, 24:48], 1.0 - 0.05, None, ALU.mult)
                    spc()
                    for hh in range(4):
                        v.tensor_scalar(smx[:, 24 + hh * 6 + 5:24 + hh * 6 + 6],
                                        smx[:, 24 + hh * 6 + 5:24 + hh * 6 + 6], 0.05, None, ALU.add)
                    spc()
                    ctx.inc(v.tensor_copy(pbt[:, (blk % 2) * 24:(blk % 2 + 1) * 24], smx[:, 24:48]), "ve", 1)
                    ctx.mark(f"ve:pb{ntc}_{blk}")
                    ctx.wait(v, f"mm:ptr{ntc}_{blk}")
                    ctx.inc(v.tensor_copy(pfm[:, blk * NP:(blk + 1) * NP], p_tr[0:24, 128:256]), "ve", 1)
                    ctx.mark(f"ve:pfm{ntc}_{blk}")
                for p in range(P_BR):
                    for hh in range(4):
                        idx = p * 4 + hh
                        ctx.wait(v, f"lg:pb{ntc}_{idx}")
                        if p == 0 and hh == 0 and ntc >= 1:
                            ctx.wait(v, f"se:osq{ntc - 1}_3")
                        for j in range(2):
                            i = 2 * hh + j
                            if p == 0:
                                mi = v.tensor_mul(csl(o_acc, i), br_tile(p, i), csl(pbt2, idx % 2))
                            else:
                                v.tensor_mul(tmp[:, :], br_tile(p, i), csl(pbt2, idx % 2))
                                mi = v.tensor_add(csl(o_acc, i), csl(o_acc, i), tmp[:, :])
                        ctx.inc(mi, "ve", 1)
                        ctx.mark(f"ve:mixt{ntc}_{idx}")
                ctx.mark(f"ve:mixd{ntc}")
                for hh in range(4):
                    ctx.wait(v, f"mm:on{ntc}_{hh}")
                    if hh >= 2:
                        ctx.wait(v, f"lh:onr{ntc}_{hh - 2}")
                    else:
                        ctx.wait(v, f"lh:sev{ntc}_{22 + hh}")
                    ctx.inc(v.tensor_copy(rowstage[0:1, (hh % 2) * NC:(hh % 2 + 1) * NC], p_row[0:1, :]), "ve", 1)
                    ctx.mark(f"ve:onre{ntc}_{hh}")
                    ctx.wait(v, f"se:ons{ntc}_{hh}")
                    spc()
                    ctx.inc(v.reciprocal(rowstage[0:1, (hh % 2) * NC:(hh % 2 + 1) * NC],
                                         rowstage[0:1, (hh % 2) * NC:(hh % 2 + 1) * NC]), "ve", 1)
                    ctx.mark(f"ve:onr{ntc}_{hh}")
                for hh in range(4):
                    ctx.wait(v, f"lg:nro{ntc}_{hh}")
                    for j in range(2):
                        i = 2 * hh + j
                        mi = v.scalar_tensor_tensor(csl(onrm, i), csl(o_acc, i), cols[:, 16 + j:17 + j],
                                                    csl(nro, hh % 2), ALU.mult, ALU.mult)
                    ctx.inc(mi, "ve", 1)
                    ctx.mark(f"ve:onrm{ntc}_{hh}")
                for mt in range(KT):
                    ctx.wait(v, f"mm:op{ntc}_{mt}")
                    if mt >= 2:
                        ctx.wait(v, f"st:s1{ntc}_{mt - 2}")
                    elif ntc >= 1:
                        ctx.wait(v, f"st:s1{ntc - 1}_{6 + mt}")
                    ctx.inc(v.tensor_copy(csl(s1stg, mt % 2), bk[mt % 2][:, :]), "ve", 1)
                    ctx.mark(f"ve:s1e{ntc}_{mt}")
                ctx.wait(v, f"se:xns{ntc}")
                v.tensor_scalar(nrx[:, :], nrx[:, :], 1e-8, None, ALU.add)
                spc()
                v.reciprocal(nrx[:, :], nrx[:, :])
                spc()
                ctx.wait(v, f"mm:g{ntc}_15")
                mi = None
                for kt in range(KT):
                    mi = v.tensor_mul(fsl(xn, kt, c0), fsl(xn, kt, c0), nrx[:, :])
                ctx.inc(mi, "ve", 1)
                ctx.mark(f"ve:xnormd{ntc}")
                for kk in range(32):
                    k = ntc * 32 + kk
                    ctx.wait(v, f"se:aa{k}")
                    dst = 0 if k < 64 else 1
                    spc()
                    ctx.inc(v.tensor_add(acol[:, dst:dst + 1], acol[:, dst:dst + 1],
                                         acol[:, 2 + k % 2:3 + k % 2]), "ve", 1)
                    ctx.mark(f"ve:ac{k}")
            ctx.mark("ve:acold")
            ctx.wait(v, "se:asum")
            spc()
            ctx.inc(v.tensor_scalar(rw2[0:1, 6:8], rw2[0:1, 6:8], 1.0 / AW, None, ALU.mult), "ve", 1)
            ctx.mark("ve:alp")

        if real:
            es_c.close()
        return ({"gpsimd": gps, "sync": syn, "scalar": sca, "tensor": ten,
                 "vector": vec},
                {"x1t": x1t, "s1_o": s1_o, "xn_o": xn_o, "alp_o": alp_o,
                 "ones_in": ones_in})


def build_launch_c1(debug=()):
    ctx = Ctx(["lg", "lh", "mm", "ve", "se", "st"])

    def emit(real, nc=None):
        es = ExitStack()
        ctx.start(real, nc, es)
        fns, _ = _phase_c1(ctx, real, nc, es, debug)
        block = es.enter_context(nc.Block()) if real else _Mock()
        for name in ("gpsimd", "sync", "scalar", "tensor", "vector"):
            if real:
                getattr(block, name)(fns[name])
            else:
                fns[name](_Mock())
        es.close()

    emit(False)
    emit(False)
    nc = bass.Bass("TRN2", target_bir_lowering=False, debug=False)
    emit(True, nc)
    return nc


def host_inputs_c1(inp, core, x1_full, v_full, dl_full):
    b, hf = core // 2, core % 2
    t0 = hf * TH
    x1 = np.asarray(x1_full[b])
    vt = np.asarray(v_full[b]).T
    dlt = np.asarray(dl_full[b]).T
    vh = np.zeros((D, TH + HALO), ml_dtypes.bfloat16)
    lo = max(0, t0 - HALO)
    vh[:, HALO - (t0 - lo):] = vt[:, lo:t0 + TH][:, :]
    gb1 = np.asarray(inp["L1_attn_gate_b1"])
    gb2 = np.asarray(inp["L1_attn_gate_b2"])
    astat = np.asarray(inp["L1_attn_alpha_stat"])
    w1 = np.asarray(inp["L1_attn_gate_w1"])
    fw = []
    for key, K in (("L1_attn_fir_w3", 3), ("L1_attn_fir_w7", 7),
                   ("L1_attn_fir_w15", 15), ("L1_attn_fir_w31", 31)):
        w = np.asarray(inp[key])
        fw.append(w.reshape(KT, NP, K).transpose(1, 0, 2).reshape(NP, KT * K))
    return {
        "x1t": _np(x1[t0:t0 + TH].T),
        "vh_t": vh,
        "dl_t": np.ascontiguousarray(dlt[:, t0:t0 + TH]),
        "w1g_in": _np(w1[:, 0:D].T),
        "w1s_in": _np(w1[:, D:D + 24].T),
        "gb1_in": _np(gb1.reshape(16, NP).T),
        "w2g_in": _np(np.asarray(inp["L1_attn_gate_w2"]).T),
        "c24_in": _np(np.stack([np.tile(np.asarray(astat), 4),
                                gb2], axis=1)),
        "ltemp_in": _np(np.asarray(inp["L1_attn_gate_log_temp"]).reshape(1, 4)),
        "onw_in": _np(np.asarray(inp["L1_attn_o_norm_w"]).reshape(2, NP).T),
        "wop_in": _np(np.asarray(inp["L1_attn_o_proj_w"]).T),
        "fwc_in": _np(np.concatenate(fw, axis=1)),
        "ident_in": np.eye(NP, dtype=ml_dtypes.bfloat16),
        "ones_in": np.ones((NP, NP), dtype=ml_dtypes.bfloat16),
        "wua_in": _np(inp["L1_ffn_w_up"]),
        "wda_in": _np(inp["L1_ffn_w_down"]),
    }


def _phase_c2(ctx, real, nc, es, debug=(), ext=None, sfx=""):
    NSTR = 64
    if True:
        if ext is None:
            ext = {}
        if real:
            def di(name, shape, dt=F32):
                if name in ext:
                    return ext[name]
                return nc.dram_tensor(name + sfx, shape, dt, kind="ExternalInput").ap()
            sbuf = lambda name, shape, dt: es.enter_context(nc.sbuf_tensor(name + sfx, shape, dt))
            psum = lambda name, shape, dt=F32: es.enter_context(nc.psum_tensor(name + sfx, shape, dt))
        else:
            di = lambda *a, **k: _Mock()
            sbuf = lambda *a, **k: _Mock()
            psum = lambda *a, **k: _Mock()

        x1t = di("x1t", [D, TH])
        s1t = di("s1t", [D, TH], BF16)
        xnt = di("xnt", [D, TH], BF16)
        ones_in = di("ones_in", [NP, NP], BF16)
        wut_in = di("wut_in", [D, 4096])
        wdt_in = di("wdt_in", [4096, D])
        alp_in = di("alp_in", [1, 2])
        if real:
            out_o = nc.dram_tensor("out_o", [D, TH], F32, kind="ExternalOutput").ap()
            scr = nc.dram_tensor("scr" + sfx, [4, TH], F32, kind="Internal").ap()
            dbg = {}
            for nm in debug:
                shape = {"d_h": [NP, 32 * NC], "d_g": [D, TH], "d_wtu": [NP, 4096]}[nm]
                dbg[nm] = nc.dram_tensor(nm, shape, BF16, kind="ExternalOutput").ap()
        else:
            out_o = scr = _Mock()
            dbg = {nm: _Mock() for nm in debug}

        cols = sbuf("cols", [NP, 8], F32)
        onec = sbuf("onec", [NP, 1], BF16)
        onecB = sbuf("onecB", [NP, NP], BF16)
        nrx2 = sbuf("nrx2", [NP, NC], F32)
        rw2 = sbuf("rw2", [1, NC], F32)
        wtu = sbuf("wtu", [NP, KT * 4096], BF16)
        wtd = sbuf("wtd", [NP, 32 * D], BF16)
        h = sbuf("h", [NP, 32 * NC], BF16)
        xnb = sbuf("xnb", [NP, 2 * KT * NC], BF16)
        hsq = sbuf("hsq", [NP, 2 * NC], BF16)
        hnr = sbuf("hnr", [NP, NC], BF16)
        def scoped(es_, side="left"):
            return (lambda name, shape, dt: es_.enter_context(nc.sbuf_tensor(name, shape, dt, side=side))) if real else (lambda *a: _Mock())
        es_t = ExitStack(); sb_t = scoped(es_t, "right")
        tstg = sb_t("tstg", [NP, 2 * 1024], F32)
        absb = sb_t("absb", [NP, 2 * 1024], F32)
        sgn = sb_t("sgn", [NP, 2 * 1024], BF16)
        msk = sb_t("msk", [NP, 1024], BF16)
        if real:
            es_t.close()
        es_c = ExitStack(); sb_c = scoped(es_c, "right")
        g_sb = sb_c("g_sb", [NP, 2 * NC], BF16)
        x1f = sb_c("x1f", [NP, 2 * NC], F32)
        s1b = sb_c("s1b", [NP, 2 * NC], BF16)
        outb = sb_c("outb", [NP, 2 * NC], F32)
        tmp2 = sb_c("tmp2", [NP, NC], F32)

        bk = [psum(f"bk{i}", [NP, NC]) for i in range(4)]
        p_hs = psum("p_hs", [NP, NC])

        csl = lambda buf, i, w=NC: buf[:, i * w:(i + 1) * w]

        def tsrc(k):
            if k < 32:
                kt, s5 = k // 4, k % 4
                return (wut_in[kt * NP:(kt + 1) * NP, s5 * 1024:(s5 + 1) * 1024],
                        wtu[:, kt * 4096 + s5 * 1024: kt * 4096 + (s5 + 1) * 1024], 0)
            k -= 32
            return (wdt_in[k * NP:(k + 1) * NP, :], wtd[:, k * D:(k + 1) * D], 1)

        def gps(g):
            ctx.inc(g.dma_start(out=rw2[0:1, 8:10], in_=alp_in), "lg", 16)
            ctx.mark("lg:alp")
            ctx.wait(g, "lh:thr")
            ctx.inc(g.dma_start(out=cols[:, 0:2], in_=scr[1:2, 0:2].partition_broadcast(NP)), "lg", 16)
            ctx.mark("lg:thrc")
            for k in range(NSTR):
                if k >= 2:
                    ctx.wait(g, f"se:tsa{k - 2}")
                ctx.inc(g.dma_start(out=csl(tstg, k % 2, 1024), in_=tsrc(k)[0]), "lg", 16)
                ctx.mark(f"lg:ts{k}")


        def syn(sy):
            ctx.inc(sy.dma_start(out=onecB[:, :], in_=ones_in), "lh", 16)
            ctx.mark("lh:ones")
            ctx.wait(sy, "ve:thr")
            ctx.inc(sy.dma_start(out=scr[1:2, 0:2], in_=rw2[0:1, 10:12]), "lh", 16)
            ctx.mark("lh:thr")
            for ntc in range(NT4):
                c0 = ntc * NC
                if ntc >= 2:
                    ctx.wait(sy, f"mm:u{ntc - 2}_31")
                for kt in range(KT):
                    ctx.inc(sy.dma_start(out=xnb[:, ((ntc % 2) * KT + kt) * NC:((ntc % 2) * KT + kt + 1) * NC],
                                         in_=xnt[kt * NP:(kt + 1) * NP, c0:c0 + NC]), "lh", 16)
                ctx.mark(f"lh:xn{ntc}")
                for mt2 in range(KT):
                    if ntc == 0 and mt2 == 0:
                        ctx.wait(sy, "ve:wt63")
                    if mt2 >= 2 or ntc >= 1:
                        pi = (ntc, mt2 - 2) if mt2 >= 2 else (ntc - 1, 6 + mt2)
                        ctx.wait(sy, f"ve:out{pi[0]}_{pi[1]}")
                    ctx.inc(sy.dma_start(out=csl(x1f, mt2 % 2), in_=x1t[mt2 * NP:(mt2 + 1) * NP, c0:c0 + NC]), "lh", 16)
                    ctx.inc(sy.dma_start(out=csl(s1b, mt2 % 2), in_=s1t[mt2 * NP:(mt2 + 1) * NP, c0:c0 + NC]), "lh", 16)
                    ctx.mark(f"lh:ep{ntc}_{mt2}")
                    ctx.wait(sy, f"ve:out{ntc}_{mt2}")
                    ctx.inc(sy.dma_start(out=out_o[mt2 * NP:(mt2 + 1) * NP, c0:c0 + NC],
                                         in_=csl(outb, mt2 % 2)), "st", 16)
                    ctx.mark(f"st:o{ntc}_{mt2}")
                if "d_h" in dbg and ntc == 0:
                    ctx.wait(sy, f"ve:hn{ntc}")
                    ctx.inc(sy.dma_start(out=dbg["d_h"], in_=h[:, :]), "st", 16)
            if "d_wtu" in dbg:
                ctx.wait(sy, "ve:wt7")
                ctx.inc(sy.dma_start(out=dbg["d_wtu"], in_=wtu[:, 0:4096]), "st", 16)
            if real:
                sy.wait_ge(ctx.sem["st"], ctx.cnt["st"])

        def ten(t):
            ctx.wait(t, "ve:ones")
            ctx.wait(t, "lh:ones")
            for ntc in range(NT4):
                ctx.wait(t, "ve:wt31")
                ctx.wait(t, f"lh:xn{ntc}")
                for mt in range(32):
                    pairs = [(wtu[:, kt * 4096 + mt * NP: kt * 4096 + (mt + 1) * NP],
                              csl(xnb, (ntc % 2) * KT + kt)) for kt in range(KT)]
                    ctx.inc(accum_mm(t, bk[mt % 2][:, :], pairs), "mm", 1)
                    ctx.mark(f"mm:u{ntc}_{mt}")
                    if mt >= 1:
                        ctx.wait(t, f"se:ug{ntc}_{mt - 1}")
                        mi = t.matmul(p_hs[:, :], onecB[:, :], csl(hsq, (mt - 1) % 2),
                                      start=(mt == 1), stop=False)
                        ctx.inc(mi, "mm", 1)
                        ctx.mark(f"mm:hs{ntc}_{mt - 1}")
                ctx.wait(t, f"se:ug{ntc}_31")
                mi = t.matmul(p_hs[:, :], onecB[:, :], csl(hsq, 31 % 2), start=False, stop=True)
                ctx.inc(mi, "mm", 1)
                ctx.mark(f"mm:hs{ntc}_31")
                ctx.wait(t, f"se:hrs{ntc}")
                ctx.wait(t, "ve:wt63")
                ctx.wait(t, f"ve:hn{ntc}")
                for mt2 in range(KT):
                    pairs = [(wtd[:, kt2 * D + mt2 * NP: kt2 * D + (mt2 + 1) * NP], csl(h, kt2))
                             for kt2 in range(32)]
                    ctx.inc(accum_mm(t, bk[2 + mt2 % 2][:, :], pairs), "mm", 1)
                    ctx.mark(f"mm:d{ntc}_{mt2}")
                    if mt2 >= 1:
                        ctx.wait(t, f"se:dg{ntc}_{mt2 - 1}")
                ctx.wait(t, f"se:dg{ntc}_7")

        def sca(s):
            ctx.wait(s, "lg:thrc")
            for k in range(NSTR):
                ctx.wait(s, f"lg:ts{k}")
                if k >= 2:
                    ctx.wait(s, f"ve:wt{k - 2}")
                s.activation(csl(sgn, k % 2, 1024), csl(tstg, k % 2, 1024), AF.Sign)
                s.activation(csl(absb, k % 2, 1024), csl(tstg, k % 2, 1024), AF.Abs)
                mi = s.activation(cols[:, 6:7], cols[:, 0:1], AF.Copy)
                ctx.inc(mi, "se", 1)
                ctx.mark(f"se:tsa{k}")
            for ntc in range(NT4):
                for mt in range(32):
                    ctx.wait(s, f"mm:u{ntc}_{mt}")
                    if mt >= 2:
                        ctx.wait(s, f"mm:hs{ntc}_{mt - 2}")
                    s.activation(csl(h, mt), bk[mt % 2][:, :], AF.Gelu)
                    mi = s.activation(csl(hsq, mt % 2), csl(h, mt), AF.Square)
                    ctx.inc(mi, "se", 1)
                    ctx.mark(f"se:ug{ntc}_{mt}")
                ctx.wait(s, f"mm:hs{ntc}_31")
                if ntc >= 1:
                    ctx.wait(s, f"ve:hn{ntc - 1}")
                s.activation(nrx2[:, :], p_hs[:, :], AF.Sqrt, scale=1.0 / 4096)
                mi = s.activation(cols[:, 6:7], cols[:, 0:1], AF.Copy)
                ctx.inc(mi, "se", 1)
                ctx.mark(f"se:hrs{ntc}")
                for mt2 in range(KT):
                    ctx.wait(s, f"mm:d{ntc}_{mt2}")
                    if mt2 >= 2:
                        ctx.wait(s, f"ve:g{ntc}_{mt2 - 2}")
                    elif ntc >= 1:
                        ctx.wait(s, f"ve:g{ntc - 1}_{6 + mt2}")
                    mi = s.activation(csl(g_sb, mt2 % 2), bk[2 + mt2 % 2][:, :], AF.Silu)
                    ctx.inc(mi, "se", 1)
                    ctx.mark(f"se:dg{ntc}_{mt2}")

        def vec(v):
            v.memset(cols[:, 7:8], 0.0)
            ctx.inc(v.memset(onec[:], 1.0), "ve", 1)
            ctx.mark("ve:ones")
            spc = lambda: v.tensor_copy(cols[:, 6:7], cols[:, 7:8])
            ctx.wait(v, "lg:alp")
            ctx.inc(v.tensor_scalar(rw2[0:1, 10:12], rw2[0:1, 8:10], 0.5, None, ALU.mult), "ve", 1)
            ctx.mark("ve:thr")
            for k in range(NSTR):
                ctx.wait(v, f"se:tsa{k}")
                dst = tsrc(k)[1]
                v.tensor_scalar(msk[:, :], csl(absb, k % 2, 1024), cols[:, tsrc(k)[2]:tsrc(k)[2] + 1],
                                None, ALU.is_gt)
                spc()
                mi = v.tensor_mul(dst, csl(sgn, k % 2, 1024), msk[:, :])
                ctx.inc(mi, "ve", 1)
                ctx.mark(f"ve:wt{k}")
            for ntc in range(NT4):
                ctx.wait(v, f"se:hrs{ntc}")
                v.tensor_scalar(nrx2[:, :], nrx2[:, :], 1e-8, None, ALU.add)
                spc()
                v.reciprocal(nrx2[:, :], nrx2[:, :])
                spc()
                mi = None
                for mt in range(32):
                    mi = v.tensor_mul(csl(h, mt), csl(h, mt), nrx2[:, :])
                ctx.inc(mi, "ve", 1)
                ctx.mark(f"ve:hn{ntc}")
                for mt2 in range(KT):
                    ctx.wait(v, f"se:dg{ntc}_{mt2}")
                    ctx.wait(v, f"lh:ep{ntc}_{mt2}")
                    if mt2 >= 2 or ntc >= 1:
                        pi = (ntc, mt2 - 2) if mt2 >= 2 else (ntc - 1, 6 + mt2)
                        ctx.wait(v, f"st:o{pi[0]}_{pi[1]}")
                    v.tensor_mul(tmp2[:, :], csl(s1b, mt2 % 2), csl(g_sb, mt2 % 2))
                    mi = v.tensor_add(csl(outb, mt2 % 2), tmp2[:, :], csl(x1f, mt2 % 2))
                    ctx.inc(mi, "ve", 1)
                    ctx.mark(f"ve:out{ntc}_{mt2}")
                if "d_g" in dbg:
                    pass

        return {"gpsimd": gps, "sync": syn, "scalar": sca, "tensor": ten,
                "vector": vec}


def build_launch_c2(debug=()):
    ctx = Ctx(["lg", "lh", "mm", "ve", "se", "st"])

    def emit(real, nc=None):
        es = ExitStack()
        ctx.start(real, nc, es)
        fns = _phase_c2(ctx, real, nc, es, debug)
        block = es.enter_context(nc.Block()) if real else _Mock()
        for name in ("gpsimd", "sync", "scalar", "tensor", "vector"):
            if real:
                getattr(block, name)(fns[name])
            else:
                fns[name](_Mock())
        es.close()

    emit(False)
    emit(False)
    nc = bass.Bass("TRN2", target_bir_lowering=False, debug=False)
    emit(True, nc)
    return nc


SEM_PRODUCER = {"gpsimd": ("lg",), "sync": ("lh", "st"), "tensor": ("mm",),
                "scalar": ("se",), "vector": ("ve",)}


def build_launch_c(debug=()):
    ctx = Ctx(["lg", "lh", "mm", "ve", "se", "st"])

    def emit(real, nc=None):
        es = ExitStack()
        ctx.start(real, nc, es)
        es1 = ExitStack()
        fns1, hand = _phase_c1(ctx, real, nc, es1, (), internal=True)
        if real:
            es1.close()
        ctx2 = Ctx2(ctx, "p2.")
        ext = {"x1t": hand["x1t"], "s1t": hand["s1_o"], "xnt": hand["xn_o"],
               "alp_in": hand["alp_o"], "ones_in": hand["ones_in"]}
        fns2 = _phase_c2(ctx2, real, nc, es, (), ext=ext, sfx="_2")
        block = es.enter_context(nc.Block()) if real else _Mock()
        for name in ("gpsimd", "sync", "scalar", "tensor", "vector"):
            def combo(eng, f1=fns1[name], f2=fns2[name], name=name):
                f1(eng)
                for s in SEM_PRODUCER[name]:
                    ctx.mark(f"{s}:p1end")
                for s in ("lg", "lh", "mm", "ve", "se", "st"):
                    ctx.wait(eng, f"{s}:p1end")
                f2(eng)
            if real:
                getattr(block, name)(combo)
            else:
                combo(_Mock())
        es.close()

    emit(False)
    emit(False)
    nc = bass.Bass("TRN2", target_bir_lowering=False, debug=False)
    emit(True, nc)
    return nc


def host_inputs_c(inp, core, x1_full, v_full, dl_full):
    m = host_inputs_c1(inp, core, x1_full, v_full, dl_full)
    m["wut_in_2"] = _np(np.asarray(inp["L1_ffn_w_up"]).T)
    m["wdt_in_2"] = _np(np.asarray(inp["L1_ffn_w_down"]).T)
    return m


def host_inputs_c2(inp, core, x1_full, s1_full, xn_full, alp):
    b, hf = core // 2, core % 2
    t0 = hf * TH
    return {
        "x1t": _np(np.asarray(x1_full[b])[t0:t0 + TH].T),
        "s1t": np.ascontiguousarray(np.asarray(s1_full[b]).T[:, t0:t0 + TH]),
        "xnt": np.ascontiguousarray(np.asarray(xn_full[b]).T[:, t0:t0 + TH]),
        "wut_in": _np(np.asarray(inp["L1_ffn_w_up"]).T),
        "wdt_in": _np(np.asarray(inp["L1_ffn_w_down"]).T),
        "alp_in": _np(np.asarray(alp).reshape(1, 2)),
        "ones_in": np.ones((NP, NP), dtype=ml_dtypes.bfloat16),
    }


_NC_CACHE = {}


def _bf(a):
    a = np.asarray(a)
    if a.dtype == np.dtype("V2"):
        a = a.view(ml_dtypes.bfloat16)
    return a


def kernel(**inputs):
    for key, bld in (("A", build_launch_a), ("B", build_launch_b),
                     ("C", build_launch_c)):
        if key not in _NC_CACHE:
            _NC_CACHE[key] = bld()
    cores = list(range(8))
    res = run_bass_kernel_spmd(_NC_CACHE["A"], [host_inputs_a(inputs, c) for c in cores],
                               core_ids=cores)
    x1 = np.zeros((B, S, D), np.float32)
    for c in cores:
        b, h = c // 2, c % 2
        x1[b, h * TH:(h + 1) * TH] = np.asarray(res.results[c]["x1t"], np.float32).T
    res = run_bass_kernel_spmd(_NC_CACHE["B"], [host_inputs_b(inputs, c, x1) for c in cores],
                               core_ids=cores)
    v_full = np.zeros((B, S, D), ml_dtypes.bfloat16)
    dl_full = np.zeros((B, S, D), ml_dtypes.bfloat16)
    for c in cores:
        b, p = c // 2, c % 2
        r = res.results[c]
        v_full[b, :, p * 512:(p + 1) * 512] = _bf(r["vfm_out"]).T
        dl_full[b, :, (2 * p) * 256:(2 * p + 1) * 256] = _bf(r["dseqA"])
        dl_full[b, :, (2 * p + 1) * 256:(2 * p + 2) * 256] = _bf(r["dseqB"])
    res = run_bass_kernel_spmd(_NC_CACHE["C"],
                               [host_inputs_c(inputs, c, x1, v_full, dl_full) for c in cores],
                               core_ids=cores)
    out = np.zeros((B, S, D), np.float32)
    for c in cores:
        b, hf = c // 2, c % 2
        out[b, hf * TH:(hf + 1) * TH] = np.asarray(res.results[c]["out_o"], np.float32).T
    return out
